# version 56
# baseline (speedup 1.0000x reference)
import contextlib
import numpy as np
import concourse.bass as bass
import concourse.mybir as mybir
from concourse.bass_utils import run_bass_kernel_spmd

F32 = mybir.dt.float32
BF16 = mybir.dt.bfloat16
I32 = mybir.dt.int32
AF = mybir.ActivationFunctionType
ALU = mybir.AluOpType
AX = mybir.AxisListType

D = 1024
T = 2048
KB = 8
C = 512
NCH = T // C
H = 512
NE = 16
NST = 16
EPS = 1e-5
ARENA_WORDS = 35400
SERIAL = False
ZIP_M = True
USE_CACHE = True

ENGS = ("pe", "act", "dve", "pool", "sp")

F_ORDER = [("f1", 0), ("f1", 1), ("f1", 2), ("f2", 0),
           ("f1", 3), ("f2", 1), ("f2", 2), ("f2", 3)]
GROUPS = [[i] for i in range(17)]


class Buf:
    __slots__ = ("name", "wr", "rds")

    def __init__(self, name):
        self.name = name
        self.wr = None
        self.rds = []


class Sched:
    def __init__(self):
        self.plan = {k: [] for k in ENGS}
        self.cnt = {k: 0 for k in ENGS}
        self.waited = {k: {} for k in ENGS}
        self.bar = {}

    def stage(self, eng, emit, reads=(), writes=(), dmaq=None, ndma=0):
        if SERIAL:
            self.bar = dict(self.cnt)
        deps = dict(self.bar)

        def need(ev):
            if ev is None:
                return
            k, v = ev
            if deps.get(k, 0) < v:
                deps[k] = v
        for b in reads:
            need(b.wr)
        for b in writes:
            need(b.wr)
            for ev in b.rds:
                need(ev)
        if eng == "pe":
            deps.pop("pe", None)
        w = self.waited[eng]
        fin = {}
        for k, v in deps.items():
            if w.get(k, 0) >= v:
                continue
            w[k] = v
            fin[k] = v
        if dmaq is None:
            self.cnt[eng] += 1
            ev = (eng, self.cnt[eng])
        else:
            assert ndma > 0
            self.cnt[dmaq] = self.cnt.get(dmaq, 0) + 16 * ndma
            ev = (dmaq, self.cnt[dmaq])
        self.plan[eng].append((fin, emit, dmaq))
        for b in writes:
            b.wr = ev
            b.rds = []
        for b in reads:
            b.rds.append(ev)
        return ev

    def barrier(self):
        self.bar = dict(self.cnt)


class Arena:
    def __init__(self, t, words):
        self.t = t
        self.words = words
        self.off = 0
        self.peak = 0

    def alloc(self, name, p, cols, dtype):
        if dtype in (BF16,):
            w = (cols + 1) // 2
        else:
            w = cols
        self.off = (self.off + 15) & ~15
        a = self.off
        self.off += w
        self.peak = max(self.peak, self.off)
        assert self.off <= self.words, f"arena overflow at {name}: {self.off * 4}B"
        view = self.t[0:p, a:a + w]
        if dtype == BF16:
            view = view.bitcast(BF16)[:, 0:cols]
        elif dtype == I32:
            view = view.bitcast(I32)
        return view, Buf(name)


def build_program():
    nc = bass.Bass("TRN2", target_bir_lowering=False, detect_race_conditions=False)

    def din(name, shape):
        return nc.dram_tensor(name, list(shape), F32, kind="ExternalInput").ap()

    xT_h = din("xT", (D, T))
    ipT_h = din("ipT", (D, 2 * D))
    dpT_h = din("dpT", (D, D))
    bpT_h = din("bpT", (D, 128))
    gwT_h = din("gwT", (D, NE))
    w1T_h = din("w1T", (NE, D, H))
    w3T_h = din("w3T", (NE, D, H))
    w2T_h = din("w2T", (NE, H, D))
    sw1T_h = din("sw1T", (D, H))
    sw3T_h = din("sw3T", (D, H))
    sw2T_h = din("sw2T", (H, D))
    cT_h = din("cT", (NST, D))
    alog_h = din("alog", (D, NST))
    convw_h = din("convw", (D, 4))
    ipb_h = din("ipb", (16, 128))
    convb_h = din("convb", (8, 128))
    bpb_h = din("bpb", (1, 128))
    yT_h = nc.dram_tensor("yT", [D, T], F32, kind="ExternalOutput").ap()
    w1c_h = nc.dram_tensor("w1c", [NE + 1, 2, 128, KB * 256], BF16, kind="Internal").ap()
    w3c_h = nc.dram_tensor("w3c", [NE + 1, 2, 128, KB * 256], BF16, kind="Internal").ap()
    w2c_h = nc.dram_tensor("w2c", [NE + 1, 2, 128, 4 * 512], BF16, kind="Internal").ap()
    ipc_h = nc.dram_tensor("ipc", [16, 128, KB * 128], BF16, kind="Internal").ap()
    dpc_h = nc.dram_tensor("dpc", [KB, 128, KB * 128], BF16, kind="Internal").ap()

    sched = Sched()
    es = contextlib.ExitStack()
    with es:
        def sb(name, p, cols, dtype):
            t = es.enter_context(nc.sbuf_tensor(name, [p, cols], dtype))
            return t, Buf(name)

        x_t = []
        x_b = []
        for kb in range(KB):
            t = es.enter_context(nc.sbuf_tensor(f"x{kb}", [128, T], F32))
            x_t.append(t)
            x_b.append([Buf(f"x{kb}_{ch}") for ch in range(NCH)])
        cpool, cpool_b = sb("cpool", 128, 8, F32)
        CONSTS = [0.0, 1.0, -1.0, 0.5, EPS]
        for j, v in enumerate(CONSTS):
            nc.const_aps.register(F32, v, cpool[:, j:j + 1])
        ones1f, ones1f_b = sb("ones1f", 1, 128, F32)
        ones128, ones128_b = sb("ones128", 128, 1, F32)
        identF, identF_b = sb("identF", 128, 128, F32)
        identB, identB_b = sb("identB", 128, 128, BF16)
        iotat, iotat_b = sb("iotat", 128, 128, I32)
        A_sb, A_b = sb("A_sb", 128, 128, F32)
        c_sb, c_b = sb("c_sb", 128, 128, F32)
        conv_sb, conv_wb = sb("conv_sb", 128, 32, F32)
        ipb_sb, ipb_b = sb("ipb_sb", 128, 16, F32)
        cvb_sb, cvb_b = sb("cvb_sb", 128, 8, F32)
        bpb_sb, bpb_b = sb("bpb_sb", 128, 1, F32)
        gw_sb, gw_b = sb("gw_sb", 128, 8 * NE, F32)
        Sbc_sb, Sbc_b = sb("Sbc_sb", 128, NE, F32)
        carry, carry_b = sb("carry", 128, 128, F32)
        btmp, btmp_b = sb("btmp", 16, 128, F32)
        srow, srow_b = sb("srow", 1, 128, F32)

        arena_t = es.enter_context(nc.sbuf_tensor("arena", [128, ARENA_WORDS], F32))
        AR = Arena(arena_t, ARENA_WORDS)

        pb_t, pb_b = [], []
        for i in range(8):
            t = es.enter_context(nc.psum_tensor(f"pb{i}", [128, 512], F32))
            pb_t.append(t)
            pb_b.append(Buf(f"pb{i}"))

        SEM = {}
        for k in ENGS:
            SEM[k] = es.enter_context(nc.semaphore(f"sem_{k}"))

        def getsem(q):
            if q not in SEM:
                SEM[q] = es.enter_context(nc.semaphore(f"sem_{q}"))
            return SEM[q]
        blk = es.enter_context(nc.Block())

        def cst(eng, fn, reads=(), writes=()):
            def emit(e, fn=fn, eng=eng):
                inst = fn(e)
                inst.then_inc(SEM[eng], 1)
            return sched.stage(eng, emit, reads, writes)

        def cchain(eng, fns, reads=(), writes=()):
            rw = list(reads) + list(writes)
            for fn in fns:
                cst(eng, fn, reads=rw, writes=list(writes))

        def dst(eng, q, transfers, reads=(), writes=()):
            getsem(q)
            def emit(e, transfers=transfers, q=q):
                for dstp, srcp, kw in transfers:
                    e.dma_start(out=dstp, in_=srcp, **kw).then_inc(SEM[q], 16)
            return sched.stage(eng, emit, reads, writes, dmaq=q, ndma=len(transfers))

        _ct_off = AR.off
        ct16, ct16_b = AR.alloc("ct16", 16, D, F32)
        AR.off = _ct_off
        xln_ap = [[None] * KB for _ in range(2)]
        xln_buf = [[None] * KB for _ in range(2)]
        for i in range(2):
            for kb in range(KB):
                xln_ap[i][kb], xln_buf[i][kb] = AR.alloc(f"xln{i}_{kb}", 128, C, BF16)
        act_ap = [[None] * 4 for _ in range(2)]
        act_buf = [[None] * 4 for _ in range(2)]
        for s in range(2):
            for hb in range(4):
                act_ap[s][hb], act_buf[s][hb] = AR.alloc(f"act{s}_{hb}", 128, C, BF16)
        w1h_ap, w1h_buf, w3h_ap, w3h_buf = [], [], [], []
        for i in range(2):
            a, b = AR.alloc(f"w1h{i}", 128, KB * 256, BF16)
            w1h_ap.append(a)
            w1h_buf.append(b)
            a, b = AR.alloc(f"w3h{i}", 128, KB * 256, BF16)
            w3h_ap.append(a)
            w3h_buf.append(b)
        w2h_ap, w2h_buf = [], []
        for i in range(3):
            a, b = AR.alloc(f"w2h{i}", 128, 4 * 512, BF16)
            w2h_ap.append(a)
            w2h_buf.append(b)
        cb_ap, cb_buf = [], []
        for i in range(2):
            a, b = AR.alloc(f"cb{i}", 128, C, F32)
            cb_ap.append(a)
            cb_buf.append(b)
        sil_ap, sil_buf, h3c_ap, h3c_buf = [], [], [], []
        for i in range(2):
            a, b = AR.alloc(f"sil{i}", 128, C, F32)
            sil_ap.append(a)
            sil_buf.append(b)
            a, b = AR.alloc(f"h3c{i}", 128, C, F32)
            h3c_ap.append(a)
            h3c_buf.append(b)
        comb_ap, comb_buf = [], []
        for i in range(2):
            a, b = AR.alloc(f"comb{i}", 16, C, BF16)
            comb_ap.append(a)
            comb_buf.append(b)
        rowsF_ap, rowsF_buf = AR.alloc("rowsF", 1, 2 * C, F32)
        sq_ap, sq_buf = [], []
        for i in range(2):
            a, b = AR.alloc(f"sqln{i}", 128, C, F32)
            sq_ap.append(a)
            sq_buf.append(b)
        tmp_ap, tmp_buf = [], []
        for i in range(2):
            a, b = AR.alloc(f"tmpln{i}", 128, 256, F32)
            tmp_ap.append(a)
            tmp_buf.append(b)
        tk_n1, tk_n1_b = AR.alloc("tkn1", 128, 4 * NE, F32)
        TKB = {}
        for nmv in ("l", "el", "t1", "sb"):
            TKB[nmv] = AR.alloc(f"tkb{nmv}", 128, 4 * NE, F32)
        TKB["vv"] = TKB["l"]
        TKB["eqi"] = AR.alloc("tkbeqi", 128, 4 * NE, I32)
        TKB["c1"] = AR.alloc("tkbc1", 128, 4, F32)
        TKB["c2"] = AR.alloc("tkbc2", 128, 4, F32)
        TKB["rc"] = AR.alloc("tkbrc", 128, 8, F32)
        mxln_ap, mxln_buf = [], []
        for kb in range(KB):
            a, b = AR.alloc(f"mxln{kb}", 128, C, BF16)
            mxln_ap.append(a)
            mxln_buf.append(b)
        cpad_ap, cpad_buf = [], []
        for r in range(KB):
            a, b = AR.alloc(f"cpad{r}", 128, C + 4, BF16)
            cpad_ap.append(a)
            cpad_buf.append(b)
        ipw_ap, ipw_buf, dpw_ap, dpw_buf = [], [], [], []
        for i in range(2):
            a, b = AR.alloc(f"ipw{i}", 128, KB * 128, BF16)
            ipw_ap.append(a)
            ipw_buf.append(b)
            a, b = AR.alloc(f"dpw{i}", 128, KB * 128, BF16)
            dpw_ap.append(a)
            dpw_buf.append(b)
        bpw_ap, bpw_buf = AR.alloc("bpw", 128, KB * 128, BF16)
        Bh_ap, Bh_buf = AR.alloc("Bh", 128, C, BF16)
        Bl_ap, Bl_buf = AR.alloc("Bl", 128, C, BF16)
        delta_ap, delta_buf, du_ap, du_buf = [], [], [], []
        dA_ap, dA_buf, dbu_ap, dbu_buf = [], [], [], []
        y_ap, y_buf = [], []
        for i in range(2):
            a, b = AR.alloc(f"delta{i}", 128, C, F32)
            delta_ap.append(a)
            delta_buf.append(b)
            a, b = AR.alloc(f"du{i}", 128, C, F32)
            du_ap.append(a)
            du_buf.append(b)
            a, b = AR.alloc(f"dA{i}", 128, C, F32)
            dA_ap.append(a)
            dA_buf.append(b)
            a, b = AR.alloc(f"dbu{i}", 128, C, F32)
            dbu_ap.append(a)
            dbu_buf.append(b)
            a, b = AR.alloc(f"y{i}", 128, C, F32)
            y_ap.append(a)
            y_buf.append(b)
        dscr_ap, dscr_buf = [], []
        for i in range(1):
            a, b = AR.alloc(f"dscr{i}", 128, C, F32)
            dscr_ap.append(a)
            dscr_buf.append(b)
        vs_ap, vs_buf = [], []
        for i in range(3):
            a, b = AR.alloc(f"vs{i}", 128, C, F32)
            vs_ap.append(a)
            vs_buf.append(b)
        rowsM_ap, rowsM_buf = AR.alloc("rowsM", 1, 2 * C, F32)

        def setup():
            cst("dve", lambda e: e.memset(cpool[:], 0.0), writes=[cpool_b])
            for j, v in enumerate(CONSTS):
                if v != 0.0:
                    cst("dve", lambda e, j=j, v=v: e.memset(cpool[:, j:j + 1], v),
                        reads=[cpool_b], writes=[cpool_b])
            cst("dve", lambda e: e.memset(ones1f[:], 1.0), writes=[ones1f_b])
            cst("dve", lambda e: e.memset(ones128[:], 1.0), writes=[ones128_b])
            cst("dve", lambda e: e.memset(carry[:], 0.0), writes=[carry_b])
            cst("dve", lambda e: e.memset(tk_n1, -1.0), writes=[tk_n1_b])
            cst("pool", lambda e: e.iota(iotat[:], pattern=[[1, 128]], base=0,
                                         channel_multiplier=-1), writes=[iotat_b])
            cst("dve", lambda e: e.tensor_single_scalar(identF[:], iotat[:], 0, ALU.is_equal),
                reads=[iotat_b], writes=[identF_b])
            cst("dve", lambda e: e.tensor_copy(identB[:], identF[:]),
                reads=[identF_b], writes=[identB_b])
            for kb in range(KB):
                def _ld(e, kb=kb):
                    e.dma_start(out=x_t[kb][:], in_=xT_h[kb * 128:(kb + 1) * 128, :]
                                ).then_inc(SEM[f"ldx{kb}"], 16)
                getsem(f"ldx{kb}")
                sched.stage("sp", _ld, writes=x_b[kb], dmaq=f"ldx{kb}", ndma=1)
            dst("sp", "lda", [(A_sb[:].rearrange("p (r n) -> p r n", n=NST),
                              alog_h.rearrange("(r p) n -> p r n", p=128), {})],
                writes=[A_b])
            dst("sp", "ldct", [(ct16[:], cT_h, {})], writes=[ct16_b])
            dst("sp", "ldcv", [(conv_sb[:].rearrange("p (r k) -> p r k", k=4),
                              convw_h.rearrange("(r p) k -> p r k", p=128), {})],
                writes=[conv_wb])
            dst("sp", "ldgw", [(gw_sb[:].rearrange("p (r n) -> p r n", n=NE),
                              gwT_h.rearrange("(r p) n -> p r n", p=128), {})],
                writes=[gw_b])
            cst("act", lambda e: e.activation(A_sb[:], A_sb[:], AF.Exp),
                reads=[A_b], writes=[A_b])
            cst("dve", lambda e: e.tensor_single_scalar(A_sb[:], A_sb[:], -1.0, ALU.mult),
                reads=[A_b], writes=[A_b])
            for r in range(KB):
                cst("pe", lambda e, r=r: e.transpose(
                    pb_t[r % 2][:, 0:16], ct16[0:16, r * 128:(r + 1) * 128],
                    identF[0:16, 0:16]),
                    reads=[ct16_b, identF_b], writes=[pb_b[r % 2]])
                cst("dve", lambda e, r=r: e.tensor_copy(
                    c_sb[:, r * 16:(r + 1) * 16], pb_t[r % 2][:, 0:16]),
                    reads=[pb_b[r % 2], c_b], writes=[c_b])
            dst("sp", "ldb1", [(btmp[:], ipb_h, {})], writes=[btmp_b])
            cst("pe", lambda e: e.transpose(pb_t[2][:, 0:16], btmp[:], identF[0:16, 0:16]),
                reads=[btmp_b, identF_b], writes=[pb_b[2]])
            cst("dve", lambda e: e.tensor_copy(ipb_sb[:], pb_t[2][:, 0:16]),
                reads=[pb_b[2]], writes=[ipb_b])
            dst("sp", "ldb2", [(btmp[0:8, :], convb_h, {})], reads=[btmp_b], writes=[btmp_b])
            cst("pe", lambda e: e.transpose(pb_t[3][:, 0:8], btmp[0:8, :], identF[0:8, 0:8]),
                reads=[btmp_b, identF_b], writes=[pb_b[3]])
            cst("dve", lambda e: e.tensor_copy(cvb_sb[:], pb_t[3][:, 0:8]),
                reads=[pb_b[3]], writes=[cvb_b])
            dst("sp", "ldb3", [(srow[:], bpb_h, {})], writes=[srow_b])
            cst("pe", lambda e: e.transpose(pb_t[4][:, 0:1], srow[:], identF[0:1, 0:1]),
                reads=[srow_b, identF_b], writes=[pb_b[4]])
            cst("dve", lambda e: e.tensor_copy(bpb_sb[:], pb_t[4][:, 0:1]),
                reads=[pb_b[4]], writes=[bpb_b])
            def _smm(e):
                for kb in range(KB):
                    inst = e.matmul(pb_t[5][0:1, 0:NE], ones128[:],
                                    gw_sb[:, kb * NE:(kb + 1) * NE],
                                    start=(kb == 0), stop=(kb == KB - 1))
                return inst
            cst("pe", _smm, reads=[ones128_b, gw_b], writes=[pb_b[5]])
            cst("dve", lambda e: e.tensor_copy(srow[0:1, 0:NE], pb_t[5][0:1, 0:NE]),
                reads=[pb_b[5], srow_b], writes=[srow_b])
            cst("pe", lambda e: e.matmul(pb_t[6][:, 0:NE], ones1f[:], srow[0:1, 0:NE],
                                         start=True, stop=True),
                reads=[ones1f_b, srow_b], writes=[pb_b[6]])
            cst("dve", lambda e: e.tensor_copy(Sbc_sb[:], pb_t[6][:, 0:NE]),
                reads=[pb_b[6]], writes=[Sbc_b])
            for r in range(KB):
                cst("dve", lambda e, r=r: e.memset(cpad_ap[r][:, 0:4], 0.0),
                    writes=[cpad_buf[r]])
            dst("pool", "ldbp",
                [(bpw_ap.rearrange("p (k m) -> p k m", k=KB),
                  bpT_h.rearrange("(k p) m -> p k m", p=128), {})],
                writes=[bpw_buf])

        def ln_units(c0, xlnv, xlnb, rows_ap, rows_buf, bank):
            pb, pbb = pb_t[bank], pb_b[bank]
            rstd_r = rows_ap[0:1, 0:C]
            nmr_r = rows_ap[0:1, C:2 * C]
            ch = c0 // C

            def emit_sq(kb):
                sp_ = kb % 2
                cst("pool", lambda e, kb=kb, sp_=sp_: e.tensor_tensor(
                    sq_ap[sp_], x_t[kb][:, c0:c0 + C], x_t[kb][:, c0:c0 + C],
                    ALU.mult),
                    reads=[x_b[kb][ch]], writes=[sq_buf[sp_]])

            def emit_mm(kb):
                sp_ = kb % 2
                def _mm(e, kb=kb, sp_=sp_):
                    e.matmul(pb[0:1, :], ones128[:], x_t[kb][:, c0:c0 + C],
                             start=(kb == 0), stop=(kb == KB - 1))
                    return e.matmul(pb[32:33, :], ones128[:], sq_ap[sp_],
                                    start=(kb == 0), stop=(kb == KB - 1))
                cst("pe", _mm, reads=[ones128_b, x_b[kb][ch], sq_buf[sp_]],
                    writes=[pbb])

            def sums():
                emit_sq(0)
                emit_sq(1)
                for kb in range(KB):
                    emit_mm(kb)
                    if kb + 2 < KB:
                        emit_sq(kb + 2)

            def rows_chain():
                cchain("dve", [
                    lambda e: e.tensor_single_scalar(nmr_r, pb[0:1, :], 1.0 / D, ALU.mult),
                    lambda e: e.tensor_single_scalar(rstd_r, pb[32:33, :], 1.0 / D, ALU.mult),
                    lambda e: e.scalar_tensor_tensor(tmp_ap[0][0:1, 0:256], nmr_r[0:1, 0:256],
                                                     1.0, nmr_r[0:1, 0:256], ALU.mult, ALU.mult),
                    lambda e: e.scalar_tensor_tensor(tmp_ap[1][0:1, 0:256], nmr_r[0:1, 256:512],
                                                     1.0, nmr_r[0:1, 256:512], ALU.mult, ALU.mult),
                    lambda e: e.tensor_tensor(rstd_r[0:1, 0:256], rstd_r[0:1, 0:256],
                                              tmp_ap[0][0:1, 0:256], ALU.subtract),
                    lambda e: e.tensor_tensor(rstd_r[0:1, 256:512], rstd_r[0:1, 256:512],
                                              tmp_ap[1][0:1, 0:256], ALU.subtract),
                ], reads=[pbb], writes=[rows_buf, tmp_buf[0], tmp_buf[1]])
                cst("dve", lambda e: e.tensor_single_scalar(
                    rstd_r, rstd_r, EPS, ALU.add), writes=[rows_buf])
                cst("act", lambda e: e.sqrt(rstd_r, rstd_r), writes=[rows_buf])
                cst("dve", lambda e: e.reciprocal(rstd_r, rstd_r), writes=[rows_buf])
                cst("dve", lambda e: e.scalar_tensor_tensor(
                    nmr_r, nmr_r, -1.0, rstd_r, ALU.mult, ALU.mult), writes=[rows_buf])

            def apply_half(hf):
                co = hf * 256
                def _bc(e, co=co):
                    e.matmul(pb[:, 0:256], ones1f[:], rstd_r[0:1, co:co + 256],
                             start=True, stop=True)
                    return e.matmul(pb[:, 256:512], ones1f[:], nmr_r[0:1, co:co + 256],
                                    start=True, stop=True)
                cst("pe", _bc, reads=[ones1f_b, rows_buf], writes=[pbb])
                for kb in range(KB):
                    tpar = kb % 2
                    cst("dve", lambda e, kb=kb, co=co, tpar=tpar: e.tensor_tensor(
                        tmp_ap[tpar],
                        x_t[kb][:, c0 + co:c0 + co + 256], pb[:, 0:256], ALU.mult),
                        reads=[x_b[kb][ch], pbb], writes=[tmp_buf[tpar]])
                    cst("dve", lambda e, kb=kb, co=co, tpar=tpar: e.tensor_tensor(
                        xlnv[kb][:, co:co + 256],
                        tmp_ap[tpar], pb[:, 256:512], ALU.add),
                        reads=[tmp_buf[tpar], pbb], writes=[xlnb[kb]])

            return [sums, rows_chain, lambda: apply_half(0), lambda: apply_half(1)]

        def routing_units(c0, par):
            ch = c0 // C
            pb, pbb = pb_t[3], pb_b[3]
            rstd_r = rowsF_ap[0:1, 0:C]
            nmr_r = rowsF_ap[0:1, C:2 * C]
            l_ap, l_b = TKB["l"]
            el_ap, el_b = TKB["el"]
            vv_ap, vv_b = TKB["vv"]
            t1_ap, t1_b = TKB["t1"]
            sb_ap, sb_b = TKB["sb"]
            eqi_ap, eqi_b = TKB["eqi"]
            c1_ap, c1_b = TKB["c1"]
            c2_ap, c2_b = TKB["c2"]
            rc_ap, rc_b = TKB["rc"]

            def seg(ap):
                return ap.rearrange("p (t n) -> p t n", n=NE)

            def colb(ap, j):
                return ap

            def unit_a():
                for tb in range(4):
                    tc = c0 + tb * 128
                    def _gmm(e, tc=tc, tb=tb):
                        for kb in range(KB):
                            inst = e.matmul(pb[:, 16 * tb:16 * tb + NE],
                                            x_t[kb][:, tc:tc + 128],
                                            gw_sb[:, kb * NE:(kb + 1) * NE],
                                            start=(kb == 0), stop=(kb == KB - 1))
                        return inst
                    cst("pe", _gmm, reads=[gw_b] + [x_b[kb][ch] for kb in range(KB)],
                        writes=[pbb])
                for tb in range(4):
                    def _tr(e, tb=tb):
                        e.transpose(pb[:, 64 + 2 * tb:65 + 2 * tb],
                                    rstd_r[0:1, tb * 128:(tb + 1) * 128],
                                    identF[0:1, 0:1])
                        return e.transpose(pb[:, 65 + 2 * tb:66 + 2 * tb],
                                           nmr_r[0:1, tb * 128:(tb + 1) * 128],
                                           identF[0:1, 0:1])
                    cst("pe", _tr, reads=[rowsF_buf, identF_b], writes=[pbb])
                cst("dve", lambda e: e.tensor_copy(rc_ap, pb[:, 64:72]),
                    reads=[pbb, rc_b], writes=[rc_b])

            def unit_b():
                rc3 = rc_ap.rearrange("p (t c) -> p t c", c=2)
                cchain("dve", [
                    lambda e: e.tensor_tensor(
                        seg(l_ap), pb[:, 0:64].rearrange("p (t n) -> p t n", n=NE),
                        rc3[:, :, 0:1].broadcast_to([128, 4, NE]), ALU.mult),
                    lambda e: e.tensor_tensor(
                        seg(sb_ap), Sbc_sb[:].unsqueeze(1).broadcast_to([128, 4, NE]),
                        rc3[:, :, 1:2].broadcast_to([128, 4, NE]), ALU.mult),
                    lambda e: e.tensor_tensor(l_ap, l_ap, sb_ap, ALU.add),
                    lambda e: e.reduce_max(c1_ap, seg(l_ap), axis=AX.X, negate=True),
                    lambda e: e.tensor_tensor(
                        seg(l_ap), seg(l_ap),
                        c1_ap.unsqueeze(2).broadcast_to([128, 4, NE]), ALU.add),
                ], reads=[pbb, Sbc_b],
                   writes=[l_b, sb_b, c1_b, rc_b])
                cst("act", lambda e: e.activation(el_ap, l_ap, AF.Exp),
                    reads=[l_b, el_b], writes=[el_b])
                fns = [lambda e: e.tensor_copy(vv_ap, el_ap)]
                for _ in range(3):
                    fns += [
                        lambda e: e.reduce_max(c2_ap, seg(vv_ap), axis=AX.X),
                        lambda e: e.tensor_tensor(
                            seg(eqi_ap), seg(vv_ap),
                            c2_ap.unsqueeze(2).broadcast_to([128, 4, NE]),
                            ALU.is_equal),
                        lambda e: e.select(vv_ap, eqi_ap, tk_n1, vv_ap),
                    ]
                fns += [
                    lambda e: e.reduce_max(c2_ap, seg(vv_ap), axis=AX.X),
                    lambda e: e.tensor_tensor(
                        seg(t1_ap), seg(el_ap),
                        c2_ap.unsqueeze(2).broadcast_to([128, 4, NE]), ALU.is_ge),
                    lambda e: e.tensor_tensor(t1_ap, el_ap, t1_ap, ALU.mult),
                    lambda e: e.reduce_sum(c1_ap, seg(t1_ap), axis=AX.X),
                    lambda e: e.reciprocal(c1_ap, c1_ap),
                    lambda e: e.tensor_tensor(
                        seg(t1_ap), seg(t1_ap),
                        c1_ap.unsqueeze(2).broadcast_to([128, 4, NE]), ALU.mult),
                ]
                cchain("dve", fns,
                       reads=[el_b, tk_n1_b],
                       writes=[vv_b, eqi_b, c1_b, c2_b, t1_b])

            def unit_c():
                for tb in range(4):
                    co = 256 + 128 * (tb % 2)
                    cst("pe", lambda e, tb=tb, co=co: e.transpose(
                        pb[0:16, co:co + 128], t1_ap[:, 16 * tb:16 * tb + NE],
                        identF[:]),
                        reads=[t1_b, identF_b], writes=[pbb])
                    cst("dve", lambda e, tb=tb, co=co: e.tensor_copy(
                        comb_ap[par][:, tb * 128:(tb + 1) * 128], pb[0:16, co:co + 128]),
                        reads=[pbb, comb_buf[par]], writes=[comb_buf[par]])

            return [unit_a, unit_b, unit_c]

        wcnt = [0]
        cache_buf = {}

        def stream_w13(e_, hf):
            wp = wcnt[0] % 2
            wcnt[0] += 1
            key = ("w13", e_, hf)
            if (not USE_CACHE) or key not in cache_buf:
                if e_ < NE:
                    s1, s3 = w1T_h[e_], w3T_h[e_]
                else:
                    s1, s3 = sw1T_h, sw3T_h
                co = hf * 256
                dst("pool", f"w13c_{wp}",
                    [(w1h_ap[wp].rearrange("p (k h) -> p k h", k=KB),
                      s1[:, co:co + 256].rearrange("(k p) h -> p k h", p=128), {}),
                     (w3h_ap[wp].rearrange("p (k h) -> p k h", k=KB),
                      s3[:, co:co + 256].rearrange("(k p) h -> p k h", p=128), {})],
                    writes=[w1h_buf[wp], w3h_buf[wp]])
                cb13 = Buf(f"c13_{e_}_{hf}")
                dst("sp", f"wb13_{wp}",
                    [(w1c_h[e_, hf], w1h_ap[wp], {}),
                     (w3c_h[e_, hf], w3h_ap[wp], {})],
                    reads=[w1h_buf[wp], w3h_buf[wp]], writes=[cb13])
                cache_buf[key] = cb13
            else:
                dst("sp", f"w13_{wp}",
                    [(w1h_ap[wp], w1c_h[e_, hf], {}),
                     (w3h_ap[wp], w3c_h[e_, hf], {})],
                    reads=[cache_buf[key]], writes=[w1h_buf[wp], w3h_buf[wp]])
            return wp

        w2cnt = [0]

        def stream_w2(e_, th):
            wp = w2cnt[0] % 3
            w2cnt[0] += 1
            key = ("w2", e_, th)
            if (not USE_CACHE) or key not in cache_buf:
                s2 = w2T_h[e_] if e_ < NE else sw2T_h
                co = th * 512
                dst("pool", f"w2c_{wp}",
                    [(w2h_ap[wp].rearrange("p (k h) -> p k h", k=4),
                      s2[:, co:co + 512].rearrange("(k p) h -> p k h", p=128), {})],
                    writes=[w2h_buf[wp]])
                cb2 = Buf(f"c2_{e_}_{th}")
                dst("sp", f"wb2_{wp}", [(w2c_h[e_, th], w2h_ap[wp], {})],
                    reads=[w2h_buf[wp]], writes=[cb2])
                cache_buf[key] = cb2
            else:
                dst("act", f"w2_{wp}", [(w2h_ap[wp], w2c_h[e_, th], {})],
                    reads=[cache_buf[key]], writes=[w2h_buf[wp]])
            return wp

        def f_chunk_gen(fi, c, par):
            c0 = c * C
            ch = c
            rot = [0]
            FBANKS = [0, 1, 2, 4, 5]

            def nb():
                b = FBANKS[rot[0] % len(FBANKS)]
                rot[0] += 1
                return b

            flat13 = []
            for gi, grp in enumerate(GROUPS):
                for si, e_ in enumerate(grp):
                    flat13.append((e_, 0))
                    flat13.append((e_, 1))
            p13 = {}
            p13[0] = stream_w13(GROUPS[0][0], 0)
            p13[1] = stream_w13(GROUPS[0][0], 1)
            next13 = [2]

            def adv13():
                if next13[0] < len(flat13):
                    e_, hf = flat13[next13[0]]
                    p13[next13[0]] = stream_w13(e_, hf)
                    next13[0] += 1

            flat2 = []
            for gi, grp in enumerate(GROUPS):
                for th in range(2):
                    flat2.append((gi, th, grp[-1]))
            p2 = {}
            next2 = [0]

            def adv2():
                if next2[0] < len(flat2):
                    gi, th, e_last = flat2[next2[0]]
                    p2[next2[0]] = stream_w2(e_last, th)
                    next2[0] += 1
            adv2()
            adv2()

            EXP = [g[0] for g in GROUPS]
            k13 = [0]

            def hpart(j, e_):
                aset = j % 2
                if e_ < NE:
                    b = nb()
                    cst("pe", lambda e, e_=e_, b=b: e.matmul(
                        pb_t[b][:], identB[0:16, e_:e_ + 1].broadcast_to([16, 128]),
                        comb_ap[par][:], start=True, stop=True),
                        reads=[identB_b, comb_buf[par]], writes=[pb_b[b]])
                    cst("act", lambda e, aset=aset, b=b: e.copy(cb_ap[aset], pb_t[b][:]),
                        reads=[pb_b[b], cb_buf[aset]], writes=[cb_buf[aset]])
                for hb in range(4):
                    hf = hb // 2
                    idx13 = k13[0] * 2 + hf
                    wp = p13[idx13]
                    b1 = nb()
                    b3 = nb()
                    hoff = (hb % 2) * 128
                    def _h1(e, wp=wp, hoff=hoff, b1=b1):
                        for kb in range(KB):
                            inst = e.matmul(
                                pb_t[b1][:],
                                w1h_ap[wp][:, kb * 256 + hoff: kb * 256 + hoff + 128],
                                xln_ap[par][kb], start=(kb == 0), stop=(kb == KB - 1))
                        return inst
                    cst("pe", _h1, reads=[w1h_buf[wp]] + xln_buf[par],
                        writes=[pb_b[b1]])
                    def _h3(e, wp=wp, hoff=hoff, b3=b3):
                        for kb in range(KB):
                            inst = e.matmul(
                                pb_t[b3][:],
                                w3h_ap[wp][:, kb * 256 + hoff: kb * 256 + hoff + 128],
                                xln_ap[par][kb], start=(kb == 0), stop=(kb == KB - 1))
                        return inst
                    cst("pe", _h3, reads=[w3h_buf[wp]] + xln_buf[par],
                        writes=[pb_b[b3]])
                    spar = hb % 2
                    cst("act", lambda e, b1=b1, spar=spar: e.activation(
                        sil_ap[spar], pb_t[b1][:], AF.Silu),
                        reads=[pb_b[b1], sil_buf[spar]], writes=[sil_buf[spar]])
                    if e_ < NE:
                        cst("dve", lambda e, b3=b3, spar=spar, aset=aset:
                            e.tensor_tensor(h3c_ap[spar], pb_t[b3][:],
                                            cb_ap[aset], ALU.mult),
                            reads=[pb_b[b3], cb_buf[aset], h3c_buf[spar]],
                            writes=[h3c_buf[spar]])
                    else:
                        cst("dve", lambda e, b3=b3, spar=spar: e.tensor_copy(
                            h3c_ap[spar], pb_t[b3][:]),
                            reads=[pb_b[b3], h3c_buf[spar]], writes=[h3c_buf[spar]])
                    cst("pool", lambda e, aset=aset, hb=hb, spar=spar:
                        e.tensor_tensor(act_ap[aset][hb], sil_ap[spar],
                                        h3c_ap[spar], ALU.mult),
                        reads=[sil_buf[spar], h3c_buf[spar], act_buf[aset][hb]],
                        writes=[act_buf[aset][hb]])
                    if hb == 1 or hb == 3:
                        adv13()
                    yield
                k13[0] += 1

            def epart(j, e_):
                aset = j % 2
                for db in range(KB):
                    th = db // 4
                    wp2 = p2[j * 2 + th]
                    eb = nb()
                    doff = (db % 4) * 128
                    def _eo(e, wp2=wp2, doff=doff, eb=eb, aset=aset):
                        for hb in range(4):
                            inst = e.matmul(
                                pb_t[eb][:],
                                w2h_ap[wp2][:, hb * 512 + doff: hb * 512 + doff + 128],
                                act_ap[aset][hb],
                                start=(hb == 0), stop=(hb == 3))
                        return inst
                    cst("pe", _eo,
                        reads=[w2h_buf[wp2]] + [act_buf[aset][hb] for hb in range(4)],
                        writes=[pb_b[eb]])
                    cst("dve", lambda e, db=db, eb=eb: e.scalar_tensor_tensor(
                        x_t[db][:, c0:c0 + C], pb_t[eb][:], 0.5,
                        x_t[db][:, c0:c0 + C], ALU.mult, ALU.add),
                        reads=[pb_b[eb], x_b[db][ch]], writes=[x_b[db][ch]])
                    if db == 3:
                        adv2()
                    if db == 7:
                        adv2()
                    yield

            yield from hpart(0, EXP[0])
            for j in range(1, len(EXP)):
                yield from hpart(j, EXP[j])
                yield from epart(j - 1, EXP[j - 1])
            yield from epart(len(EXP) - 1, EXP[-1])

        def f_head_units(fi, c, par):
            c0 = c * C
            lu = ln_units(c0, xln_ap[par], xln_buf[par], rowsF_ap, rowsF_buf, 3)
            ru = routing_units(c0, par)
            skip = lambda: None
            u = [lu[0], lu[1], skip, skip, lu[2], lu[3],
                 ru[0], ru[1], skip, skip, ru[2]]
            return u

        def m_chunk_gen(c):
            c0 = c * C
            ch = c

            def stream_ip(mb):
                pp = mb % 2
                key = ("ip", mb)
                if (not USE_CACHE) or key not in cache_buf:
                    dst("pool", f"mipc_{pp}",
                        [(ipw_ap[pp].rearrange("p (k m) -> p k m", k=KB),
                          ipT_h[:, mb * 128:(mb + 1) * 128].rearrange(
                              "(k p) m -> p k m", p=128), {})],
                        writes=[ipw_buf[pp]])
                    cbi = Buf(f"cip_{mb}")
                    dst("sp", f"wbip_{pp}", [(ipc_h[mb], ipw_ap[pp], {})],
                        reads=[ipw_buf[pp]], writes=[cbi])
                    cache_buf[key] = cbi
                else:
                    dst("sp", f"mip_{pp}", [(ipw_ap[pp], ipc_h[mb], {})],
                        reads=[cache_buf[key]], writes=[ipw_buf[pp]])

            def stream_dp(r):
                pp = r % 2
                key = ("dp", r)
                if (not USE_CACHE) or key not in cache_buf:
                    dst("pool", f"mdpc_{pp}",
                        [(dpw_ap[pp].rearrange("p (k m) -> p k m", k=KB),
                          dpT_h[:, r * 128:(r + 1) * 128].rearrange(
                              "(k p) m -> p k m", p=128), {})],
                        writes=[dpw_buf[pp]])
                    cbd = Buf(f"cdp_{r}")
                    dst("sp", f"wbdp_{pp}", [(dpc_h[r], dpw_ap[pp], {})],
                        reads=[dpw_buf[pp]], writes=[cbd])
                    cache_buf[key] = cbd
                else:
                    dst("sp", f"mdp_{pp}", [(dpw_ap[pp], dpc_h[r], {})],
                        reads=[cache_buf[key]], writes=[dpw_buf[pp]])

            for u in ln_units(c0, mxln_ap, mxln_buf, rowsM_ap, rowsM_buf, 6):
                u()
                yield
            if ch > 0:
                for r in range(KB):
                    cst("dve", lambda e, r=r: e.tensor_copy(
                        cpad_ap[r][:, 1:4], cpad_ap[r][:, C + 1:C + 4]),
                        reads=[cpad_buf[r]], writes=[cpad_buf[r]])
            yield
            stream_ip(0)
            stream_ip(1)
            for mb in range(KB):
                pp = mb % 2
                bank = 6 + pp
                def _ipmm(e, pp=pp, bank=bank):
                    for kb in range(KB):
                        inst = e.matmul(pb_t[bank][:],
                                        ipw_ap[pp][:, kb * 128:(kb + 1) * 128],
                                        mxln_ap[kb],
                                        start=(kb == 0), stop=(kb == KB - 1))
                    return inst
                cst("pe", _ipmm, reads=[ipw_buf[pp]] + mxln_buf,
                    writes=[pb_b[bank]])
                cst("dve", lambda e, mb=mb, bank=bank: e.tensor_scalar(
                    cpad_ap[mb][:, 4:4 + C], pb_t[bank][:],
                    ipb_sb[:, mb:mb + 1], None, ALU.add),
                    reads=[pb_b[bank], ipb_b, cpad_buf[mb]], writes=[cpad_buf[mb]])
                if mb + 2 < KB:
                    stream_ip(mb + 2)
                yield
            def _bmm(e):
                for kb in range(KB):
                    inst = e.matmul(pb_t[6][:], bpw_ap[:, kb * 128:(kb + 1) * 128],
                                    cpad_ap[kb][:, 4:4 + C],
                                    start=(kb == 0), stop=(kb == KB - 1))
                return inst
            cst("pe", _bmm, reads=[bpw_buf] + cpad_buf, writes=[pb_b[6]])
            cchain("dve", [
                lambda e: e.tensor_scalar(vs_ap[0], pb_t[6][:], bpb_sb[:], None, ALU.add),
                lambda e: e.tensor_copy(Bh_ap, vs_ap[0]),
                lambda e: e.tensor_tensor(vs_ap[1], vs_ap[0], Bh_ap, ALU.subtract),
                lambda e: e.tensor_copy(Bl_ap, vs_ap[1]),
            ], reads=[pb_b[6], bpb_b], writes=[Bh_buf, Bl_buf, vs_buf[0], vs_buf[1]])
            yield
            stream_dp(0)
            stream_dp(1)

            def delta_unit(r):
                rp = r % 2
                pp = r % 2
                bank = 6 + pp
                s0, s0b = dscr_ap[0], dscr_buf[0]
                def _dmm(e, pp=pp, bank=bank):
                    for kb in range(KB):
                        inst = e.matmul(pb_t[bank][:],
                                        dpw_ap[pp][:, kb * 128:(kb + 1) * 128],
                                        cpad_ap[kb][:, 4:4 + C],
                                        start=(kb == 0), stop=(kb == KB - 1))
                    return inst
                cst("pe", _dmm, reads=[dpw_buf[pp]] + cpad_buf,
                    writes=[pb_b[bank]])
                cchain("act", [
                    (lambda e, bank=bank, s0=s0: e.activation(
                        s0, pb_t[bank][:], AF.Abs)),
                    (lambda e, s0=s0: e.activation(s0, s0, AF.Exp, scale=-1.0)),
                    (lambda e, s0=s0: e.activation(s0, s0, AF.Ln, bias=1.0)),
                ], reads=[pb_b[bank]], writes=[s0b])
                cchain("dve", [
                    (lambda e, bank=bank, rp=rp: e.tensor_scalar(
                        delta_ap[rp], pb_t[bank][:], 0.0, None, ALU.max)),
                    (lambda e, rp=rp, s0=s0: e.tensor_tensor(
                        delta_ap[rp], delta_ap[rp], s0, ALU.add)),
                ], reads=[pb_b[bank], s0b], writes=[delta_buf[rp]])
                if r + 2 < KB:
                    stream_dp(r + 2)

            def convdu_unit(r):
                rp = r % 2
                cs, csb = vs_ap[0], vs_buf[0]
                cvfns = [(lambda e, r=r, cs=cs: e.tensor_scalar(
                    cs, cpad_ap[r][:, 1:1 + C],
                    conv_sb[:, r * 4:r * 4 + 1], None, ALU.mult))]
                for kk in range(1, 4):
                    cvfns.append(lambda e, r=r, cs=cs, kk=kk: e.scalar_tensor_tensor(
                        cs, cpad_ap[r][:, 1 + kk:1 + kk + C],
                        conv_sb[:, r * 4 + kk:r * 4 + kk + 1], cs,
                        ALU.mult, ALU.add))
                cchain("dve", cvfns, reads=[cpad_buf[r], conv_wb], writes=[csb])
                cst("act", lambda e, r=r, cs=cs: e.activation(
                    cs, cs, AF.Silu, bias=cvb_sb[:, r:r + 1]),
                    reads=[csb, cvb_b], writes=[csb])
                cst("dve", lambda e, rp=rp, cs=cs: e.tensor_tensor(
                    du_ap[rp], delta_ap[rp], cs, ALU.mult),
                    reads=[delta_buf[rp], csb, du_buf[rp]], writes=[du_buf[rp]])

            delta_unit(0)
            yield
            convdu_unit(0)
            yield
            for r in range(KB):
                rp = r % 2
                if r + 1 < KB:
                    delta_unit(r + 1)
                    yield
                    convdu_unit(r + 1)
                    yield
                stream_ip(8 + r)
                for pn in range(NST // 2):
                    for spar in range(2):
                        idx = r * NST + 2 * pn + spar
                        bank = 6 + spar
                        def _bc2(e, idx=idx, bank=bank):
                            e.matmul(pb_t[bank][:],
                                     identB[:, idx:idx + 1].broadcast_to([128, 128]),
                                     Bh_ap, start=True, stop=False)
                            return e.matmul(pb_t[bank][:],
                                            identB[:, idx:idx + 1].broadcast_to([128, 128]),
                                            Bl_ap, start=False, stop=True)
                        cst("pe", _bc2, reads=[identB_b, Bh_buf, Bl_buf],
                            writes=[pb_b[bank]])
                    for spar in range(2):
                        idx = r * NST + 2 * pn + spar
                        cst("act", lambda e, rp=rp, idx=idx, spar=spar: e.activation(
                            dA_ap[spar], delta_ap[rp], AF.Exp,
                            scale=A_sb[:, idx:idx + 1]),
                            reads=[delta_buf[rp], A_b, dA_buf[spar]],
                            writes=[dA_buf[spar]])
                    for spar in range(2):
                        idx = r * NST + 2 * pn + spar
                        bank = 6 + spar
                        cst("dve", lambda e, rp=rp, spar=spar, bank=bank, idx=idx:
                            e.scalar_tensor_tensor(
                                dbu_ap[spar], du_ap[rp], c_sb[:, idx:idx + 1],
                                pb_t[bank][:], ALU.mult, ALU.mult),
                            reads=[du_buf[rp], c_b, pb_b[bank], dbu_buf[spar]],
                            writes=[dbu_buf[spar]])
                    for spar in range(2):
                        idx = r * NST + 2 * pn + spar
                        init = 0.0 if ch == 0 else carry[:, idx:idx + 1]
                        cst("dve", lambda e, spar=spar, init=init: e.tensor_tensor_scan(
                            dbu_ap[spar], dA_ap[spar], dbu_ap[spar], init,
                            ALU.mult, ALU.add),
                            reads=[dA_buf[spar], carry_b, dbu_buf[spar]],
                            writes=[dbu_buf[spar]])
                    for spar in range(2):
                        idx = r * NST + 2 * pn + spar
                        yeng = "pool" if spar == 0 else "dve"
                        if pn == 0:
                            cst(yeng, lambda e, spar=spar: e.tensor_copy(
                                y_ap[spar], dbu_ap[spar]),
                                reads=[dbu_buf[spar], y_buf[spar]],
                                writes=[y_buf[spar]])
                        else:
                            cst(yeng, lambda e, spar=spar: e.tensor_tensor(
                                y_ap[spar], y_ap[spar], dbu_ap[spar], ALU.add),
                                reads=[dbu_buf[spar], y_buf[spar]],
                                writes=[y_buf[spar]])
                        cst("pool", lambda e, idx=idx, spar=spar: e.tensor_copy(
                            carry[:, idx:idx + 1], dbu_ap[spar][:, C - 1:C]),
                            reads=[dbu_buf[spar], carry_b], writes=[carry_b])
                    yield
                cst("dve", lambda e: e.tensor_tensor(
                    y_ap[0], y_ap[0], y_ap[1], ALU.add),
                    reads=[y_buf[1], y_buf[0]], writes=[y_buf[0]])
                pp = r % 2
                bank = 6 + pp
                def _vmm(e, pp=pp, bank=bank):
                    for kb in range(KB):
                        inst = e.matmul(pb_t[bank][:],
                                        ipw_ap[pp][:, kb * 128:(kb + 1) * 128],
                                        mxln_ap[kb],
                                        start=(kb == 0), stop=(kb == KB - 1))
                    return inst
                cst("pe", _vmm, reads=[ipw_buf[pp]] + mxln_buf,
                    writes=[pb_b[bank]])
                vsb, vsbb = vs_ap[0], vs_buf[0]
                t3, t3b = vs_ap[1], vs_buf[1]
                mm_, mmb = vs_ap[2], vs_buf[2]
                mb = 8 + r
                cst("dve", lambda e, mb=mb, bank=bank: e.tensor_scalar(
                    vsb, pb_t[bank][:], ipb_sb[:, mb:mb + 1], None, ALU.add),
                    reads=[pb_b[bank], ipb_b, vsbb], writes=[vsbb])
                cchain("act", [
                    lambda e: e.activation(t3, vsb, AF.Abs),
                    lambda e: e.activation(t3, t3, AF.Exp, scale=-1.0),
                    lambda e: e.activation(t3, t3, AF.Ln, bias=1.0),
                ], reads=[vsbb], writes=[t3b])
                cchain("dve", [
                    lambda e: e.tensor_scalar(mm_, vsb, 0.0, None, ALU.min),
                    lambda e: e.tensor_tensor(t3, t3, mm_, ALU.subtract),
                ], reads=[vsbb], writes=[mmb, t3b])
                cst("act", lambda e: e.activation(t3, t3, AF.Exp, scale=-1.0),
                    reads=[t3b], writes=[t3b])
                cchain("pool", [
                    (lambda e: e.tensor_tensor(
                        mm_, y_ap[0], vsb, ALU.mult)),
                    (lambda e: e.tensor_tensor(mm_, mm_, t3, ALU.mult)),
                    (lambda e, r=r, c0=c0: e.tensor_tensor(
                        x_t[r][:, c0:c0 + C], x_t[r][:, c0:c0 + C],
                        mm_, ALU.add)),
                ], reads=[y_buf[0], vsbb, t3b], writes=[mmb, x_b[r][ch]])
                yield

        setup()
        sched.barrier()
        stream_w13(GROUPS[0][0], 0)
        stream_w13(GROUPS[0][0], 1)
        stream_w2(GROUPS[0][0], 0)
        stream_w2(GROUPS[0][0], 1)

        m_gens = [m_chunk_gen(c) for c in range(NCH)]
        m_ready = []
        m_done = [False] * NCH

        def drain_m(k):
            n = 0
            while n < k and m_ready:
                c, g = m_ready[0]
                try:
                    next(g)
                    n += 1
                except StopIteration:
                    m_done[c] = True
                    m_ready.pop(0)

        def drain_m_until_done(c):
            while not m_done[c]:
                drain_m(1)
                if not m_ready:
                    break

        fi0, c0_ = F_ORDER[0]
        for u in f_head_units(fi0, c0_, 0):
            u()
        for k, (fi, c) in enumerate(F_ORDER):
            par = k % 2
            if k + 1 < len(F_ORDER):
                nfi, ncx = F_ORDER[k + 1]
                nhead = f_head_units(nfi, ncx, (k + 1) % 2)
            else:
                nfi, ncx, nhead = None, None, []
            nhq = [0]

            def poke_head():
                if nhq[0] < len(nhead) and (nfi == "f1" or m_done[ncx]):
                    nhead[nhq[0]]()
                    nhq[0] += 1

            gen = f_chunk_gen(fi, c, par)
            for _ in gen:
                drain_m(1 if ZIP_M else 0)
                poke_head()
            if fi == "f1":
                m_ready.append((c, m_gens[c]))
                if not ZIP_M:
                    drain_m_until_done(c)
            if nfi == "f2":
                drain_m_until_done(ncx)
            while nhq[0] < len(nhead):
                nhead[nhq[0]]()
                nhq[0] += 1
            if fi == "f2":
                def _st(e, c=c):
                    for kb in range(KB):
                        e.dma_start(out=yT_h[kb * 128:(kb + 1) * 128, c * C:(c + 1) * C],
                                    in_=x_t[kb][:, c * C:(c + 1) * C]
                                    ).then_inc(SEM["yst"], 16)
                getsem("yst")
                sched.stage("sp", _st, reads=[x_b[kb][c] for kb in range(KB)],
                            dmaq="yst", ndma=KB)
        while m_ready:
            drain_m(1)

        final_qx = sched.cnt["yst"]

        def mk_body(engkey):
            def run(eng):
                for waits, emit, dmaq in sched.plan[engkey]:
                    for k, v in waits.items():
                        eng.wait_ge(SEM[k], v)
                    emit(eng)
                eng.wait_ge(SEM["yst"], final_qx)
            return run

        blk.tensor(mk_body("pe"))
        blk.scalar(mk_body("act"))
        blk.vector(mk_body("dve"))
        blk.gpsimd(mk_body("pool"))
        blk.sync(mk_body("sp"))

    nstages = {k: len(v) for k, v in sched.plan.items()}
    print(f"[kernel] stages={nstages} arena_peak={AR.peak * 4}B/part")
    return nc


_cache = {}


def kernel(**inputs):
    x = np.asarray(inputs["L0_attn_x"])
    cc = np.ascontiguousarray
    shared = {
        "ipT": cc(np.asarray(inputs["L0_attn_in_proj_w"]).T),
        "dpT": cc(np.asarray(inputs["L0_attn_delta_proj_w"]).T),
        "bpT": cc(np.asarray(inputs["L0_attn_b_proj_w"]).T),
        "gwT": cc(np.asarray(inputs["L0_ffn_gate_w"]).T),
        "w1T": cc(np.asarray(inputs["L0_ffn_w1"]).transpose(0, 2, 1)),
        "w3T": cc(np.asarray(inputs["L0_ffn_w3"]).transpose(0, 2, 1)),
        "w2T": cc(np.asarray(inputs["L0_ffn_w2"]).transpose(0, 2, 1)),
        "sw1T": cc(np.asarray(inputs["L0_ffn_sw1"]).T),
        "sw3T": cc(np.asarray(inputs["L0_ffn_sw3"]).T),
        "sw2T": cc(np.asarray(inputs["L0_ffn_sw2"]).T),
        "cT": cc(np.asarray(inputs["L0_attn_c_param"]).T),
        "alog": cc(np.asarray(inputs["L0_attn_a_log"])),
        "convw": cc(np.asarray(inputs["L0_attn_conv_w"]).reshape(D, 4)),
        "ipb": cc(np.asarray(inputs["L0_attn_in_proj_b"]).reshape(16, 128)),
        "convb": cc(np.asarray(inputs["L0_attn_conv_b"]).reshape(8, 128)),
        "bpb": cc(np.asarray(inputs["L0_attn_b_proj_b"]).reshape(1, 128)),
    }
    in_maps = []
    for b in range(8):
        m = dict(shared)
        m["xT"] = cc(x[b].T)
        in_maps.append(m)

    key = (SERIAL, ZIP_M, USE_CACHE)
    if key not in _cache:
        _cache[key] = build_program()
    nc = _cache[key]
    res = run_bass_kernel_spmd(nc, in_maps, core_ids=list(range(8)))
    out = np.empty((8, T, D), np.float32)
    for b in range(8):
        out[b] = res.results[b]["yT"].T
    return out


# version 58
# speedup vs baseline: 1.0534x; 1.0534x over previous
import contextlib
import numpy as np
import concourse.bass as bass
import concourse.mybir as mybir
from concourse.bass_utils import run_bass_kernel_spmd

F32 = mybir.dt.float32
BF16 = mybir.dt.bfloat16
I32 = mybir.dt.int32
AF = mybir.ActivationFunctionType
ALU = mybir.AluOpType
AX = mybir.AxisListType

D = 1024
T = 2048
KB = 8
C = 512
NCH = T // C
H = 512
NE = 16
NST = 16
EPS = 1e-5
ARENA_WORDS = 35400
SERIAL = False
ZIP_M = True
USE_CACHE = True

ENGS = ("pe", "act", "dve", "pool", "sp")

F_ORDER = [("f1", 0), ("f1", 1), ("f1", 2), ("f2", 0),
           ("f1", 3), ("f2", 1), ("f2", 2), ("f2", 3)]
GROUPS = [[i] for i in range(17)]


class Buf:
    __slots__ = ("name", "wr", "rds")

    def __init__(self, name):
        self.name = name
        self.wr = None
        self.rds = []


class Sched:
    def __init__(self):
        self.plan = {k: [] for k in ENGS}
        self.cnt = {k: 0 for k in ENGS}
        self.waited = {k: {} for k in ENGS}
        self.bar = {}

    def stage(self, eng, emit, reads=(), writes=(), dmaq=None, ndma=0):
        if SERIAL:
            self.bar = dict(self.cnt)
        deps = dict(self.bar)

        def need(ev):
            if ev is None:
                return
            k, v = ev
            if deps.get(k, 0) < v:
                deps[k] = v
        for b in reads:
            need(b.wr)
        for b in writes:
            need(b.wr)
            for ev in b.rds:
                need(ev)
        if eng == "pe":
            deps.pop("pe", None)
        w = self.waited[eng]
        fin = {}
        for k, v in deps.items():
            if w.get(k, 0) >= v:
                continue
            w[k] = v
            fin[k] = v
        if dmaq is None:
            self.cnt[eng] += 1
            ev = (eng, self.cnt[eng])
        else:
            assert ndma > 0
            self.cnt[dmaq] = self.cnt.get(dmaq, 0) + 16 * ndma
            ev = (dmaq, self.cnt[dmaq])
        self.plan[eng].append((fin, emit, dmaq))
        for b in writes:
            b.wr = ev
            b.rds = []
        for b in reads:
            b.rds.append(ev)
        return ev

    def barrier(self):
        self.bar = dict(self.cnt)


class Arena:
    def __init__(self, t, words):
        self.t = t
        self.words = words
        self.off = 0
        self.peak = 0

    def alloc(self, name, p, cols, dtype):
        if dtype in (BF16,):
            w = (cols + 1) // 2
        else:
            w = cols
        self.off = (self.off + 15) & ~15
        a = self.off
        self.off += w
        self.peak = max(self.peak, self.off)
        assert self.off <= self.words, f"arena overflow at {name}: {self.off * 4}B"
        view = self.t[0:p, a:a + w]
        if dtype == BF16:
            view = view.bitcast(BF16)[:, 0:cols]
        elif dtype == I32:
            view = view.bitcast(I32)
        return view, Buf(name)


def build_program():
    nc = bass.Bass("TRN2", target_bir_lowering=False, detect_race_conditions=False)

    def din(name, shape):
        return nc.dram_tensor(name, list(shape), F32, kind="ExternalInput").ap()

    xT_h = din("xT", (D, T))
    ipT_h = din("ipT", (D, 2 * D))
    dpT_h = din("dpT", (D, D))
    bpT_h = din("bpT", (D, 128))
    gwT_h = din("gwT", (D, NE))
    w1T_h = din("w1T", (NE, D, H))
    w3T_h = din("w3T", (NE, D, H))
    w2T_h = din("w2T", (NE, H, D))
    sw1T_h = din("sw1T", (D, H))
    sw3T_h = din("sw3T", (D, H))
    sw2T_h = din("sw2T", (H, D))
    cT_h = din("cT", (NST, D))
    alog_h = din("alog", (D, NST))
    convw_h = din("convw", (D, 4))
    ipb_h = din("ipb", (16, 128))
    convb_h = din("convb", (8, 128))
    bpb_h = din("bpb", (1, 128))
    yT_h = nc.dram_tensor("yT", [D, T], F32, kind="ExternalOutput").ap()
    w1c_h = nc.dram_tensor("w1c", [NE + 1, 2, 128, KB * 256], BF16, kind="Internal").ap()
    w3c_h = nc.dram_tensor("w3c", [NE + 1, 2, 128, KB * 256], BF16, kind="Internal").ap()
    w2c_h = nc.dram_tensor("w2c", [NE + 1, 2, 128, 4 * 512], BF16, kind="Internal").ap()
    ipc_h = nc.dram_tensor("ipc", [16, 128, KB * 128], BF16, kind="Internal").ap()
    dpc_h = nc.dram_tensor("dpc", [KB, 128, KB * 128], BF16, kind="Internal").ap()

    sched = Sched()
    es = contextlib.ExitStack()
    with es:
        def sb(name, p, cols, dtype):
            t = es.enter_context(nc.sbuf_tensor(name, [p, cols], dtype))
            return t, Buf(name)

        x_t = []
        x_b = []
        for kb in range(KB):
            t = es.enter_context(nc.sbuf_tensor(f"x{kb}", [128, T], F32))
            x_t.append(t)
            x_b.append([Buf(f"x{kb}_{ch}") for ch in range(NCH)])
        cpool, cpool_b = sb("cpool", 128, 8, F32)
        CONSTS = [0.0, 1.0, -1.0, 0.5, EPS]
        for j, v in enumerate(CONSTS):
            nc.const_aps.register(F32, v, cpool[:, j:j + 1])
        ones1f, ones1f_b = sb("ones1f", 1, 128, F32)
        ones128, ones128_b = sb("ones128", 128, 1, F32)
        identF, identF_b = sb("identF", 128, 128, F32)
        identB, identB_b = sb("identB", 128, 128, BF16)
        iotat, iotat_b = sb("iotat", 128, 128, I32)
        A_sb, A_b = sb("A_sb", 128, 128, F32)
        c_sb, c_b = sb("c_sb", 128, 128, F32)
        conv_sb, conv_wb = sb("conv_sb", 128, 32, F32)
        ipb_sb, ipb_b = sb("ipb_sb", 128, 16, F32)
        cvb_sb, cvb_b = sb("cvb_sb", 128, 8, F32)
        bpb_sb, bpb_b = sb("bpb_sb", 128, 1, F32)
        gw_sb, gw_b = sb("gw_sb", 128, 8 * NE, F32)
        Sbc_sb, Sbc_b = sb("Sbc_sb", 128, NE, F32)
        carry, carry_b = sb("carry", 128, 128, F32)
        btmp, btmp_b = sb("btmp", 16, 128, F32)
        srow, srow_b = sb("srow", 1, 128, F32)

        arena_t = es.enter_context(nc.sbuf_tensor("arena", [128, ARENA_WORDS], F32))
        AR = Arena(arena_t, ARENA_WORDS)

        pb_t, pb_b = [], []
        for i in range(8):
            t = es.enter_context(nc.psum_tensor(f"pb{i}", [128, 512], F32))
            pb_t.append(t)
            pb_b.append(Buf(f"pb{i}"))

        SEM = {}
        for k in ENGS:
            SEM[k] = es.enter_context(nc.semaphore(f"sem_{k}"))

        def getsem(q):
            if q not in SEM:
                SEM[q] = es.enter_context(nc.semaphore(f"sem_{q}"))
            return SEM[q]
        blk = es.enter_context(nc.Block())

        def cst(eng, fn, reads=(), writes=()):
            def emit(e, fn=fn, eng=eng):
                inst = fn(e)
                inst.then_inc(SEM[eng], 1)
            return sched.stage(eng, emit, reads, writes)

        def cchain(eng, fns, reads=(), writes=()):
            rw = list(reads) + list(writes)
            for fn in fns:
                cst(eng, fn, reads=rw, writes=list(writes))

        def dst(eng, q, transfers, reads=(), writes=()):
            getsem(q)
            def emit(e, transfers=transfers, q=q):
                for dstp, srcp, kw in transfers:
                    e.dma_start(out=dstp, in_=srcp, **kw).then_inc(SEM[q], 16)
            return sched.stage(eng, emit, reads, writes, dmaq=q, ndma=len(transfers))

        _ct_off = AR.off
        ct16, ct16_b = AR.alloc("ct16", 16, D, F32)
        AR.off = _ct_off
        xln_ap = [[None] * KB for _ in range(2)]
        xln_buf = [[None] * KB for _ in range(2)]
        for i in range(2):
            for kb in range(KB):
                xln_ap[i][kb], xln_buf[i][kb] = AR.alloc(f"xln{i}_{kb}", 128, C, BF16)
        act_ap = [[None] * 4 for _ in range(2)]
        act_buf = [[None] * 4 for _ in range(2)]
        for s in range(2):
            for hb in range(4):
                act_ap[s][hb], act_buf[s][hb] = AR.alloc(f"act{s}_{hb}", 128, C, BF16)
        w1h_ap, w1h_buf, w3h_ap, w3h_buf = [], [], [], []
        for i in range(2):
            a, b = AR.alloc(f"w1h{i}", 128, KB * 256, BF16)
            w1h_ap.append(a)
            w1h_buf.append(b)
            a, b = AR.alloc(f"w3h{i}", 128, KB * 256, BF16)
            w3h_ap.append(a)
            w3h_buf.append(b)
        w2h_ap, w2h_buf = [], []
        for i in range(3):
            a, b = AR.alloc(f"w2h{i}", 128, 4 * 512, BF16)
            w2h_ap.append(a)
            w2h_buf.append(b)
        cb_ap, cb_buf = [], []
        for i in range(2):
            a, b = AR.alloc(f"cb{i}", 128, C, F32)
            cb_ap.append(a)
            cb_buf.append(b)
        sil_ap, sil_buf, h3c_ap, h3c_buf = [], [], [], []
        for i in range(2):
            a, b = AR.alloc(f"sil{i}", 128, C, F32)
            sil_ap.append(a)
            sil_buf.append(b)
            a, b = AR.alloc(f"h3c{i}", 128, C, F32)
            h3c_ap.append(a)
            h3c_buf.append(b)
        comb_ap, comb_buf = [], []
        for i in range(2):
            a, b = AR.alloc(f"comb{i}", 16, C, BF16)
            comb_ap.append(a)
            comb_buf.append(b)
        rowsF_ap, rowsF_buf = AR.alloc("rowsF", 1, 2 * C, F32)
        sq_ap, sq_buf = [], []
        for i in range(2):
            a, b = AR.alloc(f"sqln{i}", 128, C, F32)
            sq_ap.append(a)
            sq_buf.append(b)
        tmp_ap, tmp_buf = [], []
        for i in range(2):
            a, b = AR.alloc(f"tmpln{i}", 128, 256, F32)
            tmp_ap.append(a)
            tmp_buf.append(b)
        tk_n1, tk_n1_b = AR.alloc("tkn1", 128, 4 * NE, F32)
        TKB = {}
        for nmv in ("l", "el", "t1", "sb"):
            TKB[nmv] = AR.alloc(f"tkb{nmv}", 128, 4 * NE, F32)
        TKB["vv"] = TKB["l"]
        TKB["eqi"] = AR.alloc("tkbeqi", 128, 4 * NE, I32)
        TKB["c1"] = AR.alloc("tkbc1", 128, 4, F32)
        TKB["c2"] = AR.alloc("tkbc2", 128, 4, F32)
        TKB["rc"] = AR.alloc("tkbrc", 128, 8, F32)
        mxln_ap, mxln_buf = [], []
        for kb in range(KB):
            a, b = AR.alloc(f"mxln{kb}", 128, C, BF16)
            mxln_ap.append(a)
            mxln_buf.append(b)
        cpad_ap, cpad_buf = [], []
        for r in range(KB):
            a, b = AR.alloc(f"cpad{r}", 128, C + 4, BF16)
            cpad_ap.append(a)
            cpad_buf.append(b)
        ipw_ap, ipw_buf, dpw_ap, dpw_buf = [], [], [], []
        for i in range(2):
            a, b = AR.alloc(f"ipw{i}", 128, KB * 128, BF16)
            ipw_ap.append(a)
            ipw_buf.append(b)
            a, b = AR.alloc(f"dpw{i}", 128, KB * 128, BF16)
            dpw_ap.append(a)
            dpw_buf.append(b)
        bpw_ap, bpw_buf = AR.alloc("bpw", 128, KB * 128, BF16)
        Bh_ap, Bh_buf = AR.alloc("Bh", 128, C, BF16)
        Bl_ap, Bl_buf = AR.alloc("Bl", 128, C, BF16)
        delta_ap, delta_buf, du_ap, du_buf = [], [], [], []
        dA_ap, dA_buf, dbu_ap, dbu_buf = [], [], [], []
        y_ap, y_buf = [], []
        for i in range(2):
            a, b = AR.alloc(f"delta{i}", 128, C, F32)
            delta_ap.append(a)
            delta_buf.append(b)
            a, b = AR.alloc(f"du{i}", 128, C, F32)
            du_ap.append(a)
            du_buf.append(b)
            a, b = AR.alloc(f"dA{i}", 128, C, F32)
            dA_ap.append(a)
            dA_buf.append(b)
            a, b = AR.alloc(f"dbu{i}", 128, C, F32)
            dbu_ap.append(a)
            dbu_buf.append(b)
            a, b = AR.alloc(f"y{i}", 128, C, F32)
            y_ap.append(a)
            y_buf.append(b)
        dscr_ap, dscr_buf = [], []
        for i in range(1):
            a, b = AR.alloc(f"dscr{i}", 128, C, F32)
            dscr_ap.append(a)
            dscr_buf.append(b)
        vs_ap, vs_buf = [], []
        for i in range(3):
            a, b = AR.alloc(f"vs{i}", 128, C, F32)
            vs_ap.append(a)
            vs_buf.append(b)
        rowsM_ap, rowsM_buf = AR.alloc("rowsM", 1, 2 * C, F32)

        def setup():
            cst("dve", lambda e: e.memset(cpool[:], 0.0), writes=[cpool_b])
            for j, v in enumerate(CONSTS):
                if v != 0.0:
                    cst("dve", lambda e, j=j, v=v: e.memset(cpool[:, j:j + 1], v),
                        reads=[cpool_b], writes=[cpool_b])
            cst("dve", lambda e: e.memset(ones1f[:], 1.0), writes=[ones1f_b])
            cst("dve", lambda e: e.memset(ones128[:], 1.0), writes=[ones128_b])
            cst("dve", lambda e: e.memset(carry[:], 0.0), writes=[carry_b])
            cst("dve", lambda e: e.memset(tk_n1, -1.0), writes=[tk_n1_b])
            cst("pool", lambda e: e.iota(iotat[:], pattern=[[1, 128]], base=0,
                                         channel_multiplier=-1), writes=[iotat_b])
            cst("dve", lambda e: e.tensor_single_scalar(identF[:], iotat[:], 0, ALU.is_equal),
                reads=[iotat_b], writes=[identF_b])
            cst("dve", lambda e: e.tensor_copy(identB[:], identF[:]),
                reads=[identF_b], writes=[identB_b])
            for kb in range(KB):
                def _ld(e, kb=kb):
                    e.dma_start(out=x_t[kb][:], in_=xT_h[kb * 128:(kb + 1) * 128, :]
                                ).then_inc(SEM[f"ldx{kb}"], 16)
                getsem(f"ldx{kb}")
                sched.stage("sp", _ld, writes=x_b[kb], dmaq=f"ldx{kb}", ndma=1)
            dst("sp", "lda", [(A_sb[:].rearrange("p (r n) -> p r n", n=NST),
                              alog_h.rearrange("(r p) n -> p r n", p=128), {})],
                writes=[A_b])
            dst("sp", "ldct", [(ct16[:], cT_h, {})], writes=[ct16_b])
            dst("sp", "ldcv", [(conv_sb[:].rearrange("p (r k) -> p r k", k=4),
                              convw_h.rearrange("(r p) k -> p r k", p=128), {})],
                writes=[conv_wb])
            dst("sp", "ldgw", [(gw_sb[:].rearrange("p (r n) -> p r n", n=NE),
                              gwT_h.rearrange("(r p) n -> p r n", p=128), {})],
                writes=[gw_b])
            cst("act", lambda e: e.activation(A_sb[:], A_sb[:], AF.Exp),
                reads=[A_b], writes=[A_b])
            cst("dve", lambda e: e.tensor_single_scalar(A_sb[:], A_sb[:], -1.0, ALU.mult),
                reads=[A_b], writes=[A_b])
            for r in range(KB):
                cst("pe", lambda e, r=r: e.transpose(
                    pb_t[r % 2][:, 0:16], ct16[0:16, r * 128:(r + 1) * 128],
                    identF[0:16, 0:16]),
                    reads=[ct16_b, identF_b], writes=[pb_b[r % 2]])
                cst("dve", lambda e, r=r: e.tensor_copy(
                    c_sb[:, r * 16:(r + 1) * 16], pb_t[r % 2][:, 0:16]),
                    reads=[pb_b[r % 2], c_b], writes=[c_b])
            dst("sp", "ldb1", [(btmp[:], ipb_h, {})], writes=[btmp_b])
            cst("pe", lambda e: e.transpose(pb_t[2][:, 0:16], btmp[:], identF[0:16, 0:16]),
                reads=[btmp_b, identF_b], writes=[pb_b[2]])
            cst("dve", lambda e: e.tensor_copy(ipb_sb[:], pb_t[2][:, 0:16]),
                reads=[pb_b[2]], writes=[ipb_b])
            dst("sp", "ldb2", [(btmp[0:8, :], convb_h, {})], reads=[btmp_b], writes=[btmp_b])
            cst("pe", lambda e: e.transpose(pb_t[3][:, 0:8], btmp[0:8, :], identF[0:8, 0:8]),
                reads=[btmp_b, identF_b], writes=[pb_b[3]])
            cst("dve", lambda e: e.tensor_copy(cvb_sb[:], pb_t[3][:, 0:8]),
                reads=[pb_b[3]], writes=[cvb_b])
            dst("sp", "ldb3", [(srow[:], bpb_h, {})], writes=[srow_b])
            cst("pe", lambda e: e.transpose(pb_t[4][:, 0:1], srow[:], identF[0:1, 0:1]),
                reads=[srow_b, identF_b], writes=[pb_b[4]])
            cst("dve", lambda e: e.tensor_copy(bpb_sb[:], pb_t[4][:, 0:1]),
                reads=[pb_b[4]], writes=[bpb_b])
            def _smm(e):
                for kb in range(KB):
                    inst = e.matmul(pb_t[5][0:1, 0:NE], ones128[:],
                                    gw_sb[:, kb * NE:(kb + 1) * NE],
                                    start=(kb == 0), stop=(kb == KB - 1))
                return inst
            cst("pe", _smm, reads=[ones128_b, gw_b], writes=[pb_b[5]])
            cst("dve", lambda e: e.tensor_copy(srow[0:1, 0:NE], pb_t[5][0:1, 0:NE]),
                reads=[pb_b[5], srow_b], writes=[srow_b])
            cst("pe", lambda e: e.matmul(pb_t[6][:, 0:NE], ones1f[:], srow[0:1, 0:NE],
                                         start=True, stop=True),
                reads=[ones1f_b, srow_b], writes=[pb_b[6]])
            cst("dve", lambda e: e.tensor_copy(Sbc_sb[:], pb_t[6][:, 0:NE]),
                reads=[pb_b[6]], writes=[Sbc_b])
            for r in range(KB):
                cst("dve", lambda e, r=r: e.memset(cpad_ap[r][:, 0:4], 0.0),
                    writes=[cpad_buf[r]])
            dst("pool", "ldbp",
                [(bpw_ap.rearrange("p (k m) -> p k m", k=KB),
                  bpT_h.rearrange("(k p) m -> p k m", p=128), {})],
                writes=[bpw_buf])

        def ln_units(c0, xlnv, xlnb, rows_ap, rows_buf, bank):
            pb, pbb = pb_t[bank], pb_b[bank]
            rstd_r = rows_ap[0:1, 0:C]
            nmr_r = rows_ap[0:1, C:2 * C]
            ch = c0 // C

            def emit_sq(kb):
                sp_ = kb % 2
                cst("pool", lambda e, kb=kb, sp_=sp_: e.tensor_tensor(
                    sq_ap[sp_], x_t[kb][:, c0:c0 + C], x_t[kb][:, c0:c0 + C],
                    ALU.mult),
                    reads=[x_b[kb][ch]], writes=[sq_buf[sp_]])

            def emit_mm(kb):
                sp_ = kb % 2
                def _mm(e, kb=kb, sp_=sp_):
                    e.matmul(pb[0:1, :], ones128[:], x_t[kb][:, c0:c0 + C],
                             start=(kb == 0), stop=(kb == KB - 1))
                    return e.matmul(pb[32:33, :], ones128[:], sq_ap[sp_],
                                    start=(kb == 0), stop=(kb == KB - 1))
                cst("pe", _mm, reads=[ones128_b, x_b[kb][ch], sq_buf[sp_]],
                    writes=[pbb])

            def sums():
                emit_sq(0)
                emit_sq(1)
                for kb in range(KB):
                    emit_mm(kb)
                    if kb + 2 < KB:
                        emit_sq(kb + 2)

            def rows_chain():
                cchain("dve", [
                    lambda e: e.tensor_single_scalar(nmr_r, pb[0:1, :], 1.0 / D, ALU.mult),
                    lambda e: e.tensor_single_scalar(rstd_r, pb[32:33, :], 1.0 / D, ALU.mult),
                    lambda e: e.scalar_tensor_tensor(tmp_ap[0][0:1, 0:256], nmr_r[0:1, 0:256],
                                                     1.0, nmr_r[0:1, 0:256], ALU.mult, ALU.mult),
                    lambda e: e.scalar_tensor_tensor(tmp_ap[1][0:1, 0:256], nmr_r[0:1, 256:512],
                                                     1.0, nmr_r[0:1, 256:512], ALU.mult, ALU.mult),
                    lambda e: e.tensor_tensor(rstd_r[0:1, 0:256], rstd_r[0:1, 0:256],
                                              tmp_ap[0][0:1, 0:256], ALU.subtract),
                    lambda e: e.tensor_tensor(rstd_r[0:1, 256:512], rstd_r[0:1, 256:512],
                                              tmp_ap[1][0:1, 0:256], ALU.subtract),
                ], reads=[pbb], writes=[rows_buf, tmp_buf[0], tmp_buf[1]])
                cchain("act", [
                    lambda e: e.activation(rstd_r, rstd_r, AF.Ln, bias=EPS),
                    lambda e: e.activation(rstd_r, rstd_r, AF.Exp, scale=-0.5),
                ], writes=[rows_buf])
                cst("dve", lambda e: e.scalar_tensor_tensor(
                    nmr_r, nmr_r, -1.0, rstd_r, ALU.mult, ALU.mult), writes=[rows_buf])

            def apply_half(hf):
                co = hf * 256
                def _bc(e, co=co):
                    e.matmul(pb[:, 0:256], ones1f[:], rstd_r[0:1, co:co + 256],
                             start=True, stop=True)
                    return e.matmul(pb[:, 256:512], ones1f[:], nmr_r[0:1, co:co + 256],
                                    start=True, stop=True)
                cst("pe", _bc, reads=[ones1f_b, rows_buf], writes=[pbb])
                for kb in range(KB):
                    tpar = kb % 2
                    cst("dve", lambda e, kb=kb, co=co, tpar=tpar: e.tensor_tensor(
                        tmp_ap[tpar],
                        x_t[kb][:, c0 + co:c0 + co + 256], pb[:, 0:256], ALU.mult),
                        reads=[x_b[kb][ch], pbb], writes=[tmp_buf[tpar]])
                    cst("dve", lambda e, kb=kb, co=co, tpar=tpar: e.tensor_tensor(
                        xlnv[kb][:, co:co + 256],
                        tmp_ap[tpar], pb[:, 256:512], ALU.add),
                        reads=[tmp_buf[tpar], pbb], writes=[xlnb[kb]])

            return [sums, rows_chain, lambda: apply_half(0), lambda: apply_half(1)]

        def routing_units(c0, par):
            ch = c0 // C
            pb, pbb = pb_t[3], pb_b[3]
            rstd_r = rowsF_ap[0:1, 0:C]
            nmr_r = rowsF_ap[0:1, C:2 * C]
            l_ap, l_b = TKB["l"]
            el_ap, el_b = TKB["el"]
            vv_ap, vv_b = TKB["vv"]
            t1_ap, t1_b = TKB["t1"]
            sb_ap, sb_b = TKB["sb"]
            eqi_ap, eqi_b = TKB["eqi"]
            c1_ap, c1_b = TKB["c1"]
            c2_ap, c2_b = TKB["c2"]
            rc_ap, rc_b = TKB["rc"]

            def seg(ap):
                return ap.rearrange("p (t n) -> p t n", n=NE)

            def colb(ap, j):
                return ap

            def unit_a():
                for tb in range(4):
                    tc = c0 + tb * 128
                    def _gmm(e, tc=tc, tb=tb):
                        for kb in range(KB):
                            inst = e.matmul(pb[:, 16 * tb:16 * tb + NE],
                                            x_t[kb][:, tc:tc + 128],
                                            gw_sb[:, kb * NE:(kb + 1) * NE],
                                            start=(kb == 0), stop=(kb == KB - 1))
                        return inst
                    cst("pe", _gmm, reads=[gw_b] + [x_b[kb][ch] for kb in range(KB)],
                        writes=[pbb])
                for tb in range(4):
                    def _tr(e, tb=tb):
                        e.transpose(pb[:, 64 + 2 * tb:65 + 2 * tb],
                                    rstd_r[0:1, tb * 128:(tb + 1) * 128],
                                    identF[0:1, 0:1])
                        return e.transpose(pb[:, 65 + 2 * tb:66 + 2 * tb],
                                           nmr_r[0:1, tb * 128:(tb + 1) * 128],
                                           identF[0:1, 0:1])
                    cst("pe", _tr, reads=[rowsF_buf, identF_b], writes=[pbb])
                cst("dve", lambda e: e.tensor_copy(rc_ap, pb[:, 64:72]),
                    reads=[pbb, rc_b], writes=[rc_b])

            def unit_b():
                rc3 = rc_ap.rearrange("p (t c) -> p t c", c=2)
                cchain("dve", [
                    lambda e: e.tensor_tensor(
                        seg(l_ap), pb[:, 0:64].rearrange("p (t n) -> p t n", n=NE),
                        rc3[:, :, 0:1].broadcast_to([128, 4, NE]), ALU.mult),
                    lambda e: e.tensor_tensor(
                        seg(sb_ap), Sbc_sb[:].unsqueeze(1).broadcast_to([128, 4, NE]),
                        rc3[:, :, 1:2].broadcast_to([128, 4, NE]), ALU.mult),
                    lambda e: e.tensor_tensor(l_ap, l_ap, sb_ap, ALU.add),
                    lambda e: e.reduce_max(c1_ap, seg(l_ap), axis=AX.X, negate=True),
                    lambda e: e.tensor_tensor(
                        seg(l_ap), seg(l_ap),
                        c1_ap.unsqueeze(2).broadcast_to([128, 4, NE]), ALU.add),
                ], reads=[pbb, Sbc_b],
                   writes=[l_b, sb_b, c1_b, rc_b])
                cst("act", lambda e: e.activation(el_ap, l_ap, AF.Exp),
                    reads=[l_b, el_b], writes=[el_b])
                fns = [lambda e: e.tensor_copy(vv_ap, el_ap)]
                for _ in range(3):
                    fns += [
                        lambda e: e.reduce_max(c2_ap, seg(vv_ap), axis=AX.X),
                        lambda e: e.tensor_tensor(
                            seg(eqi_ap), seg(vv_ap),
                            c2_ap.unsqueeze(2).broadcast_to([128, 4, NE]),
                            ALU.is_equal),
                        lambda e: e.select(vv_ap, eqi_ap, tk_n1, vv_ap),
                    ]
                fns += [
                    lambda e: e.reduce_max(c2_ap, seg(vv_ap), axis=AX.X),
                    lambda e: e.tensor_tensor(
                        seg(t1_ap), seg(el_ap),
                        c2_ap.unsqueeze(2).broadcast_to([128, 4, NE]), ALU.is_ge),
                    lambda e: e.tensor_tensor(t1_ap, el_ap, t1_ap, ALU.mult),
                    lambda e: e.reduce_sum(c1_ap, seg(t1_ap), axis=AX.X),
                    lambda e: e.reciprocal(c1_ap, c1_ap),
                    lambda e: e.tensor_tensor(
                        seg(t1_ap), seg(t1_ap),
                        c1_ap.unsqueeze(2).broadcast_to([128, 4, NE]), ALU.mult),
                ]
                cchain("dve", fns,
                       reads=[el_b, tk_n1_b],
                       writes=[vv_b, eqi_b, c1_b, c2_b, t1_b])

            def unit_c():
                for tb in range(4):
                    co = 256 + 128 * (tb % 2)
                    cst("pe", lambda e, tb=tb, co=co: e.transpose(
                        pb[0:16, co:co + 128], t1_ap[:, 16 * tb:16 * tb + NE],
                        identF[:]),
                        reads=[t1_b, identF_b], writes=[pbb])
                    cst("dve", lambda e, tb=tb, co=co: e.tensor_copy(
                        comb_ap[par][:, tb * 128:(tb + 1) * 128], pb[0:16, co:co + 128]),
                        reads=[pbb, comb_buf[par]], writes=[comb_buf[par]])

            return [unit_a, unit_b, unit_c]

        wcnt = [0]
        cache_buf = {}

        def stream_w13(e_, hf):
            wp = wcnt[0] % 2
            wcnt[0] += 1
            key = ("w13", e_, hf)
            if (not USE_CACHE) or key not in cache_buf:
                if e_ < NE:
                    s1, s3 = w1T_h[e_], w3T_h[e_]
                else:
                    s1, s3 = sw1T_h, sw3T_h
                co = hf * 256
                dst("pool", f"w13c_{wp}",
                    [(w1h_ap[wp].rearrange("p (k h) -> p k h", k=KB),
                      s1[:, co:co + 256].rearrange("(k p) h -> p k h", p=128), {}),
                     (w3h_ap[wp].rearrange("p (k h) -> p k h", k=KB),
                      s3[:, co:co + 256].rearrange("(k p) h -> p k h", p=128), {})],
                    writes=[w1h_buf[wp], w3h_buf[wp]])
                cb13 = Buf(f"c13_{e_}_{hf}")
                dst("sp", f"wb13_{wp}",
                    [(w1c_h[e_, hf], w1h_ap[wp], {}),
                     (w3c_h[e_, hf], w3h_ap[wp], {})],
                    reads=[w1h_buf[wp], w3h_buf[wp]], writes=[cb13])
                cache_buf[key] = cb13
            else:
                dst("sp", f"w13_{wp}",
                    [(w1h_ap[wp], w1c_h[e_, hf], {}),
                     (w3h_ap[wp], w3c_h[e_, hf], {})],
                    reads=[cache_buf[key]], writes=[w1h_buf[wp], w3h_buf[wp]])
            return wp

        w2cnt = [0]

        def stream_w2(e_, th):
            wp = w2cnt[0] % 3
            w2cnt[0] += 1
            key = ("w2", e_, th)
            if (not USE_CACHE) or key not in cache_buf:
                s2 = w2T_h[e_] if e_ < NE else sw2T_h
                co = th * 512
                dst("pool", f"w2c_{wp}",
                    [(w2h_ap[wp].rearrange("p (k h) -> p k h", k=4),
                      s2[:, co:co + 512].rearrange("(k p) h -> p k h", p=128), {})],
                    writes=[w2h_buf[wp]])
                cb2 = Buf(f"c2_{e_}_{th}")
                dst("sp", f"wb2_{wp}", [(w2c_h[e_, th], w2h_ap[wp], {})],
                    reads=[w2h_buf[wp]], writes=[cb2])
                cache_buf[key] = cb2
            else:
                dst("act", f"w2_{wp}", [(w2h_ap[wp], w2c_h[e_, th], {})],
                    reads=[cache_buf[key]], writes=[w2h_buf[wp]])
            return wp

        def f_chunk_gen(fi, c, par):
            c0 = c * C
            ch = c
            rot = [0]
            FBANKS = [0, 1, 2, 4, 5]

            def nb():
                b = FBANKS[rot[0] % len(FBANKS)]
                rot[0] += 1
                return b

            flat13 = []
            for gi, grp in enumerate(GROUPS):
                for si, e_ in enumerate(grp):
                    flat13.append((e_, 0))
                    flat13.append((e_, 1))
            p13 = {}
            p13[0] = stream_w13(GROUPS[0][0], 0)
            p13[1] = stream_w13(GROUPS[0][0], 1)
            next13 = [2]

            def adv13():
                if next13[0] < len(flat13):
                    e_, hf = flat13[next13[0]]
                    p13[next13[0]] = stream_w13(e_, hf)
                    next13[0] += 1

            flat2 = []
            for gi, grp in enumerate(GROUPS):
                for th in range(2):
                    flat2.append((gi, th, grp[-1]))
            p2 = {}
            next2 = [0]

            def adv2():
                if next2[0] < len(flat2):
                    gi, th, e_last = flat2[next2[0]]
                    p2[next2[0]] = stream_w2(e_last, th)
                    next2[0] += 1
            adv2()
            adv2()

            EXP = [g[0] for g in GROUPS]
            k13 = [0]

            def hpart(j, e_):
                aset = j % 2
                if e_ < NE:
                    b = nb()
                    cst("pe", lambda e, e_=e_, b=b: e.matmul(
                        pb_t[b][:], identB[0:16, e_:e_ + 1].broadcast_to([16, 128]),
                        comb_ap[par][:], start=True, stop=True),
                        reads=[identB_b, comb_buf[par]], writes=[pb_b[b]])
                    cst("act", lambda e, aset=aset, b=b: e.copy(cb_ap[aset], pb_t[b][:]),
                        reads=[pb_b[b], cb_buf[aset]], writes=[cb_buf[aset]])
                for hb in range(4):
                    hf = hb // 2
                    idx13 = k13[0] * 2 + hf
                    wp = p13[idx13]
                    b1 = nb()
                    b3 = nb()
                    hoff = (hb % 2) * 128
                    def _h1(e, wp=wp, hoff=hoff, b1=b1):
                        for kb in range(KB):
                            inst = e.matmul(
                                pb_t[b1][:],
                                w1h_ap[wp][:, kb * 256 + hoff: kb * 256 + hoff + 128],
                                xln_ap[par][kb], start=(kb == 0), stop=(kb == KB - 1))
                        return inst
                    cst("pe", _h1, reads=[w1h_buf[wp]] + xln_buf[par],
                        writes=[pb_b[b1]])
                    def _h3(e, wp=wp, hoff=hoff, b3=b3):
                        for kb in range(KB):
                            inst = e.matmul(
                                pb_t[b3][:],
                                w3h_ap[wp][:, kb * 256 + hoff: kb * 256 + hoff + 128],
                                xln_ap[par][kb], start=(kb == 0), stop=(kb == KB - 1))
                        return inst
                    cst("pe", _h3, reads=[w3h_buf[wp]] + xln_buf[par],
                        writes=[pb_b[b3]])
                    spar = hb % 2
                    cst("act", lambda e, b1=b1, spar=spar: e.activation(
                        sil_ap[spar], pb_t[b1][:], AF.Silu),
                        reads=[pb_b[b1], sil_buf[spar]], writes=[sil_buf[spar]])
                    if e_ < NE:
                        cst("dve", lambda e, b3=b3, spar=spar, aset=aset:
                            e.tensor_tensor(h3c_ap[spar], pb_t[b3][:],
                                            cb_ap[aset], ALU.mult),
                            reads=[pb_b[b3], cb_buf[aset], h3c_buf[spar]],
                            writes=[h3c_buf[spar]])
                    else:
                        cst("dve", lambda e, b3=b3, spar=spar: e.tensor_copy(
                            h3c_ap[spar], pb_t[b3][:]),
                            reads=[pb_b[b3], h3c_buf[spar]], writes=[h3c_buf[spar]])
                    cst("pool", lambda e, aset=aset, hb=hb, spar=spar:
                        e.tensor_tensor(act_ap[aset][hb], sil_ap[spar],
                                        h3c_ap[spar], ALU.mult),
                        reads=[sil_buf[spar], h3c_buf[spar], act_buf[aset][hb]],
                        writes=[act_buf[aset][hb]])
                    if hb == 1 or hb == 3:
                        adv13()
                    yield
                k13[0] += 1

            def epart(j, e_):
                aset = j % 2
                for db in range(KB):
                    th = db // 4
                    wp2 = p2[j * 2 + th]
                    eb = nb()
                    doff = (db % 4) * 128
                    def _eo(e, wp2=wp2, doff=doff, eb=eb, aset=aset):
                        for hb in range(4):
                            inst = e.matmul(
                                pb_t[eb][:],
                                w2h_ap[wp2][:, hb * 512 + doff: hb * 512 + doff + 128],
                                act_ap[aset][hb],
                                start=(hb == 0), stop=(hb == 3))
                        return inst
                    cst("pe", _eo,
                        reads=[w2h_buf[wp2]] + [act_buf[aset][hb] for hb in range(4)],
                        writes=[pb_b[eb]])
                    cst("dve", lambda e, db=db, eb=eb: e.scalar_tensor_tensor(
                        x_t[db][:, c0:c0 + C], pb_t[eb][:], 0.5,
                        x_t[db][:, c0:c0 + C], ALU.mult, ALU.add),
                        reads=[pb_b[eb], x_b[db][ch]], writes=[x_b[db][ch]])
                    if db == 3:
                        adv2()
                    if db == 7:
                        adv2()
                    yield

            yield from hpart(0, EXP[0])
            for j in range(1, len(EXP)):
                yield from hpart(j, EXP[j])
                yield from epart(j - 1, EXP[j - 1])
            yield from epart(len(EXP) - 1, EXP[-1])

        def f_head_units(fi, c, par):
            c0 = c * C
            lu = ln_units(c0, xln_ap[par], xln_buf[par], rowsF_ap, rowsF_buf, 3)
            ru = routing_units(c0, par)
            skip = lambda: None
            u = [lu[0], lu[1], skip, skip, lu[2], lu[3],
                 ru[0], ru[1], skip, skip, ru[2]]
            return u

        def m_chunk_gen(c):
            c0 = c * C
            ch = c

            def stream_ip(mb):
                pp = mb % 2
                key = ("ip", mb)
                if (not USE_CACHE) or key not in cache_buf:
                    dst("pool", f"mipc_{pp}",
                        [(ipw_ap[pp].rearrange("p (k m) -> p k m", k=KB),
                          ipT_h[:, mb * 128:(mb + 1) * 128].rearrange(
                              "(k p) m -> p k m", p=128), {})],
                        writes=[ipw_buf[pp]])
                    cbi = Buf(f"cip_{mb}")
                    dst("sp", f"wbip_{pp}", [(ipc_h[mb], ipw_ap[pp], {})],
                        reads=[ipw_buf[pp]], writes=[cbi])
                    cache_buf[key] = cbi
                else:
                    dst("sp", f"mip_{pp}", [(ipw_ap[pp], ipc_h[mb], {})],
                        reads=[cache_buf[key]], writes=[ipw_buf[pp]])

            def stream_dp(r):
                pp = r % 2
                key = ("dp", r)
                if (not USE_CACHE) or key not in cache_buf:
                    dst("pool", f"mdpc_{pp}",
                        [(dpw_ap[pp].rearrange("p (k m) -> p k m", k=KB),
                          dpT_h[:, r * 128:(r + 1) * 128].rearrange(
                              "(k p) m -> p k m", p=128), {})],
                        writes=[dpw_buf[pp]])
                    cbd = Buf(f"cdp_{r}")
                    dst("sp", f"wbdp_{pp}", [(dpc_h[r], dpw_ap[pp], {})],
                        reads=[dpw_buf[pp]], writes=[cbd])
                    cache_buf[key] = cbd
                else:
                    dst("sp", f"mdp_{pp}", [(dpw_ap[pp], dpc_h[r], {})],
                        reads=[cache_buf[key]], writes=[dpw_buf[pp]])

            for u in ln_units(c0, mxln_ap, mxln_buf, rowsM_ap, rowsM_buf, 6):
                u()
                yield
            if ch > 0:
                for r in range(KB):
                    cst("dve", lambda e, r=r: e.tensor_copy(
                        cpad_ap[r][:, 1:4], cpad_ap[r][:, C + 1:C + 4]),
                        reads=[cpad_buf[r]], writes=[cpad_buf[r]])
            yield
            stream_ip(0)
            stream_ip(1)
            for mb in range(KB):
                pp = mb % 2
                bank = 6 + pp
                def _ipmm(e, pp=pp, bank=bank):
                    for kb in range(KB):
                        inst = e.matmul(pb_t[bank][:],
                                        ipw_ap[pp][:, kb * 128:(kb + 1) * 128],
                                        mxln_ap[kb],
                                        start=(kb == 0), stop=(kb == KB - 1))
                    return inst
                cst("pe", _ipmm, reads=[ipw_buf[pp]] + mxln_buf,
                    writes=[pb_b[bank]])
                cst("dve", lambda e, mb=mb, bank=bank: e.tensor_scalar(
                    cpad_ap[mb][:, 4:4 + C], pb_t[bank][:],
                    ipb_sb[:, mb:mb + 1], None, ALU.add),
                    reads=[pb_b[bank], ipb_b, cpad_buf[mb]], writes=[cpad_buf[mb]])
                if mb + 2 < KB:
                    stream_ip(mb + 2)
                yield
            def _bmm(e):
                for kb in range(KB):
                    inst = e.matmul(pb_t[6][:], bpw_ap[:, kb * 128:(kb + 1) * 128],
                                    cpad_ap[kb][:, 4:4 + C],
                                    start=(kb == 0), stop=(kb == KB - 1))
                return inst
            cst("pe", _bmm, reads=[bpw_buf] + cpad_buf, writes=[pb_b[6]])
            cchain("dve", [
                lambda e: e.tensor_scalar(vs_ap[0], pb_t[6][:], bpb_sb[:], None, ALU.add),
                lambda e: e.tensor_copy(Bh_ap, vs_ap[0]),
                lambda e: e.tensor_tensor(vs_ap[1], vs_ap[0], Bh_ap, ALU.subtract),
                lambda e: e.tensor_copy(Bl_ap, vs_ap[1]),
            ], reads=[pb_b[6], bpb_b], writes=[Bh_buf, Bl_buf, vs_buf[0], vs_buf[1]])
            yield
            stream_dp(0)
            stream_dp(1)

            def delta_unit(r):
                rp = r % 2
                pp = r % 2
                bank = 6 + pp
                s0, s0b = dscr_ap[0], dscr_buf[0]
                def _dmm(e, pp=pp, bank=bank):
                    for kb in range(KB):
                        inst = e.matmul(pb_t[bank][:],
                                        dpw_ap[pp][:, kb * 128:(kb + 1) * 128],
                                        cpad_ap[kb][:, 4:4 + C],
                                        start=(kb == 0), stop=(kb == KB - 1))
                    return inst
                cst("pe", _dmm, reads=[dpw_buf[pp]] + cpad_buf,
                    writes=[pb_b[bank]])
                cchain("act", [
                    (lambda e, bank=bank, s0=s0: e.activation(
                        s0, pb_t[bank][:], AF.Abs)),
                    (lambda e, s0=s0: e.activation(s0, s0, AF.Exp, scale=-1.0)),
                    (lambda e, s0=s0: e.activation(s0, s0, AF.Ln, bias=1.0)),
                ], reads=[pb_b[bank]], writes=[s0b])
                cchain("dve", [
                    (lambda e, bank=bank, rp=rp: e.tensor_scalar(
                        delta_ap[rp], pb_t[bank][:], 0.0, None, ALU.max)),
                    (lambda e, rp=rp, s0=s0: e.tensor_tensor(
                        delta_ap[rp], delta_ap[rp], s0, ALU.add)),
                ], reads=[pb_b[bank], s0b], writes=[delta_buf[rp]])
                if r + 2 < KB:
                    stream_dp(r + 2)

            def convdu_unit(r):
                rp = r % 2
                cs, csb = vs_ap[0], vs_buf[0]
                cvfns = [(lambda e, r=r, cs=cs: e.tensor_scalar(
                    cs, cpad_ap[r][:, 1:1 + C],
                    conv_sb[:, r * 4:r * 4 + 1], None, ALU.mult))]
                for kk in range(1, 4):
                    cvfns.append(lambda e, r=r, cs=cs, kk=kk: e.scalar_tensor_tensor(
                        cs, cpad_ap[r][:, 1 + kk:1 + kk + C],
                        conv_sb[:, r * 4 + kk:r * 4 + kk + 1], cs,
                        ALU.mult, ALU.add))
                cchain("dve", cvfns, reads=[cpad_buf[r], conv_wb], writes=[csb])
                cst("act", lambda e, r=r, cs=cs: e.activation(
                    cs, cs, AF.Silu, bias=cvb_sb[:, r:r + 1]),
                    reads=[csb, cvb_b], writes=[csb])
                cst("dve", lambda e, rp=rp, cs=cs: e.tensor_tensor(
                    du_ap[rp], delta_ap[rp], cs, ALU.mult),
                    reads=[delta_buf[rp], csb, du_buf[rp]], writes=[du_buf[rp]])

            delta_unit(0)
            yield
            convdu_unit(0)
            yield
            for r in range(KB):
                rp = r % 2
                if r + 1 < KB:
                    delta_unit(r + 1)
                    yield
                    convdu_unit(r + 1)
                    yield
                stream_ip(8 + r)
                for pn in range(NST // 2):
                    for spar in range(2):
                        idx = r * NST + 2 * pn + spar
                        bank = 6 + spar
                        def _bc2(e, idx=idx, bank=bank):
                            e.matmul(pb_t[bank][:],
                                     identB[:, idx:idx + 1].broadcast_to([128, 128]),
                                     Bh_ap, start=True, stop=False)
                            return e.matmul(pb_t[bank][:],
                                            identB[:, idx:idx + 1].broadcast_to([128, 128]),
                                            Bl_ap, start=False, stop=True)
                        cst("pe", _bc2, reads=[identB_b, Bh_buf, Bl_buf],
                            writes=[pb_b[bank]])
                    for spar in range(2):
                        idx = r * NST + 2 * pn + spar
                        cst("act", lambda e, rp=rp, idx=idx, spar=spar: e.activation(
                            dA_ap[spar], delta_ap[rp], AF.Exp,
                            scale=A_sb[:, idx:idx + 1]),
                            reads=[delta_buf[rp], A_b, dA_buf[spar]],
                            writes=[dA_buf[spar]])
                    for spar in range(2):
                        idx = r * NST + 2 * pn + spar
                        bank = 6 + spar
                        cst("dve", lambda e, rp=rp, spar=spar, bank=bank, idx=idx:
                            e.scalar_tensor_tensor(
                                dbu_ap[spar], du_ap[rp], c_sb[:, idx:idx + 1],
                                pb_t[bank][:], ALU.mult, ALU.mult),
                            reads=[du_buf[rp], c_b, pb_b[bank], dbu_buf[spar]],
                            writes=[dbu_buf[spar]])
                    for spar in range(2):
                        idx = r * NST + 2 * pn + spar
                        init = 0.0 if ch == 0 else carry[:, idx:idx + 1]
                        cst("dve", lambda e, spar=spar, init=init: e.tensor_tensor_scan(
                            dbu_ap[spar], dA_ap[spar], dbu_ap[spar], init,
                            ALU.mult, ALU.add),
                            reads=[dA_buf[spar], carry_b, dbu_buf[spar]],
                            writes=[dbu_buf[spar]])
                    for spar in range(2):
                        idx = r * NST + 2 * pn + spar
                        yeng = "pool" if spar == 0 else "dve"
                        if pn == 0:
                            cst(yeng, lambda e, spar=spar: e.tensor_copy(
                                y_ap[spar], dbu_ap[spar]),
                                reads=[dbu_buf[spar], y_buf[spar]],
                                writes=[y_buf[spar]])
                        else:
                            cst(yeng, lambda e, spar=spar: e.tensor_tensor(
                                y_ap[spar], y_ap[spar], dbu_ap[spar], ALU.add),
                                reads=[dbu_buf[spar], y_buf[spar]],
                                writes=[y_buf[spar]])
                        cst("pool", lambda e, idx=idx, spar=spar: e.tensor_copy(
                            carry[:, idx:idx + 1], dbu_ap[spar][:, C - 1:C]),
                            reads=[dbu_buf[spar], carry_b], writes=[carry_b])
                    yield
                cst("dve", lambda e: e.tensor_tensor(
                    y_ap[0], y_ap[0], y_ap[1], ALU.add),
                    reads=[y_buf[1], y_buf[0]], writes=[y_buf[0]])
                pp = r % 2
                bank = 6 + pp
                def _vmm(e, pp=pp, bank=bank):
                    for kb in range(KB):
                        inst = e.matmul(pb_t[bank][:],
                                        ipw_ap[pp][:, kb * 128:(kb + 1) * 128],
                                        mxln_ap[kb],
                                        start=(kb == 0), stop=(kb == KB - 1))
                    return inst
                cst("pe", _vmm, reads=[ipw_buf[pp]] + mxln_buf,
                    writes=[pb_b[bank]])
                vsb, vsbb = vs_ap[0], vs_buf[0]
                t3, t3b = vs_ap[1], vs_buf[1]
                mm_, mmb = vs_ap[2], vs_buf[2]
                mb = 8 + r
                cst("dve", lambda e, mb=mb, bank=bank: e.tensor_scalar(
                    vsb, pb_t[bank][:], ipb_sb[:, mb:mb + 1], None, ALU.add),
                    reads=[pb_b[bank], ipb_b, vsbb], writes=[vsbb])
                cchain("act", [
                    lambda e: e.activation(t3, vsb, AF.Abs),
                    lambda e: e.activation(t3, t3, AF.Exp, scale=-1.0),
                    lambda e: e.activation(t3, t3, AF.Ln, bias=1.0),
                ], reads=[vsbb], writes=[t3b])
                cchain("dve", [
                    lambda e: e.tensor_scalar(mm_, vsb, 0.0, None, ALU.min),
                    lambda e: e.tensor_tensor(t3, t3, mm_, ALU.subtract),
                ], reads=[vsbb], writes=[mmb, t3b])
                cst("act", lambda e: e.activation(t3, t3, AF.Exp, scale=-1.0),
                    reads=[t3b], writes=[t3b])
                cchain("pool", [
                    (lambda e: e.tensor_tensor(
                        mm_, y_ap[0], vsb, ALU.mult)),
                    (lambda e: e.tensor_tensor(mm_, mm_, t3, ALU.mult)),
                    (lambda e, r=r, c0=c0: e.tensor_tensor(
                        x_t[r][:, c0:c0 + C], x_t[r][:, c0:c0 + C],
                        mm_, ALU.add)),
                ], reads=[y_buf[0], vsbb, t3b], writes=[mmb, x_b[r][ch]])
                yield

        setup()
        sched.barrier()
        stream_w13(GROUPS[0][0], 0)
        stream_w13(GROUPS[0][0], 1)
        stream_w2(GROUPS[0][0], 0)
        stream_w2(GROUPS[0][0], 1)

        m_gens = [m_chunk_gen(c) for c in range(NCH)]
        m_ready = []
        m_done = [False] * NCH

        def drain_m(k):
            n = 0
            while n < k and m_ready:
                c, g = m_ready[0]
                try:
                    next(g)
                    n += 1
                except StopIteration:
                    m_done[c] = True
                    m_ready.pop(0)

        def drain_m_until_done(c):
            while not m_done[c]:
                drain_m(1)
                if not m_ready:
                    break

        fi0, c0_ = F_ORDER[0]
        for u in f_head_units(fi0, c0_, 0):
            u()
        for k, (fi, c) in enumerate(F_ORDER):
            par = k % 2
            if k + 1 < len(F_ORDER):
                nfi, ncx = F_ORDER[k + 1]
                nhead = f_head_units(nfi, ncx, (k + 1) % 2)
            else:
                nfi, ncx, nhead = None, None, []
            nhq = [0]

            def poke_head():
                if nhq[0] < len(nhead) and (nfi == "f1" or m_done[ncx]):
                    nhead[nhq[0]]()
                    nhq[0] += 1

            gen = f_chunk_gen(fi, c, par)
            beat = [0]
            for _ in gen:
                beat[0] += 1
                drain_m((2 if beat[0] % 2 == 0 else 0) if ZIP_M else 0)
                poke_head()
            if fi == "f1":
                m_ready.append((c, m_gens[c]))
                if not ZIP_M:
                    drain_m_until_done(c)
            if nfi == "f2":
                drain_m_until_done(ncx)
            while nhq[0] < len(nhead):
                nhead[nhq[0]]()
                nhq[0] += 1
            if fi == "f2":
                def _st(e, c=c):
                    for kb in range(KB):
                        e.dma_start(out=yT_h[kb * 128:(kb + 1) * 128, c * C:(c + 1) * C],
                                    in_=x_t[kb][:, c * C:(c + 1) * C]
                                    ).then_inc(SEM["yst"], 16)
                getsem("yst")
                sched.stage("sp", _st, reads=[x_b[kb][c] for kb in range(KB)],
                            dmaq="yst", ndma=KB)
        while m_ready:
            drain_m(1)

        final_qx = sched.cnt["yst"]

        def mk_body(engkey):
            def run(eng):
                for waits, emit, dmaq in sched.plan[engkey]:
                    for k, v in waits.items():
                        eng.wait_ge(SEM[k], v)
                    emit(eng)
                eng.wait_ge(SEM["yst"], final_qx)
            return run

        blk.tensor(mk_body("pe"))
        blk.scalar(mk_body("act"))
        blk.vector(mk_body("dve"))
        blk.gpsimd(mk_body("pool"))
        blk.sync(mk_body("sp"))

    nstages = {k: len(v) for k, v in sched.plan.items()}
    print(f"[kernel] stages={nstages} arena_peak={AR.peak * 4}B/part")
    return nc


_cache = {}


def kernel(**inputs):
    x = np.asarray(inputs["L0_attn_x"])
    cc = np.ascontiguousarray
    shared = {
        "ipT": cc(np.asarray(inputs["L0_attn_in_proj_w"]).T),
        "dpT": cc(np.asarray(inputs["L0_attn_delta_proj_w"]).T),
        "bpT": cc(np.asarray(inputs["L0_attn_b_proj_w"]).T),
        "gwT": cc(np.asarray(inputs["L0_ffn_gate_w"]).T),
        "w1T": cc(np.asarray(inputs["L0_ffn_w1"]).transpose(0, 2, 1)),
        "w3T": cc(np.asarray(inputs["L0_ffn_w3"]).transpose(0, 2, 1)),
        "w2T": cc(np.asarray(inputs["L0_ffn_w2"]).transpose(0, 2, 1)),
        "sw1T": cc(np.asarray(inputs["L0_ffn_sw1"]).T),
        "sw3T": cc(np.asarray(inputs["L0_ffn_sw3"]).T),
        "sw2T": cc(np.asarray(inputs["L0_ffn_sw2"]).T),
        "cT": cc(np.asarray(inputs["L0_attn_c_param"]).T),
        "alog": cc(np.asarray(inputs["L0_attn_a_log"])),
        "convw": cc(np.asarray(inputs["L0_attn_conv_w"]).reshape(D, 4)),
        "ipb": cc(np.asarray(inputs["L0_attn_in_proj_b"]).reshape(16, 128)),
        "convb": cc(np.asarray(inputs["L0_attn_conv_b"]).reshape(8, 128)),
        "bpb": cc(np.asarray(inputs["L0_attn_b_proj_b"]).reshape(1, 128)),
    }
    in_maps = []
    for b in range(8):
        m = dict(shared)
        m["xT"] = cc(x[b].T)
        in_maps.append(m)

    key = (SERIAL, ZIP_M, USE_CACHE)
    if key not in _cache:
        _cache[key] = build_program()
    nc = _cache[key]
    res = run_bass_kernel_spmd(nc, in_maps, core_ids=list(range(8)))
    out = np.empty((8, T, D), np.float32)
    for b in range(8):
        out[b] = res.results[b]["yT"].T
    return out


# version 59
# speedup vs baseline: 1.0638x; 1.0099x over previous
import contextlib
import numpy as np
import concourse.bass as bass
import concourse.mybir as mybir
from concourse.bass_utils import run_bass_kernel_spmd

F32 = mybir.dt.float32
BF16 = mybir.dt.bfloat16
I32 = mybir.dt.int32
AF = mybir.ActivationFunctionType
ALU = mybir.AluOpType
AX = mybir.AxisListType

D = 1024
T = 2048
KB = 8
C = 512
NCH = T // C
H = 512
NE = 16
NST = 16
EPS = 1e-5
ARENA_WORDS = 35400
SERIAL = False
ZIP_M = True
USE_CACHE = True

ENGS = ("pe", "act", "dve", "pool", "sp")

F_ORDER = [("f1", 0), ("f1", 1), ("f1", 2), ("f2", 0),
           ("f1", 3), ("f2", 1), ("f2", 2), ("f2", 3)]
GROUPS = [[i] for i in range(17)]


class Buf:
    __slots__ = ("name", "wr", "rds")

    def __init__(self, name):
        self.name = name
        self.wr = None
        self.rds = []


class Sched:
    def __init__(self):
        self.plan = {k: [] for k in ENGS}
        self.cnt = {k: 0 for k in ENGS}
        self.waited = {k: {} for k in ENGS}
        self.bar = {}

    def stage(self, eng, emit, reads=(), writes=(), dmaq=None, ndma=0):
        if SERIAL:
            self.bar = dict(self.cnt)
        deps = dict(self.bar)

        def need(ev):
            if ev is None:
                return
            k, v = ev
            if deps.get(k, 0) < v:
                deps[k] = v
        for b in reads:
            need(b.wr)
        for b in writes:
            need(b.wr)
            for ev in b.rds:
                need(ev)
        if eng == "pe":
            deps.pop("pe", None)
        w = self.waited[eng]
        fin = {}
        for k, v in deps.items():
            if w.get(k, 0) >= v:
                continue
            w[k] = v
            fin[k] = v
        if dmaq is None:
            self.cnt[eng] += 1
            ev = (eng, self.cnt[eng])
        else:
            assert ndma > 0
            self.cnt[dmaq] = self.cnt.get(dmaq, 0) + 16 * ndma
            ev = (dmaq, self.cnt[dmaq])
        self.plan[eng].append((fin, emit, dmaq))
        for b in writes:
            b.wr = ev
            b.rds = []
        for b in reads:
            b.rds.append(ev)
        return ev

    def barrier(self):
        self.bar = dict(self.cnt)


class Arena:
    def __init__(self, t, words):
        self.t = t
        self.words = words
        self.off = 0
        self.peak = 0

    def alloc(self, name, p, cols, dtype):
        if dtype in (BF16,):
            w = (cols + 1) // 2
        else:
            w = cols
        self.off = (self.off + 15) & ~15
        a = self.off
        self.off += w
        self.peak = max(self.peak, self.off)
        assert self.off <= self.words, f"arena overflow at {name}: {self.off * 4}B"
        view = self.t[0:p, a:a + w]
        if dtype == BF16:
            view = view.bitcast(BF16)[:, 0:cols]
        elif dtype == I32:
            view = view.bitcast(I32)
        return view, Buf(name)


def build_program():
    nc = bass.Bass("TRN2", target_bir_lowering=False, detect_race_conditions=False)

    def din(name, shape):
        return nc.dram_tensor(name, list(shape), F32, kind="ExternalInput").ap()

    xT_h = din("xT", (D, T))
    ipT_h = din("ipT", (D, 2 * D))
    dpT_h = din("dpT", (D, D))
    bpT_h = din("bpT", (D, 128))
    gwT_h = din("gwT", (D, NE))
    w1T_h = din("w1T", (NE, D, H))
    w3T_h = din("w3T", (NE, D, H))
    w2T_h = din("w2T", (NE, H, D))
    sw1T_h = din("sw1T", (D, H))
    sw3T_h = din("sw3T", (D, H))
    sw2T_h = din("sw2T", (H, D))
    cT_h = din("cT", (NST, D))
    alog_h = din("alog", (D, NST))
    convw_h = din("convw", (D, 4))
    ipb_h = din("ipb", (16, 128))
    convb_h = din("convb", (8, 128))
    bpb_h = din("bpb", (1, 128))
    yT_h = nc.dram_tensor("yT", [D, T], F32, kind="ExternalOutput").ap()
    w1c_h = nc.dram_tensor("w1c", [NE + 1, 2, 128, KB * 256], BF16, kind="Internal").ap()
    w3c_h = nc.dram_tensor("w3c", [NE + 1, 2, 128, KB * 256], BF16, kind="Internal").ap()
    w2c_h = nc.dram_tensor("w2c", [NE + 1, 2, 128, 4 * 512], BF16, kind="Internal").ap()
    ipc_h = nc.dram_tensor("ipc", [16, 128, KB * 128], BF16, kind="Internal").ap()
    dpc_h = nc.dram_tensor("dpc", [KB, 128, KB * 128], BF16, kind="Internal").ap()

    sched = Sched()
    es = contextlib.ExitStack()
    with es:
        def sb(name, p, cols, dtype):
            t = es.enter_context(nc.sbuf_tensor(name, [p, cols], dtype))
            return t, Buf(name)

        x_t = []
        x_b = []
        for kb in range(KB):
            t = es.enter_context(nc.sbuf_tensor(f"x{kb}", [128, T], F32))
            x_t.append(t)
            x_b.append([Buf(f"x{kb}_{ch}") for ch in range(NCH)])
        cpool, cpool_b = sb("cpool", 128, 8, F32)
        CONSTS = [0.0, 1.0, -1.0, 0.5, EPS]
        for j, v in enumerate(CONSTS):
            nc.const_aps.register(F32, v, cpool[:, j:j + 1])
        ones1f, ones1f_b = sb("ones1f", 1, 128, F32)
        ones128, ones128_b = sb("ones128", 128, 1, F32)
        identF, identF_b = sb("identF", 128, 128, F32)
        identB, identB_b = sb("identB", 128, 128, BF16)
        iotat, iotat_b = sb("iotat", 128, 128, I32)
        A_sb, A_b = sb("A_sb", 128, 128, F32)
        c_sb, c_b = sb("c_sb", 128, 128, F32)
        conv_sb, conv_wb = sb("conv_sb", 128, 32, F32)
        ipb_sb, ipb_b = sb("ipb_sb", 128, 16, F32)
        cvb_sb, cvb_b = sb("cvb_sb", 128, 8, F32)
        bpb_sb, bpb_b = sb("bpb_sb", 128, 1, F32)
        gw_sb, gw_b = sb("gw_sb", 128, 8 * NE, F32)
        Sbc_sb, Sbc_b = sb("Sbc_sb", 128, NE, F32)
        carry, carry_b = sb("carry", 128, 128, F32)
        btmp, btmp_b = sb("btmp", 16, 128, F32)
        srow, srow_b = sb("srow", 1, 128, F32)

        arena_t = es.enter_context(nc.sbuf_tensor("arena", [128, ARENA_WORDS], F32))
        AR = Arena(arena_t, ARENA_WORDS)

        pb_t, pb_b = [], []
        for i in range(8):
            t = es.enter_context(nc.psum_tensor(f"pb{i}", [128, 512], F32))
            pb_t.append(t)
            pb_b.append(Buf(f"pb{i}"))

        SEM = {}
        for k in ENGS:
            SEM[k] = es.enter_context(nc.semaphore(f"sem_{k}"))

        def getsem(q):
            if q not in SEM:
                SEM[q] = es.enter_context(nc.semaphore(f"sem_{q}"))
            return SEM[q]
        blk = es.enter_context(nc.Block())

        def cst(eng, fn, reads=(), writes=()):
            def emit(e, fn=fn, eng=eng):
                inst = fn(e)
                inst.then_inc(SEM[eng], 1)
            return sched.stage(eng, emit, reads, writes)

        def cchain(eng, fns, reads=(), writes=()):
            rw = list(reads) + list(writes)
            for fn in fns:
                cst(eng, fn, reads=rw, writes=list(writes))

        def dst(eng, q, transfers, reads=(), writes=()):
            getsem(q)
            def emit(e, transfers=transfers, q=q):
                for dstp, srcp, kw in transfers:
                    e.dma_start(out=dstp, in_=srcp, **kw).then_inc(SEM[q], 16)
            return sched.stage(eng, emit, reads, writes, dmaq=q, ndma=len(transfers))

        _ct_off = AR.off
        ct16, ct16_b = AR.alloc("ct16", 16, D, F32)
        AR.off = _ct_off
        xln_ap = [[None] * KB for _ in range(2)]
        xln_buf = [[None] * KB for _ in range(2)]
        for i in range(2):
            for kb in range(KB):
                xln_ap[i][kb], xln_buf[i][kb] = AR.alloc(f"xln{i}_{kb}", 128, C, BF16)
        act_ap = [[None] * 4 for _ in range(2)]
        act_buf = [[None] * 4 for _ in range(2)]
        for s in range(2):
            for hb in range(4):
                act_ap[s][hb], act_buf[s][hb] = AR.alloc(f"act{s}_{hb}", 128, C, BF16)
        w1h_ap, w1h_buf, w3h_ap, w3h_buf = [], [], [], []
        for i in range(2):
            a, b = AR.alloc(f"w1h{i}", 128, KB * 256, BF16)
            w1h_ap.append(a)
            w1h_buf.append(b)
            a, b = AR.alloc(f"w3h{i}", 128, KB * 256, BF16)
            w3h_ap.append(a)
            w3h_buf.append(b)
        w2h_ap, w2h_buf = [], []
        for i in range(3):
            a, b = AR.alloc(f"w2h{i}", 128, 4 * 512, BF16)
            w2h_ap.append(a)
            w2h_buf.append(b)
        cb_ap, cb_buf = [], []
        for i in range(2):
            a, b = AR.alloc(f"cb{i}", 128, C, F32)
            cb_ap.append(a)
            cb_buf.append(b)
        sil_ap, sil_buf, h3c_ap, h3c_buf = [], [], [], []
        for i in range(2):
            a, b = AR.alloc(f"sil{i}", 128, C, F32)
            sil_ap.append(a)
            sil_buf.append(b)
            a, b = AR.alloc(f"h3c{i}", 128, C, F32)
            h3c_ap.append(a)
            h3c_buf.append(b)
        comb_ap, comb_buf = [], []
        for i in range(2):
            a, b = AR.alloc(f"comb{i}", 16, C, BF16)
            comb_ap.append(a)
            comb_buf.append(b)
        rowsF_ap, rowsF_buf = AR.alloc("rowsF", 1, 2 * C, F32)
        sq_ap, sq_buf = [], []
        for i in range(2):
            a, b = AR.alloc(f"sqln{i}", 128, C, F32)
            sq_ap.append(a)
            sq_buf.append(b)
        tmp_ap, tmp_buf = [], []
        for i in range(2):
            a, b = AR.alloc(f"tmpln{i}", 128, 256, F32)
            tmp_ap.append(a)
            tmp_buf.append(b)
        tk_n1, tk_n1_b = AR.alloc("tkn1", 128, 4 * NE, F32)
        TKB = {}
        for nmv in ("l", "el", "t1", "sb"):
            TKB[nmv] = AR.alloc(f"tkb{nmv}", 128, 4 * NE, F32)
        TKB["vv"] = TKB["l"]
        TKB["eqi"] = AR.alloc("tkbeqi", 128, 4 * NE, I32)
        TKB["c1"] = AR.alloc("tkbc1", 128, 4, F32)
        TKB["c2"] = AR.alloc("tkbc2", 128, 4, F32)
        TKB["rc"] = AR.alloc("tkbrc", 128, 8, F32)
        mxln_ap, mxln_buf = [], []
        for kb in range(KB):
            a, b = AR.alloc(f"mxln{kb}", 128, C, BF16)
            mxln_ap.append(a)
            mxln_buf.append(b)
        cpad_ap, cpad_buf = [], []
        for r in range(KB):
            a, b = AR.alloc(f"cpad{r}", 128, C + 4, BF16)
            cpad_ap.append(a)
            cpad_buf.append(b)
        ipw_ap, ipw_buf, dpw_ap, dpw_buf = [], [], [], []
        for i in range(2):
            a, b = AR.alloc(f"ipw{i}", 128, KB * 128, BF16)
            ipw_ap.append(a)
            ipw_buf.append(b)
            a, b = AR.alloc(f"dpw{i}", 128, KB * 128, BF16)
            dpw_ap.append(a)
            dpw_buf.append(b)
        bpw_ap, bpw_buf = AR.alloc("bpw", 128, KB * 128, BF16)
        Bh_ap, Bh_buf = AR.alloc("Bh", 128, C, BF16)
        Bl_ap, Bl_buf = AR.alloc("Bl", 128, C, BF16)
        delta_ap, delta_buf, du_ap, du_buf = [], [], [], []
        dA_ap, dA_buf, dbu_ap, dbu_buf = [], [], [], []
        y_ap, y_buf = [], []
        for i in range(2):
            a, b = AR.alloc(f"delta{i}", 128, C, F32)
            delta_ap.append(a)
            delta_buf.append(b)
            a, b = AR.alloc(f"du{i}", 128, C, F32)
            du_ap.append(a)
            du_buf.append(b)
            a, b = AR.alloc(f"dA{i}", 128, C, F32)
            dA_ap.append(a)
            dA_buf.append(b)
            a, b = AR.alloc(f"dbu{i}", 128, C, F32)
            dbu_ap.append(a)
            dbu_buf.append(b)
            a, b = AR.alloc(f"y{i}", 128, C, F32)
            y_ap.append(a)
            y_buf.append(b)
        dscr_ap, dscr_buf = [], []
        for i in range(1):
            a, b = AR.alloc(f"dscr{i}", 128, C, F32)
            dscr_ap.append(a)
            dscr_buf.append(b)
        vs_ap, vs_buf = [], []
        for i in range(3):
            a, b = AR.alloc(f"vs{i}", 128, C, F32)
            vs_ap.append(a)
            vs_buf.append(b)
        rowsM_ap, rowsM_buf = AR.alloc("rowsM", 1, 2 * C, F32)

        def setup():
            cst("dve", lambda e: e.memset(cpool[:], 0.0), writes=[cpool_b])
            for j, v in enumerate(CONSTS):
                if v != 0.0:
                    cst("dve", lambda e, j=j, v=v: e.memset(cpool[:, j:j + 1], v),
                        reads=[cpool_b], writes=[cpool_b])
            cst("dve", lambda e: e.memset(ones1f[:], 1.0), writes=[ones1f_b])
            cst("dve", lambda e: e.memset(ones128[:], 1.0), writes=[ones128_b])
            cst("dve", lambda e: e.memset(carry[:], 0.0), writes=[carry_b])
            cst("dve", lambda e: e.memset(tk_n1, -1.0), writes=[tk_n1_b])
            cst("pool", lambda e: e.iota(iotat[:], pattern=[[1, 128]], base=0,
                                         channel_multiplier=-1), writes=[iotat_b])
            cst("dve", lambda e: e.tensor_single_scalar(identF[:], iotat[:], 0, ALU.is_equal),
                reads=[iotat_b], writes=[identF_b])
            cst("dve", lambda e: e.tensor_copy(identB[:], identF[:]),
                reads=[identF_b], writes=[identB_b])
            for kb in range(KB):
                def _ld(e, kb=kb):
                    e.dma_start(out=x_t[kb][:], in_=xT_h[kb * 128:(kb + 1) * 128, :]
                                ).then_inc(SEM[f"ldx{kb}"], 16)
                getsem(f"ldx{kb}")
                sched.stage("sp", _ld, writes=x_b[kb], dmaq=f"ldx{kb}", ndma=1)
            dst("sp", "lda", [(A_sb[:].rearrange("p (r n) -> p r n", n=NST),
                              alog_h.rearrange("(r p) n -> p r n", p=128), {})],
                writes=[A_b])
            dst("sp", "ldct", [(ct16[:], cT_h, {})], writes=[ct16_b])
            dst("sp", "ldcv", [(conv_sb[:].rearrange("p (r k) -> p r k", k=4),
                              convw_h.rearrange("(r p) k -> p r k", p=128), {})],
                writes=[conv_wb])
            dst("sp", "ldgw", [(gw_sb[:].rearrange("p (r n) -> p r n", n=NE),
                              gwT_h.rearrange("(r p) n -> p r n", p=128), {})],
                writes=[gw_b])
            cst("act", lambda e: e.activation(A_sb[:], A_sb[:], AF.Exp),
                reads=[A_b], writes=[A_b])
            cst("dve", lambda e: e.tensor_single_scalar(A_sb[:], A_sb[:], -1.0, ALU.mult),
                reads=[A_b], writes=[A_b])
            for r in range(KB):
                cst("pe", lambda e, r=r: e.transpose(
                    pb_t[r % 2][:, 0:16], ct16[0:16, r * 128:(r + 1) * 128],
                    identF[0:16, 0:16]),
                    reads=[ct16_b, identF_b], writes=[pb_b[r % 2]])
                cst("dve", lambda e, r=r: e.tensor_copy(
                    c_sb[:, r * 16:(r + 1) * 16], pb_t[r % 2][:, 0:16]),
                    reads=[pb_b[r % 2], c_b], writes=[c_b])
            dst("sp", "ldb1", [(btmp[:], ipb_h, {})], writes=[btmp_b])
            cst("pe", lambda e: e.transpose(pb_t[2][:, 0:16], btmp[:], identF[0:16, 0:16]),
                reads=[btmp_b, identF_b], writes=[pb_b[2]])
            cst("dve", lambda e: e.tensor_copy(ipb_sb[:], pb_t[2][:, 0:16]),
                reads=[pb_b[2]], writes=[ipb_b])
            dst("sp", "ldb2", [(btmp[0:8, :], convb_h, {})], reads=[btmp_b], writes=[btmp_b])
            cst("pe", lambda e: e.transpose(pb_t[3][:, 0:8], btmp[0:8, :], identF[0:8, 0:8]),
                reads=[btmp_b, identF_b], writes=[pb_b[3]])
            cst("dve", lambda e: e.tensor_copy(cvb_sb[:], pb_t[3][:, 0:8]),
                reads=[pb_b[3]], writes=[cvb_b])
            dst("sp", "ldb3", [(srow[:], bpb_h, {})], writes=[srow_b])
            cst("pe", lambda e: e.transpose(pb_t[4][:, 0:1], srow[:], identF[0:1, 0:1]),
                reads=[srow_b, identF_b], writes=[pb_b[4]])
            cst("dve", lambda e: e.tensor_copy(bpb_sb[:], pb_t[4][:, 0:1]),
                reads=[pb_b[4]], writes=[bpb_b])
            def _smm(e):
                for kb in range(KB):
                    inst = e.matmul(pb_t[5][0:1, 0:NE], ones128[:],
                                    gw_sb[:, kb * NE:(kb + 1) * NE],
                                    start=(kb == 0), stop=(kb == KB - 1))
                return inst
            cst("pe", _smm, reads=[ones128_b, gw_b], writes=[pb_b[5]])
            cst("dve", lambda e: e.tensor_copy(srow[0:1, 0:NE], pb_t[5][0:1, 0:NE]),
                reads=[pb_b[5], srow_b], writes=[srow_b])
            cst("pe", lambda e: e.matmul(pb_t[6][:, 0:NE], ones1f[:], srow[0:1, 0:NE],
                                         start=True, stop=True),
                reads=[ones1f_b, srow_b], writes=[pb_b[6]])
            cst("dve", lambda e: e.tensor_copy(Sbc_sb[:], pb_t[6][:, 0:NE]),
                reads=[pb_b[6]], writes=[Sbc_b])
            for r in range(KB):
                cst("dve", lambda e, r=r: e.memset(cpad_ap[r][:, 0:4], 0.0),
                    writes=[cpad_buf[r]])
            dst("pool", "ldbp",
                [(bpw_ap.rearrange("p (k m) -> p k m", k=KB),
                  bpT_h.rearrange("(k p) m -> p k m", p=128), {})],
                writes=[bpw_buf])

        def ln_units(c0, xlnv, xlnb, rows_ap, rows_buf, bank):
            pb, pbb = pb_t[bank], pb_b[bank]
            rstd_r = rows_ap[0:1, 0:C]
            nmr_r = rows_ap[0:1, C:2 * C]
            ch = c0 // C

            def emit_sq(kb):
                sp_ = kb % 2
                cst("pool", lambda e, kb=kb, sp_=sp_: e.tensor_tensor(
                    sq_ap[sp_], x_t[kb][:, c0:c0 + C], x_t[kb][:, c0:c0 + C],
                    ALU.mult),
                    reads=[x_b[kb][ch]], writes=[sq_buf[sp_]])

            def emit_mm(kb):
                sp_ = kb % 2
                def _mm(e, kb=kb, sp_=sp_):
                    e.matmul(pb[0:1, :], ones128[:], x_t[kb][:, c0:c0 + C],
                             start=(kb == 0), stop=(kb == KB - 1))
                    return e.matmul(pb[32:33, :], ones128[:], sq_ap[sp_],
                                    start=(kb == 0), stop=(kb == KB - 1))
                cst("pe", _mm, reads=[ones128_b, x_b[kb][ch], sq_buf[sp_]],
                    writes=[pbb])

            def sums():
                emit_sq(0)
                emit_sq(1)
                for kb in range(KB):
                    emit_mm(kb)
                    if kb + 2 < KB:
                        emit_sq(kb + 2)

            def rows_chain():
                cchain("dve", [
                    lambda e: e.tensor_single_scalar(nmr_r, pb[0:1, :], 1.0 / D, ALU.mult),
                    lambda e: e.tensor_single_scalar(rstd_r, pb[32:33, :], 1.0 / D, ALU.mult),
                    lambda e: e.scalar_tensor_tensor(tmp_ap[0][0:1, 0:256], nmr_r[0:1, 0:256],
                                                     1.0, nmr_r[0:1, 0:256], ALU.mult, ALU.mult),
                    lambda e: e.scalar_tensor_tensor(tmp_ap[1][0:1, 0:256], nmr_r[0:1, 256:512],
                                                     1.0, nmr_r[0:1, 256:512], ALU.mult, ALU.mult),
                    lambda e: e.tensor_tensor(rstd_r[0:1, 0:256], rstd_r[0:1, 0:256],
                                              tmp_ap[0][0:1, 0:256], ALU.subtract),
                    lambda e: e.tensor_tensor(rstd_r[0:1, 256:512], rstd_r[0:1, 256:512],
                                              tmp_ap[1][0:1, 0:256], ALU.subtract),
                ], reads=[pbb], writes=[rows_buf, tmp_buf[0], tmp_buf[1]])
                cchain("act", [
                    lambda e: e.activation(rstd_r, rstd_r, AF.Ln, bias=EPS),
                    lambda e: e.activation(rstd_r, rstd_r, AF.Exp, scale=-0.5),
                ], writes=[rows_buf])
                cst("dve", lambda e: e.scalar_tensor_tensor(
                    nmr_r, nmr_r, -1.0, rstd_r, ALU.mult, ALU.mult), writes=[rows_buf])

            def apply_half(hf):
                co = hf * 256
                def _bc(e, co=co):
                    e.matmul(pb[:, 0:256], ones1f[:], rstd_r[0:1, co:co + 256],
                             start=True, stop=True)
                    return e.matmul(pb[:, 256:512], ones1f[:], nmr_r[0:1, co:co + 256],
                                    start=True, stop=True)
                cst("pe", _bc, reads=[ones1f_b, rows_buf], writes=[pbb])
                for kb in range(KB):
                    tpar = kb % 2
                    cst("dve", lambda e, kb=kb, co=co, tpar=tpar: e.tensor_tensor(
                        tmp_ap[tpar],
                        x_t[kb][:, c0 + co:c0 + co + 256], pb[:, 0:256], ALU.mult),
                        reads=[x_b[kb][ch], pbb], writes=[tmp_buf[tpar]])
                    cst("dve", lambda e, kb=kb, co=co, tpar=tpar: e.tensor_tensor(
                        xlnv[kb][:, co:co + 256],
                        tmp_ap[tpar], pb[:, 256:512], ALU.add),
                        reads=[tmp_buf[tpar], pbb], writes=[xlnb[kb]])

            return [sums, rows_chain, lambda: apply_half(0), lambda: apply_half(1)]

        def routing_units(c0, par):
            ch = c0 // C
            pb, pbb = pb_t[3], pb_b[3]
            rstd_r = rowsF_ap[0:1, 0:C]
            nmr_r = rowsF_ap[0:1, C:2 * C]
            l_ap, l_b = TKB["l"]
            el_ap, el_b = TKB["el"]
            vv_ap, vv_b = TKB["vv"]
            t1_ap, t1_b = TKB["t1"]
            sb_ap, sb_b = TKB["sb"]
            eqi_ap, eqi_b = TKB["eqi"]
            c1_ap, c1_b = TKB["c1"]
            c2_ap, c2_b = TKB["c2"]
            rc_ap, rc_b = TKB["rc"]

            def seg(ap):
                return ap.rearrange("p (t n) -> p t n", n=NE)

            def colb(ap, j):
                return ap

            def unit_a():
                for tb in range(4):
                    tc = c0 + tb * 128
                    def _gmm(e, tc=tc, tb=tb):
                        for kb in range(KB):
                            inst = e.matmul(pb[:, 16 * tb:16 * tb + NE],
                                            x_t[kb][:, tc:tc + 128],
                                            gw_sb[:, kb * NE:(kb + 1) * NE],
                                            start=(kb == 0), stop=(kb == KB - 1))
                        return inst
                    cst("pe", _gmm, reads=[gw_b] + [x_b[kb][ch] for kb in range(KB)],
                        writes=[pbb])
                for tb in range(4):
                    def _tr(e, tb=tb):
                        e.transpose(pb[:, 64 + 2 * tb:65 + 2 * tb],
                                    rstd_r[0:1, tb * 128:(tb + 1) * 128],
                                    identF[0:1, 0:1])
                        return e.transpose(pb[:, 65 + 2 * tb:66 + 2 * tb],
                                           nmr_r[0:1, tb * 128:(tb + 1) * 128],
                                           identF[0:1, 0:1])
                    cst("pe", _tr, reads=[rowsF_buf, identF_b], writes=[pbb])
                cst("dve", lambda e: e.tensor_copy(rc_ap, pb[:, 64:72]),
                    reads=[pbb, rc_b], writes=[rc_b])

            def unit_b():
                rc3 = rc_ap.rearrange("p (t c) -> p t c", c=2)
                cchain("dve", [
                    lambda e: e.tensor_tensor(
                        seg(l_ap), pb[:, 0:64].rearrange("p (t n) -> p t n", n=NE),
                        rc3[:, :, 0:1].broadcast_to([128, 4, NE]), ALU.mult),
                    lambda e: e.tensor_tensor(
                        seg(sb_ap), Sbc_sb[:].unsqueeze(1).broadcast_to([128, 4, NE]),
                        rc3[:, :, 1:2].broadcast_to([128, 4, NE]), ALU.mult),
                    lambda e: e.tensor_tensor(l_ap, l_ap, sb_ap, ALU.add),
                    lambda e: e.reduce_max(c1_ap, seg(l_ap), axis=AX.X, negate=True),
                    lambda e: e.tensor_tensor(
                        seg(l_ap), seg(l_ap),
                        c1_ap.unsqueeze(2).broadcast_to([128, 4, NE]), ALU.add),
                ], reads=[pbb, Sbc_b],
                   writes=[l_b, sb_b, c1_b, rc_b])
                cst("act", lambda e: e.activation(el_ap, l_ap, AF.Exp),
                    reads=[l_b, el_b], writes=[el_b])
                fns = [lambda e: e.tensor_copy(vv_ap, el_ap)]
                for _ in range(3):
                    fns += [
                        lambda e: e.reduce_max(c2_ap, seg(vv_ap), axis=AX.X),
                        lambda e: e.tensor_tensor(
                            seg(eqi_ap), seg(vv_ap),
                            c2_ap.unsqueeze(2).broadcast_to([128, 4, NE]),
                            ALU.is_equal),
                        lambda e: e.select(vv_ap, eqi_ap, tk_n1, vv_ap),
                    ]
                fns += [
                    lambda e: e.reduce_max(c2_ap, seg(vv_ap), axis=AX.X),
                    lambda e: e.tensor_tensor(
                        seg(t1_ap), seg(el_ap),
                        c2_ap.unsqueeze(2).broadcast_to([128, 4, NE]), ALU.is_ge),
                    lambda e: e.tensor_tensor(t1_ap, el_ap, t1_ap, ALU.mult),
                    lambda e: e.reduce_sum(c1_ap, seg(t1_ap), axis=AX.X),
                    lambda e: e.reciprocal(c1_ap, c1_ap),
                    lambda e: e.tensor_tensor(
                        seg(t1_ap), seg(t1_ap),
                        c1_ap.unsqueeze(2).broadcast_to([128, 4, NE]), ALU.mult),
                ]
                cchain("dve", fns,
                       reads=[el_b, tk_n1_b],
                       writes=[vv_b, eqi_b, c1_b, c2_b, t1_b])

            def unit_c():
                for tb in range(4):
                    co = 256 + 128 * (tb % 2)
                    cst("pe", lambda e, tb=tb, co=co: e.transpose(
                        pb[0:16, co:co + 128], t1_ap[:, 16 * tb:16 * tb + NE],
                        identF[:]),
                        reads=[t1_b, identF_b], writes=[pbb])
                    cst("dve", lambda e, tb=tb, co=co: e.tensor_copy(
                        comb_ap[par][:, tb * 128:(tb + 1) * 128], pb[0:16, co:co + 128]),
                        reads=[pbb, comb_buf[par]], writes=[comb_buf[par]])

            return [unit_a, unit_b, unit_c]

        wcnt = [0]
        cache_buf = {}

        def stream_w13(e_, hf):
            wp = wcnt[0] % 2
            wcnt[0] += 1
            key = ("w13", e_, hf)
            if (not USE_CACHE) or key not in cache_buf:
                if e_ < NE:
                    s1, s3 = w1T_h[e_], w3T_h[e_]
                else:
                    s1, s3 = sw1T_h, sw3T_h
                co = hf * 256
                dst("pool", f"w13c_{wp}",
                    [(w1h_ap[wp].rearrange("p (k h) -> p k h", k=KB),
                      s1[:, co:co + 256].rearrange("(k p) h -> p k h", p=128), {}),
                     (w3h_ap[wp].rearrange("p (k h) -> p k h", k=KB),
                      s3[:, co:co + 256].rearrange("(k p) h -> p k h", p=128), {})],
                    writes=[w1h_buf[wp], w3h_buf[wp]])
                cb13 = Buf(f"c13_{e_}_{hf}")
                dst("sp", f"wb13_{wp}",
                    [(w1c_h[e_, hf], w1h_ap[wp], {}),
                     (w3c_h[e_, hf], w3h_ap[wp], {})],
                    reads=[w1h_buf[wp], w3h_buf[wp]], writes=[cb13])
                cache_buf[key] = cb13
            else:
                dst("sp", f"w13_{wp}",
                    [(w1h_ap[wp], w1c_h[e_, hf], {}),
                     (w3h_ap[wp], w3c_h[e_, hf], {})],
                    reads=[cache_buf[key]], writes=[w1h_buf[wp], w3h_buf[wp]])
            return wp

        w2cnt = [0]

        def stream_w2(e_, th):
            wp = w2cnt[0] % 3
            w2cnt[0] += 1
            key = ("w2", e_, th)
            if (not USE_CACHE) or key not in cache_buf:
                s2 = w2T_h[e_] if e_ < NE else sw2T_h
                co = th * 512
                dst("pool", f"w2c_{wp}",
                    [(w2h_ap[wp].rearrange("p (k h) -> p k h", k=4),
                      s2[:, co:co + 512].rearrange("(k p) h -> p k h", p=128), {})],
                    writes=[w2h_buf[wp]])
                cb2 = Buf(f"c2_{e_}_{th}")
                dst("sp", f"wb2_{wp}", [(w2c_h[e_, th], w2h_ap[wp], {})],
                    reads=[w2h_buf[wp]], writes=[cb2])
                cache_buf[key] = cb2
            else:
                dst("act", f"w2_{wp}", [(w2h_ap[wp], w2c_h[e_, th], {})],
                    reads=[cache_buf[key]], writes=[w2h_buf[wp]])
            return wp

        def f_chunk_gen(fi, c, par):
            c0 = c * C
            ch = c
            rot = [0]
            FBANKS = [0, 1, 2, 4, 5]

            def nb():
                b = FBANKS[rot[0] % len(FBANKS)]
                rot[0] += 1
                return b

            flat13 = []
            for gi, grp in enumerate(GROUPS):
                for si, e_ in enumerate(grp):
                    flat13.append((e_, 0))
                    flat13.append((e_, 1))
            p13 = {}
            p13[0] = stream_w13(GROUPS[0][0], 0)
            p13[1] = stream_w13(GROUPS[0][0], 1)
            next13 = [2]

            def adv13():
                if next13[0] < len(flat13):
                    e_, hf = flat13[next13[0]]
                    p13[next13[0]] = stream_w13(e_, hf)
                    next13[0] += 1

            flat2 = []
            for gi, grp in enumerate(GROUPS):
                for th in range(2):
                    flat2.append((gi, th, grp[-1]))
            p2 = {}
            next2 = [0]

            def adv2():
                if next2[0] < len(flat2):
                    gi, th, e_last = flat2[next2[0]]
                    p2[next2[0]] = stream_w2(e_last, th)
                    next2[0] += 1
            adv2()
            adv2()

            EXP = [g[0] for g in GROUPS]
            k13 = [0]

            def hpart(j, e_):
                aset = j % 2
                if e_ < NE:
                    b = nb()
                    cst("pe", lambda e, e_=e_, b=b: e.matmul(
                        pb_t[b][:], identB[0:16, e_:e_ + 1].broadcast_to([16, 128]),
                        comb_ap[par][:], start=True, stop=True),
                        reads=[identB_b, comb_buf[par]], writes=[pb_b[b]])
                    cst("act", lambda e, aset=aset, b=b: e.copy(cb_ap[aset], pb_t[b][:]),
                        reads=[pb_b[b], cb_buf[aset]], writes=[cb_buf[aset]])
                for hb in range(4):
                    hf = hb // 2
                    idx13 = k13[0] * 2 + hf
                    wp = p13[idx13]
                    b1 = nb()
                    b3 = nb()
                    hoff = (hb % 2) * 128
                    def _h1(e, wp=wp, hoff=hoff, b1=b1):
                        for kb in range(KB):
                            inst = e.matmul(
                                pb_t[b1][:],
                                w1h_ap[wp][:, kb * 256 + hoff: kb * 256 + hoff + 128],
                                xln_ap[par][kb], start=(kb == 0), stop=(kb == KB - 1))
                        return inst
                    cst("pe", _h1, reads=[w1h_buf[wp]] + xln_buf[par],
                        writes=[pb_b[b1]])
                    def _h3(e, wp=wp, hoff=hoff, b3=b3):
                        for kb in range(KB):
                            inst = e.matmul(
                                pb_t[b3][:],
                                w3h_ap[wp][:, kb * 256 + hoff: kb * 256 + hoff + 128],
                                xln_ap[par][kb], start=(kb == 0), stop=(kb == KB - 1))
                        return inst
                    cst("pe", _h3, reads=[w3h_buf[wp]] + xln_buf[par],
                        writes=[pb_b[b3]])
                    spar = hb % 2
                    cst("act", lambda e, b1=b1, spar=spar: e.activation(
                        sil_ap[spar], pb_t[b1][:], AF.Silu),
                        reads=[pb_b[b1], sil_buf[spar]], writes=[sil_buf[spar]])
                    if e_ < NE:
                        cst("dve", lambda e, b3=b3, spar=spar, aset=aset:
                            e.tensor_tensor(h3c_ap[spar], pb_t[b3][:],
                                            cb_ap[aset], ALU.mult),
                            reads=[pb_b[b3], cb_buf[aset], h3c_buf[spar]],
                            writes=[h3c_buf[spar]])
                    else:
                        cst("dve", lambda e, b3=b3, spar=spar: e.tensor_copy(
                            h3c_ap[spar], pb_t[b3][:]),
                            reads=[pb_b[b3], h3c_buf[spar]], writes=[h3c_buf[spar]])
                    cst("pool", lambda e, aset=aset, hb=hb, spar=spar:
                        e.tensor_tensor(act_ap[aset][hb], sil_ap[spar],
                                        h3c_ap[spar], ALU.mult),
                        reads=[sil_buf[spar], h3c_buf[spar], act_buf[aset][hb]],
                        writes=[act_buf[aset][hb]])
                    if hb == 1 or hb == 3:
                        adv13()
                    yield "h"
                k13[0] += 1

            def epart(j, e_):
                aset = j % 2
                for db in range(KB):
                    th = db // 4
                    wp2 = p2[j * 2 + th]
                    eb = nb()
                    doff = (db % 4) * 128
                    def _eo(e, wp2=wp2, doff=doff, eb=eb, aset=aset):
                        for hb in range(4):
                            inst = e.matmul(
                                pb_t[eb][:],
                                w2h_ap[wp2][:, hb * 512 + doff: hb * 512 + doff + 128],
                                act_ap[aset][hb],
                                start=(hb == 0), stop=(hb == 3))
                        return inst
                    cst("pe", _eo,
                        reads=[w2h_buf[wp2]] + [act_buf[aset][hb] for hb in range(4)],
                        writes=[pb_b[eb]])
                    cst("dve", lambda e, db=db, eb=eb: e.scalar_tensor_tensor(
                        x_t[db][:, c0:c0 + C], pb_t[eb][:], 0.5,
                        x_t[db][:, c0:c0 + C], ALU.mult, ALU.add),
                        reads=[pb_b[eb], x_b[db][ch]], writes=[x_b[db][ch]])
                    if db == 3:
                        adv2()
                    if db == 7:
                        adv2()
                    yield "e"

            yield from hpart(0, EXP[0])
            for j in range(1, len(EXP)):
                yield from hpart(j, EXP[j])
                yield from epart(j - 1, EXP[j - 1])
            yield from epart(len(EXP) - 1, EXP[-1])

        def f_head_units(fi, c, par):
            c0 = c * C
            lu = ln_units(c0, xln_ap[par], xln_buf[par], rowsF_ap, rowsF_buf, 3)
            ru = routing_units(c0, par)
            skip = lambda: None
            u = [lu[0], lu[1], skip, skip, lu[2], lu[3],
                 ru[0], ru[1], skip, skip, ru[2]]
            return u

        def m_chunk_gen(c):
            c0 = c * C
            ch = c

            def stream_ip(mb):
                pp = mb % 2
                key = ("ip", mb)
                if (not USE_CACHE) or key not in cache_buf:
                    dst("pool", f"mipc_{pp}",
                        [(ipw_ap[pp].rearrange("p (k m) -> p k m", k=KB),
                          ipT_h[:, mb * 128:(mb + 1) * 128].rearrange(
                              "(k p) m -> p k m", p=128), {})],
                        writes=[ipw_buf[pp]])
                    cbi = Buf(f"cip_{mb}")
                    dst("sp", f"wbip_{pp}", [(ipc_h[mb], ipw_ap[pp], {})],
                        reads=[ipw_buf[pp]], writes=[cbi])
                    cache_buf[key] = cbi
                else:
                    dst("sp", f"mip_{pp}", [(ipw_ap[pp], ipc_h[mb], {})],
                        reads=[cache_buf[key]], writes=[ipw_buf[pp]])

            def stream_dp(r):
                pp = r % 2
                key = ("dp", r)
                if (not USE_CACHE) or key not in cache_buf:
                    dst("pool", f"mdpc_{pp}",
                        [(dpw_ap[pp].rearrange("p (k m) -> p k m", k=KB),
                          dpT_h[:, r * 128:(r + 1) * 128].rearrange(
                              "(k p) m -> p k m", p=128), {})],
                        writes=[dpw_buf[pp]])
                    cbd = Buf(f"cdp_{r}")
                    dst("sp", f"wbdp_{pp}", [(dpc_h[r], dpw_ap[pp], {})],
                        reads=[dpw_buf[pp]], writes=[cbd])
                    cache_buf[key] = cbd
                else:
                    dst("sp", f"mdp_{pp}", [(dpw_ap[pp], dpc_h[r], {})],
                        reads=[cache_buf[key]], writes=[dpw_buf[pp]])

            for u in ln_units(c0, mxln_ap, mxln_buf, rowsM_ap, rowsM_buf, 6):
                u()
                yield
            if ch > 0:
                for r in range(KB):
                    cst("dve", lambda e, r=r: e.tensor_copy(
                        cpad_ap[r][:, 1:4], cpad_ap[r][:, C + 1:C + 4]),
                        reads=[cpad_buf[r]], writes=[cpad_buf[r]])
            yield
            stream_ip(0)
            stream_ip(1)
            for mb in range(KB):
                pp = mb % 2
                bank = 6 + pp
                def _ipmm(e, pp=pp, bank=bank):
                    for kb in range(KB):
                        inst = e.matmul(pb_t[bank][:],
                                        ipw_ap[pp][:, kb * 128:(kb + 1) * 128],
                                        mxln_ap[kb],
                                        start=(kb == 0), stop=(kb == KB - 1))
                    return inst
                cst("pe", _ipmm, reads=[ipw_buf[pp]] + mxln_buf,
                    writes=[pb_b[bank]])
                cst("dve", lambda e, mb=mb, bank=bank: e.tensor_scalar(
                    cpad_ap[mb][:, 4:4 + C], pb_t[bank][:],
                    ipb_sb[:, mb:mb + 1], None, ALU.add),
                    reads=[pb_b[bank], ipb_b, cpad_buf[mb]], writes=[cpad_buf[mb]])
                if mb + 2 < KB:
                    stream_ip(mb + 2)
                yield
            def _bmm(e):
                for kb in range(KB):
                    inst = e.matmul(pb_t[6][:], bpw_ap[:, kb * 128:(kb + 1) * 128],
                                    cpad_ap[kb][:, 4:4 + C],
                                    start=(kb == 0), stop=(kb == KB - 1))
                return inst
            cst("pe", _bmm, reads=[bpw_buf] + cpad_buf, writes=[pb_b[6]])
            cchain("dve", [
                lambda e: e.tensor_scalar(vs_ap[0], pb_t[6][:], bpb_sb[:], None, ALU.add),
                lambda e: e.tensor_copy(Bh_ap, vs_ap[0]),
                lambda e: e.tensor_tensor(vs_ap[1], vs_ap[0], Bh_ap, ALU.subtract),
                lambda e: e.tensor_copy(Bl_ap, vs_ap[1]),
            ], reads=[pb_b[6], bpb_b], writes=[Bh_buf, Bl_buf, vs_buf[0], vs_buf[1]])
            yield
            stream_dp(0)
            stream_dp(1)

            def delta_unit(r):
                rp = r % 2
                pp = r % 2
                bank = 6 + pp
                s0, s0b = dscr_ap[0], dscr_buf[0]
                def _dmm(e, pp=pp, bank=bank):
                    for kb in range(KB):
                        inst = e.matmul(pb_t[bank][:],
                                        dpw_ap[pp][:, kb * 128:(kb + 1) * 128],
                                        cpad_ap[kb][:, 4:4 + C],
                                        start=(kb == 0), stop=(kb == KB - 1))
                    return inst
                cst("pe", _dmm, reads=[dpw_buf[pp]] + cpad_buf,
                    writes=[pb_b[bank]])
                cchain("act", [
                    (lambda e, bank=bank, s0=s0: e.activation(
                        s0, pb_t[bank][:], AF.Abs)),
                    (lambda e, s0=s0: e.activation(s0, s0, AF.Exp, scale=-1.0)),
                    (lambda e, s0=s0: e.activation(s0, s0, AF.Ln, bias=1.0)),
                ], reads=[pb_b[bank]], writes=[s0b])
                cchain("dve", [
                    (lambda e, bank=bank, rp=rp: e.tensor_scalar(
                        delta_ap[rp], pb_t[bank][:], 0.0, None, ALU.max)),
                    (lambda e, rp=rp, s0=s0: e.tensor_tensor(
                        delta_ap[rp], delta_ap[rp], s0, ALU.add)),
                ], reads=[pb_b[bank], s0b], writes=[delta_buf[rp]])
                if r + 2 < KB:
                    stream_dp(r + 2)

            def convdu_unit(r):
                rp = r % 2
                cs, csb = vs_ap[0], vs_buf[0]
                cvfns = [(lambda e, r=r, cs=cs: e.tensor_scalar(
                    cs, cpad_ap[r][:, 1:1 + C],
                    conv_sb[:, r * 4:r * 4 + 1], None, ALU.mult))]
                for kk in range(1, 4):
                    cvfns.append(lambda e, r=r, cs=cs, kk=kk: e.scalar_tensor_tensor(
                        cs, cpad_ap[r][:, 1 + kk:1 + kk + C],
                        conv_sb[:, r * 4 + kk:r * 4 + kk + 1], cs,
                        ALU.mult, ALU.add))
                cchain("dve", cvfns, reads=[cpad_buf[r], conv_wb], writes=[csb])
                cst("act", lambda e, r=r, cs=cs: e.activation(
                    cs, cs, AF.Silu, bias=cvb_sb[:, r:r + 1]),
                    reads=[csb, cvb_b], writes=[csb])
                cst("dve", lambda e, rp=rp, cs=cs: e.tensor_tensor(
                    du_ap[rp], delta_ap[rp], cs, ALU.mult),
                    reads=[delta_buf[rp], csb, du_buf[rp]], writes=[du_buf[rp]])

            delta_unit(0)
            yield
            convdu_unit(0)
            yield
            for r in range(KB):
                rp = r % 2
                if r + 1 < KB:
                    delta_unit(r + 1)
                    yield
                    convdu_unit(r + 1)
                    yield
                stream_ip(8 + r)
                for pn in range(NST // 2):
                    for spar in range(2):
                        idx = r * NST + 2 * pn + spar
                        bank = 6 + spar
                        def _bc2(e, idx=idx, bank=bank):
                            e.matmul(pb_t[bank][:],
                                     identB[:, idx:idx + 1].broadcast_to([128, 128]),
                                     Bh_ap, start=True, stop=False)
                            return e.matmul(pb_t[bank][:],
                                            identB[:, idx:idx + 1].broadcast_to([128, 128]),
                                            Bl_ap, start=False, stop=True)
                        cst("pe", _bc2, reads=[identB_b, Bh_buf, Bl_buf],
                            writes=[pb_b[bank]])
                    for spar in range(2):
                        idx = r * NST + 2 * pn + spar
                        cst("act", lambda e, rp=rp, idx=idx, spar=spar: e.activation(
                            dA_ap[spar], delta_ap[rp], AF.Exp,
                            scale=A_sb[:, idx:idx + 1]),
                            reads=[delta_buf[rp], A_b, dA_buf[spar]],
                            writes=[dA_buf[spar]])
                    for spar in range(2):
                        idx = r * NST + 2 * pn + spar
                        bank = 6 + spar
                        cst("dve", lambda e, rp=rp, spar=spar, bank=bank, idx=idx:
                            e.scalar_tensor_tensor(
                                dbu_ap[spar], du_ap[rp], c_sb[:, idx:idx + 1],
                                pb_t[bank][:], ALU.mult, ALU.mult),
                            reads=[du_buf[rp], c_b, pb_b[bank], dbu_buf[spar]],
                            writes=[dbu_buf[spar]])
                    for spar in range(2):
                        idx = r * NST + 2 * pn + spar
                        init = 0.0 if ch == 0 else carry[:, idx:idx + 1]
                        cst("dve", lambda e, spar=spar, init=init: e.tensor_tensor_scan(
                            dbu_ap[spar], dA_ap[spar], dbu_ap[spar], init,
                            ALU.mult, ALU.add),
                            reads=[dA_buf[spar], carry_b, dbu_buf[spar]],
                            writes=[dbu_buf[spar]])
                    for spar in range(2):
                        idx = r * NST + 2 * pn + spar
                        yeng = "pool" if spar == 0 else "dve"
                        if pn == 0:
                            cst(yeng, lambda e, spar=spar: e.tensor_copy(
                                y_ap[spar], dbu_ap[spar]),
                                reads=[dbu_buf[spar], y_buf[spar]],
                                writes=[y_buf[spar]])
                        else:
                            cst(yeng, lambda e, spar=spar: e.tensor_tensor(
                                y_ap[spar], y_ap[spar], dbu_ap[spar], ALU.add),
                                reads=[dbu_buf[spar], y_buf[spar]],
                                writes=[y_buf[spar]])
                        cst("pool", lambda e, idx=idx, spar=spar: e.tensor_copy(
                            carry[:, idx:idx + 1], dbu_ap[spar][:, C - 1:C]),
                            reads=[dbu_buf[spar], carry_b], writes=[carry_b])
                    yield
                cst("dve", lambda e: e.tensor_tensor(
                    y_ap[0], y_ap[0], y_ap[1], ALU.add),
                    reads=[y_buf[1], y_buf[0]], writes=[y_buf[0]])
                pp = r % 2
                bank = 6 + pp
                def _vmm(e, pp=pp, bank=bank):
                    for kb in range(KB):
                        inst = e.matmul(pb_t[bank][:],
                                        ipw_ap[pp][:, kb * 128:(kb + 1) * 128],
                                        mxln_ap[kb],
                                        start=(kb == 0), stop=(kb == KB - 1))
                    return inst
                cst("pe", _vmm, reads=[ipw_buf[pp]] + mxln_buf,
                    writes=[pb_b[bank]])
                vsb, vsbb = vs_ap[0], vs_buf[0]
                t3, t3b = vs_ap[1], vs_buf[1]
                mm_, mmb = vs_ap[2], vs_buf[2]
                mb = 8 + r
                cst("dve", lambda e, mb=mb, bank=bank: e.tensor_scalar(
                    vsb, pb_t[bank][:], ipb_sb[:, mb:mb + 1], None, ALU.add),
                    reads=[pb_b[bank], ipb_b, vsbb], writes=[vsbb])
                cchain("act", [
                    lambda e: e.activation(t3, vsb, AF.Abs),
                    lambda e: e.activation(t3, t3, AF.Exp, scale=-1.0),
                    lambda e: e.activation(t3, t3, AF.Ln, bias=1.0),
                ], reads=[vsbb], writes=[t3b])
                cchain("dve", [
                    lambda e: e.tensor_scalar(mm_, vsb, 0.0, None, ALU.min),
                    lambda e: e.tensor_tensor(t3, t3, mm_, ALU.subtract),
                ], reads=[vsbb], writes=[mmb, t3b])
                cst("act", lambda e: e.activation(t3, t3, AF.Exp, scale=-1.0),
                    reads=[t3b], writes=[t3b])
                cchain("pool", [
                    (lambda e: e.tensor_tensor(
                        mm_, y_ap[0], vsb, ALU.mult)),
                    (lambda e: e.tensor_tensor(mm_, mm_, t3, ALU.mult)),
                    (lambda e, r=r, c0=c0: e.tensor_tensor(
                        x_t[r][:, c0:c0 + C], x_t[r][:, c0:c0 + C],
                        mm_, ALU.add)),
                ], reads=[y_buf[0], vsbb, t3b], writes=[mmb, x_b[r][ch]])
                yield

        setup()
        sched.barrier()
        stream_w13(GROUPS[0][0], 0)
        stream_w13(GROUPS[0][0], 1)
        stream_w2(GROUPS[0][0], 0)
        stream_w2(GROUPS[0][0], 1)

        m_gens = [m_chunk_gen(c) for c in range(NCH)]
        m_ready = []
        m_done = [False] * NCH

        def drain_m(k):
            n = 0
            while n < k and m_ready:
                c, g = m_ready[0]
                try:
                    next(g)
                    n += 1
                except StopIteration:
                    m_done[c] = True
                    m_ready.pop(0)

        def drain_m_until_done(c):
            while not m_done[c]:
                drain_m(1)
                if not m_ready:
                    break

        fi0, c0_ = F_ORDER[0]
        for u in f_head_units(fi0, c0_, 0):
            u()
        for k, (fi, c) in enumerate(F_ORDER):
            par = k % 2
            if k + 1 < len(F_ORDER):
                nfi, ncx = F_ORDER[k + 1]
                nhead = f_head_units(nfi, ncx, (k + 1) % 2)
            else:
                nfi, ncx, nhead = None, None, []
            nhq = [0]

            def poke_head():
                if nhq[0] < len(nhead) and (nfi == "f1" or m_done[ncx]):
                    nhead[nhq[0]]()
                    nhq[0] += 1

            gen = f_chunk_gen(fi, c, par)
            beat = [0]
            for tag in gen:
                if tag == "h" and ZIP_M:
                    beat[0] += 1
                    drain_m(2 if beat[0] % 2 == 0 else 1)
                poke_head()
            if fi == "f1":
                m_ready.append((c, m_gens[c]))
                if not ZIP_M:
                    drain_m_until_done(c)
            if nfi == "f2":
                drain_m_until_done(ncx)
            while nhq[0] < len(nhead):
                nhead[nhq[0]]()
                nhq[0] += 1
            if fi == "f2":
                def _st(e, c=c):
                    for kb in range(KB):
                        e.dma_start(out=yT_h[kb * 128:(kb + 1) * 128, c * C:(c + 1) * C],
                                    in_=x_t[kb][:, c * C:(c + 1) * C]
                                    ).then_inc(SEM["yst"], 16)
                getsem("yst")
                sched.stage("sp", _st, reads=[x_b[kb][c] for kb in range(KB)],
                            dmaq="yst", ndma=KB)
        while m_ready:
            drain_m(1)

        final_qx = sched.cnt["yst"]

        def mk_body(engkey):
            def run(eng):
                for waits, emit, dmaq in sched.plan[engkey]:
                    for k, v in waits.items():
                        eng.wait_ge(SEM[k], v)
                    emit(eng)
                eng.wait_ge(SEM["yst"], final_qx)
            return run

        blk.tensor(mk_body("pe"))
        blk.scalar(mk_body("act"))
        blk.vector(mk_body("dve"))
        blk.gpsimd(mk_body("pool"))
        blk.sync(mk_body("sp"))

    nstages = {k: len(v) for k, v in sched.plan.items()}
    print(f"[kernel] stages={nstages} arena_peak={AR.peak * 4}B/part")
    return nc


_cache = {}


def kernel(**inputs):
    x = np.asarray(inputs["L0_attn_x"])
    cc = np.ascontiguousarray
    shared = {
        "ipT": cc(np.asarray(inputs["L0_attn_in_proj_w"]).T),
        "dpT": cc(np.asarray(inputs["L0_attn_delta_proj_w"]).T),
        "bpT": cc(np.asarray(inputs["L0_attn_b_proj_w"]).T),
        "gwT": cc(np.asarray(inputs["L0_ffn_gate_w"]).T),
        "w1T": cc(np.asarray(inputs["L0_ffn_w1"]).transpose(0, 2, 1)),
        "w3T": cc(np.asarray(inputs["L0_ffn_w3"]).transpose(0, 2, 1)),
        "w2T": cc(np.asarray(inputs["L0_ffn_w2"]).transpose(0, 2, 1)),
        "sw1T": cc(np.asarray(inputs["L0_ffn_sw1"]).T),
        "sw3T": cc(np.asarray(inputs["L0_ffn_sw3"]).T),
        "sw2T": cc(np.asarray(inputs["L0_ffn_sw2"]).T),
        "cT": cc(np.asarray(inputs["L0_attn_c_param"]).T),
        "alog": cc(np.asarray(inputs["L0_attn_a_log"])),
        "convw": cc(np.asarray(inputs["L0_attn_conv_w"]).reshape(D, 4)),
        "ipb": cc(np.asarray(inputs["L0_attn_in_proj_b"]).reshape(16, 128)),
        "convb": cc(np.asarray(inputs["L0_attn_conv_b"]).reshape(8, 128)),
        "bpb": cc(np.asarray(inputs["L0_attn_b_proj_b"]).reshape(1, 128)),
    }
    in_maps = []
    for b in range(8):
        m = dict(shared)
        m["xT"] = cc(x[b].T)
        in_maps.append(m)

    key = (SERIAL, ZIP_M, USE_CACHE)
    if key not in _cache:
        _cache[key] = build_program()
    nc = _cache[key]
    res = run_bass_kernel_spmd(nc, in_maps, core_ids=list(range(8)))
    out = np.empty((8, T, D), np.float32)
    for b in range(8):
        out[b] = res.results[b]["yT"].T
    return out


# version 60
# speedup vs baseline: 1.0835x; 1.0185x over previous
import contextlib
import numpy as np
import concourse.bass as bass
import concourse.mybir as mybir
from concourse.bass_utils import run_bass_kernel_spmd

F32 = mybir.dt.float32
BF16 = mybir.dt.bfloat16
I32 = mybir.dt.int32
AF = mybir.ActivationFunctionType
ALU = mybir.AluOpType
AX = mybir.AxisListType

D = 1024
T = 2048
KB = 8
C = 512
NCH = T // C
H = 512
NE = 16
NST = 16
EPS = 1e-5
ARENA_WORDS = 35400
SERIAL = False
ZIP_M = True
USE_CACHE = True

ENGS = ("pe", "act", "dve", "pool", "sp")

F_ORDER = [("f1", 0), ("f1", 1), ("f1", 2), ("f2", 0),
           ("f1", 3), ("f2", 1), ("f2", 2), ("f2", 3)]
GROUPS = [[i] for i in range(17)]


class Buf:
    __slots__ = ("name", "wr", "rds")

    def __init__(self, name):
        self.name = name
        self.wr = None
        self.rds = []


class Sched:
    def __init__(self):
        self.plan = {k: [] for k in ENGS}
        self.cnt = {k: 0 for k in ENGS}
        self.waited = {k: {} for k in ENGS}
        self.bar = {}

    def stage(self, eng, emit, reads=(), writes=(), dmaq=None, ndma=0):
        if SERIAL:
            self.bar = dict(self.cnt)
        deps = dict(self.bar)

        def need(ev):
            if ev is None:
                return
            k, v = ev
            if deps.get(k, 0) < v:
                deps[k] = v
        for b in reads:
            need(b.wr)
        for b in writes:
            need(b.wr)
            for ev in b.rds:
                need(ev)
        if eng == "pe":
            deps.pop("pe", None)
        w = self.waited[eng]
        fin = {}
        for k, v in deps.items():
            if w.get(k, 0) >= v:
                continue
            w[k] = v
            fin[k] = v
        if dmaq is None:
            self.cnt[eng] += 1
            ev = (eng, self.cnt[eng])
        else:
            assert ndma > 0
            self.cnt[dmaq] = self.cnt.get(dmaq, 0) + 16 * ndma
            ev = (dmaq, self.cnt[dmaq])
        self.plan[eng].append((fin, emit, dmaq))
        for b in writes:
            b.wr = ev
            b.rds = []
        for b in reads:
            b.rds.append(ev)
        return ev

    def barrier(self):
        self.bar = dict(self.cnt)


class Arena:
    def __init__(self, t, words):
        self.t = t
        self.words = words
        self.off = 0
        self.peak = 0

    def alloc(self, name, p, cols, dtype):
        if dtype in (BF16,):
            w = (cols + 1) // 2
        else:
            w = cols
        self.off = (self.off + 15) & ~15
        a = self.off
        self.off += w
        self.peak = max(self.peak, self.off)
        assert self.off <= self.words, f"arena overflow at {name}: {self.off * 4}B"
        view = self.t[0:p, a:a + w]
        if dtype == BF16:
            view = view.bitcast(BF16)[:, 0:cols]
        elif dtype == I32:
            view = view.bitcast(I32)
        return view, Buf(name)


def build_program():
    nc = bass.Bass("TRN2", target_bir_lowering=False, detect_race_conditions=False)

    def din(name, shape):
        return nc.dram_tensor(name, list(shape), F32, kind="ExternalInput").ap()

    xT_h = din("xT", (D, T))
    ipT_h = din("ipT", (D, 2 * D))
    dpT_h = din("dpT", (D, D))
    bpT_h = din("bpT", (D, 128))
    gwT_h = din("gwT", (D, NE))
    w1T_h = din("w1T", (NE, D, H))
    w3T_h = din("w3T", (NE, D, H))
    w2T_h = din("w2T", (NE, H, D))
    sw1T_h = din("sw1T", (D, H))
    sw3T_h = din("sw3T", (D, H))
    sw2T_h = din("sw2T", (H, D))
    cT_h = din("cT", (NST, D))
    alog_h = din("alog", (D, NST))
    convw_h = din("convw", (D, 4))
    ipb_h = din("ipb", (16, 128))
    convb_h = din("convb", (8, 128))
    bpb_h = din("bpb", (1, 128))
    yT_h = nc.dram_tensor("yT", [D, T], F32, kind="ExternalOutput").ap()
    w1c_h = nc.dram_tensor("w1c", [NE + 1, 2, 128, KB * 256], BF16, kind="Internal").ap()
    w3c_h = nc.dram_tensor("w3c", [NE + 1, 2, 128, KB * 256], BF16, kind="Internal").ap()
    w2c_h = nc.dram_tensor("w2c", [NE + 1, 2, 128, 4 * 512], BF16, kind="Internal").ap()
    ipc_h = nc.dram_tensor("ipc", [16, 128, KB * 128], BF16, kind="Internal").ap()
    dpc_h = nc.dram_tensor("dpc", [KB, 128, KB * 128], BF16, kind="Internal").ap()

    sched = Sched()
    es = contextlib.ExitStack()
    with es:
        def sb(name, p, cols, dtype):
            t = es.enter_context(nc.sbuf_tensor(name, [p, cols], dtype))
            return t, Buf(name)

        x_t = []
        x_b = []
        for kb in range(KB):
            t = es.enter_context(nc.sbuf_tensor(f"x{kb}", [128, T], F32))
            x_t.append(t)
            x_b.append([Buf(f"x{kb}_{ch}") for ch in range(NCH)])
        cpool, cpool_b = sb("cpool", 128, 8, F32)
        CONSTS = [0.0, 1.0, -1.0, 0.5, EPS]
        for j, v in enumerate(CONSTS):
            nc.const_aps.register(F32, v, cpool[:, j:j + 1])
        ones1f, ones1f_b = sb("ones1f", 1, 128, F32)
        ones128, ones128_b = sb("ones128", 128, 1, F32)
        identF, identF_b = sb("identF", 128, 128, F32)
        identB, identB_b = sb("identB", 128, 128, BF16)
        iotat, iotat_b = sb("iotat", 128, 128, I32)
        A_sb, A_b = sb("A_sb", 128, 128, F32)
        c_sb, c_b = sb("c_sb", 128, 128, F32)
        conv_sb, conv_wb = sb("conv_sb", 128, 32, F32)
        ipb_sb, ipb_b = sb("ipb_sb", 128, 16, F32)
        cvb_sb, cvb_b = sb("cvb_sb", 128, 8, F32)
        bpb_sb, bpb_b = sb("bpb_sb", 128, 1, F32)
        gw_sb, gw_b = sb("gw_sb", 128, 8 * NE, F32)
        Sbc_sb, Sbc_b = sb("Sbc_sb", 128, NE, F32)
        carry, carry_b = sb("carry", 128, 128, F32)
        btmp, btmp_b = sb("btmp", 16, 128, F32)
        srow, srow_b = sb("srow", 1, 128, F32)

        arena_t = es.enter_context(nc.sbuf_tensor("arena", [128, ARENA_WORDS], F32))
        AR = Arena(arena_t, ARENA_WORDS)

        pb_t, pb_b = [], []
        for i in range(8):
            t = es.enter_context(nc.psum_tensor(f"pb{i}", [128, 512], F32))
            pb_t.append(t)
            pb_b.append(Buf(f"pb{i}"))

        SEM = {}
        for k in ENGS:
            SEM[k] = es.enter_context(nc.semaphore(f"sem_{k}"))

        def getsem(q):
            if q not in SEM:
                SEM[q] = es.enter_context(nc.semaphore(f"sem_{q}"))
            return SEM[q]
        blk = es.enter_context(nc.Block())

        def cst(eng, fn, reads=(), writes=()):
            def emit(e, fn=fn, eng=eng):
                inst = fn(e)
                inst.then_inc(SEM[eng], 1)
            return sched.stage(eng, emit, reads, writes)

        def cchain(eng, fns, reads=(), writes=()):
            rw = list(reads) + list(writes)
            for fn in fns:
                cst(eng, fn, reads=rw, writes=list(writes))

        def dst(eng, q, transfers, reads=(), writes=()):
            getsem(q)
            def emit(e, transfers=transfers, q=q):
                for dstp, srcp, kw in transfers:
                    e.dma_start(out=dstp, in_=srcp, **kw).then_inc(SEM[q], 16)
            return sched.stage(eng, emit, reads, writes, dmaq=q, ndma=len(transfers))

        _ct_off = AR.off
        ct16, ct16_b = AR.alloc("ct16", 16, D, F32)
        AR.off = _ct_off
        xln_ap = [[None] * KB for _ in range(2)]
        xln_buf = [[None] * KB for _ in range(2)]
        for i in range(2):
            for kb in range(KB):
                xln_ap[i][kb], xln_buf[i][kb] = AR.alloc(f"xln{i}_{kb}", 128, C, BF16)
        act_ap = [[None] * 4 for _ in range(2)]
        act_buf = [[None] * 4 for _ in range(2)]
        for s in range(2):
            for hb in range(4):
                act_ap[s][hb], act_buf[s][hb] = AR.alloc(f"act{s}_{hb}", 128, C, BF16)
        w1h_ap, w1h_buf, w3h_ap, w3h_buf = [], [], [], []
        for i in range(2):
            a, b = AR.alloc(f"w1h{i}", 128, KB * 256, BF16)
            w1h_ap.append(a)
            w1h_buf.append(b)
            a, b = AR.alloc(f"w3h{i}", 128, KB * 256, BF16)
            w3h_ap.append(a)
            w3h_buf.append(b)
        w2h_ap, w2h_buf = [], []
        for i in range(3):
            a, b = AR.alloc(f"w2h{i}", 128, 4 * 512, BF16)
            w2h_ap.append(a)
            w2h_buf.append(b)
        cb_ap, cb_buf = [], []
        for i in range(2):
            a, b = AR.alloc(f"cb{i}", 128, C, F32)
            cb_ap.append(a)
            cb_buf.append(b)
        sil_ap, sil_buf, h3c_ap, h3c_buf = [], [], [], []
        for i in range(2):
            a, b = AR.alloc(f"sil{i}", 128, C, F32)
            sil_ap.append(a)
            sil_buf.append(b)
            a, b = AR.alloc(f"h3c{i}", 128, C, F32)
            h3c_ap.append(a)
            h3c_buf.append(b)
        comb_ap, comb_buf = [], []
        for i in range(2):
            a, b = AR.alloc(f"comb{i}", 16, C, BF16)
            comb_ap.append(a)
            comb_buf.append(b)
        rowsF_ap, rowsF_buf = AR.alloc("rowsF", 1, 2 * C, F32)
        sq_ap, sq_buf = [], []
        for i in range(2):
            a, b = AR.alloc(f"sqln{i}", 128, C, F32)
            sq_ap.append(a)
            sq_buf.append(b)
        tmp_ap, tmp_buf = [], []
        for i in range(2):
            a, b = AR.alloc(f"tmpln{i}", 128, 256, F32)
            tmp_ap.append(a)
            tmp_buf.append(b)
        tk_n1, tk_n1_b = AR.alloc("tkn1", 128, 4 * NE, F32)
        TKB = {}
        for nmv in ("l", "el", "t1", "sb"):
            TKB[nmv] = AR.alloc(f"tkb{nmv}", 128, 4 * NE, F32)
        TKB["vv"] = TKB["l"]
        TKB["eqi"] = AR.alloc("tkbeqi", 128, 4 * NE, I32)
        TKB["c1"] = AR.alloc("tkbc1", 128, 4, F32)
        TKB["c2"] = AR.alloc("tkbc2", 128, 4, F32)
        TKB["rc"] = AR.alloc("tkbrc", 128, 8, F32)
        mxln_ap, mxln_buf = [], []
        for kb in range(KB):
            a, b = AR.alloc(f"mxln{kb}", 128, C, BF16)
            mxln_ap.append(a)
            mxln_buf.append(b)
        cpad_ap, cpad_buf = [], []
        for r in range(KB):
            a, b = AR.alloc(f"cpad{r}", 128, C + 4, BF16)
            cpad_ap.append(a)
            cpad_buf.append(b)
        ipw_ap, ipw_buf, dpw_ap, dpw_buf = [], [], [], []
        for i in range(2):
            a, b = AR.alloc(f"ipw{i}", 128, KB * 128, BF16)
            ipw_ap.append(a)
            ipw_buf.append(b)
            a, b = AR.alloc(f"dpw{i}", 128, KB * 128, BF16)
            dpw_ap.append(a)
            dpw_buf.append(b)
        bpw_ap, bpw_buf = AR.alloc("bpw", 128, KB * 128, BF16)
        Bh_ap, Bh_buf = AR.alloc("Bh", 128, C, BF16)
        Bl_ap, Bl_buf = AR.alloc("Bl", 128, C, BF16)
        delta_ap, delta_buf, du_ap, du_buf = [], [], [], []
        dA_ap, dA_buf, dbu_ap, dbu_buf = [], [], [], []
        y_ap, y_buf = [], []
        for i in range(2):
            a, b = AR.alloc(f"delta{i}", 128, C, F32)
            delta_ap.append(a)
            delta_buf.append(b)
            a, b = AR.alloc(f"du{i}", 128, C, F32)
            du_ap.append(a)
            du_buf.append(b)
            a, b = AR.alloc(f"dA{i}", 128, C, F32)
            dA_ap.append(a)
            dA_buf.append(b)
            a, b = AR.alloc(f"dbu{i}", 128, C, F32)
            dbu_ap.append(a)
            dbu_buf.append(b)
            a, b = AR.alloc(f"y{i}", 128, C, F32)
            y_ap.append(a)
            y_buf.append(b)
        dscr_ap, dscr_buf = [], []
        for i in range(1):
            a, b = AR.alloc(f"dscr{i}", 128, C, F32)
            dscr_ap.append(a)
            dscr_buf.append(b)
        vs_ap, vs_buf = [], []
        for i in range(3):
            a, b = AR.alloc(f"vs{i}", 128, C, F32)
            vs_ap.append(a)
            vs_buf.append(b)
        rowsM_ap, rowsM_buf = AR.alloc("rowsM", 1, 2 * C, F32)

        def setup():
            cst("dve", lambda e: e.memset(cpool[:], 0.0), writes=[cpool_b])
            for j, v in enumerate(CONSTS):
                if v != 0.0:
                    cst("dve", lambda e, j=j, v=v: e.memset(cpool[:, j:j + 1], v),
                        reads=[cpool_b], writes=[cpool_b])
            cst("dve", lambda e: e.memset(ones1f[:], 1.0), writes=[ones1f_b])
            cst("dve", lambda e: e.memset(ones128[:], 1.0), writes=[ones128_b])
            cst("dve", lambda e: e.memset(carry[:], 0.0), writes=[carry_b])
            cst("dve", lambda e: e.memset(tk_n1, -1.0), writes=[tk_n1_b])
            cst("pool", lambda e: e.iota(iotat[:], pattern=[[1, 128]], base=0,
                                         channel_multiplier=-1), writes=[iotat_b])
            cst("dve", lambda e: e.tensor_single_scalar(identF[:], iotat[:], 0, ALU.is_equal),
                reads=[iotat_b], writes=[identF_b])
            cst("dve", lambda e: e.tensor_copy(identB[:], identF[:]),
                reads=[identF_b], writes=[identB_b])
            for kb in range(KB):
                def _ld(e, kb=kb):
                    e.dma_start(out=x_t[kb][:], in_=xT_h[kb * 128:(kb + 1) * 128, :]
                                ).then_inc(SEM[f"ldx{kb}"], 16)
                getsem(f"ldx{kb}")
                sched.stage("sp", _ld, writes=x_b[kb], dmaq=f"ldx{kb}", ndma=1)
            dst("sp", "lda", [(A_sb[:].rearrange("p (r n) -> p r n", n=NST),
                              alog_h.rearrange("(r p) n -> p r n", p=128), {})],
                writes=[A_b])
            dst("sp", "ldct", [(ct16[:], cT_h, {})], writes=[ct16_b])
            dst("sp", "ldcv", [(conv_sb[:].rearrange("p (r k) -> p r k", k=4),
                              convw_h.rearrange("(r p) k -> p r k", p=128), {})],
                writes=[conv_wb])
            dst("sp", "ldgw", [(gw_sb[:].rearrange("p (r n) -> p r n", n=NE),
                              gwT_h.rearrange("(r p) n -> p r n", p=128), {})],
                writes=[gw_b])
            cst("act", lambda e: e.activation(A_sb[:], A_sb[:], AF.Exp),
                reads=[A_b], writes=[A_b])
            cst("dve", lambda e: e.tensor_single_scalar(A_sb[:], A_sb[:], -1.0, ALU.mult),
                reads=[A_b], writes=[A_b])
            for r in range(KB):
                cst("pe", lambda e, r=r: e.transpose(
                    pb_t[r % 2][:, 0:16], ct16[0:16, r * 128:(r + 1) * 128],
                    identF[0:16, 0:16]),
                    reads=[ct16_b, identF_b], writes=[pb_b[r % 2]])
                cst("dve", lambda e, r=r: e.tensor_copy(
                    c_sb[:, r * 16:(r + 1) * 16], pb_t[r % 2][:, 0:16]),
                    reads=[pb_b[r % 2], c_b], writes=[c_b])
            dst("sp", "ldb1", [(btmp[:], ipb_h, {})], writes=[btmp_b])
            cst("pe", lambda e: e.transpose(pb_t[2][:, 0:16], btmp[:], identF[0:16, 0:16]),
                reads=[btmp_b, identF_b], writes=[pb_b[2]])
            cst("dve", lambda e: e.tensor_copy(ipb_sb[:], pb_t[2][:, 0:16]),
                reads=[pb_b[2]], writes=[ipb_b])
            dst("sp", "ldb2", [(btmp[0:8, :], convb_h, {})], reads=[btmp_b], writes=[btmp_b])
            cst("pe", lambda e: e.transpose(pb_t[3][:, 0:8], btmp[0:8, :], identF[0:8, 0:8]),
                reads=[btmp_b, identF_b], writes=[pb_b[3]])
            cst("dve", lambda e: e.tensor_copy(cvb_sb[:], pb_t[3][:, 0:8]),
                reads=[pb_b[3]], writes=[cvb_b])
            dst("sp", "ldb3", [(srow[:], bpb_h, {})], writes=[srow_b])
            cst("pe", lambda e: e.transpose(pb_t[4][:, 0:1], srow[:], identF[0:1, 0:1]),
                reads=[srow_b, identF_b], writes=[pb_b[4]])
            cst("dve", lambda e: e.tensor_copy(bpb_sb[:], pb_t[4][:, 0:1]),
                reads=[pb_b[4]], writes=[bpb_b])
            def _smm(e):
                for kb in range(KB):
                    inst = e.matmul(pb_t[5][0:1, 0:NE], ones128[:],
                                    gw_sb[:, kb * NE:(kb + 1) * NE],
                                    start=(kb == 0), stop=(kb == KB - 1))
                return inst
            cst("pe", _smm, reads=[ones128_b, gw_b], writes=[pb_b[5]])
            cst("dve", lambda e: e.tensor_copy(srow[0:1, 0:NE], pb_t[5][0:1, 0:NE]),
                reads=[pb_b[5], srow_b], writes=[srow_b])
            cst("pe", lambda e: e.matmul(pb_t[6][:, 0:NE], ones1f[:], srow[0:1, 0:NE],
                                         start=True, stop=True),
                reads=[ones1f_b, srow_b], writes=[pb_b[6]])
            cst("dve", lambda e: e.tensor_copy(Sbc_sb[:], pb_t[6][:, 0:NE]),
                reads=[pb_b[6]], writes=[Sbc_b])
            for r in range(KB):
                cst("dve", lambda e, r=r: e.memset(cpad_ap[r][:, 0:4], 0.0),
                    writes=[cpad_buf[r]])
            dst("pool", "ldbp",
                [(bpw_ap.rearrange("p (k m) -> p k m", k=KB),
                  bpT_h.rearrange("(k p) m -> p k m", p=128), {})],
                writes=[bpw_buf])

        def ln_units(c0, xlnv, xlnb, rows_ap, rows_buf, bank):
            pb, pbb = pb_t[bank], pb_b[bank]
            rstd_r = rows_ap[0:1, 0:C]
            nmr_r = rows_ap[0:1, C:2 * C]
            ch = c0 // C

            def emit_sq(kb):
                sp_ = kb % 2
                cst("pool", lambda e, kb=kb, sp_=sp_: e.tensor_tensor(
                    sq_ap[sp_], x_t[kb][:, c0:c0 + C], x_t[kb][:, c0:c0 + C],
                    ALU.mult),
                    reads=[x_b[kb][ch]], writes=[sq_buf[sp_]])

            def emit_mm(kb):
                sp_ = kb % 2
                def _mm(e, kb=kb, sp_=sp_):
                    e.matmul(pb[0:1, :], ones128[:], x_t[kb][:, c0:c0 + C],
                             start=(kb == 0), stop=(kb == KB - 1))
                    return e.matmul(pb[32:33, :], ones128[:], sq_ap[sp_],
                                    start=(kb == 0), stop=(kb == KB - 1))
                cst("pe", _mm, reads=[ones128_b, x_b[kb][ch], sq_buf[sp_]],
                    writes=[pbb])

            def sums():
                emit_sq(0)
                emit_sq(1)
                for kb in range(KB):
                    emit_mm(kb)
                    if kb + 2 < KB:
                        emit_sq(kb + 2)

            def rows_chain():
                cchain("dve", [
                    lambda e: e.tensor_single_scalar(nmr_r, pb[0:1, :], 1.0 / D, ALU.mult),
                    lambda e: e.tensor_single_scalar(rstd_r, pb[32:33, :], 1.0 / D, ALU.mult),
                    lambda e: e.scalar_tensor_tensor(tmp_ap[0][0:1, 0:256], nmr_r[0:1, 0:256],
                                                     1.0, nmr_r[0:1, 0:256], ALU.mult, ALU.mult),
                    lambda e: e.scalar_tensor_tensor(tmp_ap[1][0:1, 0:256], nmr_r[0:1, 256:512],
                                                     1.0, nmr_r[0:1, 256:512], ALU.mult, ALU.mult),
                    lambda e: e.tensor_tensor(rstd_r[0:1, 0:256], rstd_r[0:1, 0:256],
                                              tmp_ap[0][0:1, 0:256], ALU.subtract),
                    lambda e: e.tensor_tensor(rstd_r[0:1, 256:512], rstd_r[0:1, 256:512],
                                              tmp_ap[1][0:1, 0:256], ALU.subtract),
                ], reads=[pbb], writes=[rows_buf, tmp_buf[0], tmp_buf[1]])
                cchain("act", [
                    lambda e: e.activation(rstd_r, rstd_r, AF.Ln, bias=EPS),
                    lambda e: e.activation(rstd_r, rstd_r, AF.Exp, scale=-0.5),
                ], writes=[rows_buf])
                cst("dve", lambda e: e.scalar_tensor_tensor(
                    nmr_r, nmr_r, -1.0, rstd_r, ALU.mult, ALU.mult), writes=[rows_buf])

            def apply_half(hf):
                co = hf * 256
                def _bc(e, co=co):
                    e.matmul(pb[:, 0:256], ones1f[:], rstd_r[0:1, co:co + 256],
                             start=True, stop=True)
                    return e.matmul(pb[:, 256:512], ones1f[:], nmr_r[0:1, co:co + 256],
                                    start=True, stop=True)
                cst("pe", _bc, reads=[ones1f_b, rows_buf], writes=[pbb])
                for kb in range(KB):
                    tpar = kb % 2
                    cst("dve", lambda e, kb=kb, co=co, tpar=tpar: e.tensor_tensor(
                        tmp_ap[tpar],
                        x_t[kb][:, c0 + co:c0 + co + 256], pb[:, 0:256], ALU.mult),
                        reads=[x_b[kb][ch], pbb], writes=[tmp_buf[tpar]])
                    cst("dve", lambda e, kb=kb, co=co, tpar=tpar: e.tensor_tensor(
                        xlnv[kb][:, co:co + 256],
                        tmp_ap[tpar], pb[:, 256:512], ALU.add),
                        reads=[tmp_buf[tpar], pbb], writes=[xlnb[kb]])

            return [sums, rows_chain, lambda: apply_half(0), lambda: apply_half(1)]

        def routing_units(c0, par):
            ch = c0 // C
            pb, pbb = pb_t[3], pb_b[3]
            rstd_r = rowsF_ap[0:1, 0:C]
            nmr_r = rowsF_ap[0:1, C:2 * C]
            l_ap, l_b = TKB["l"]
            el_ap, el_b = TKB["el"]
            vv_ap, vv_b = TKB["vv"]
            t1_ap, t1_b = TKB["t1"]
            sb_ap, sb_b = TKB["sb"]
            eqi_ap, eqi_b = TKB["eqi"]
            c1_ap, c1_b = TKB["c1"]
            c2_ap, c2_b = TKB["c2"]
            rc_ap, rc_b = TKB["rc"]

            def seg(ap):
                return ap.rearrange("p (t n) -> p t n", n=NE)

            def colb(ap, j):
                return ap

            def unit_a():
                for tb in range(4):
                    tc = c0 + tb * 128
                    def _gmm(e, tc=tc, tb=tb):
                        for kb in range(KB):
                            inst = e.matmul(pb[:, 16 * tb:16 * tb + NE],
                                            x_t[kb][:, tc:tc + 128],
                                            gw_sb[:, kb * NE:(kb + 1) * NE],
                                            start=(kb == 0), stop=(kb == KB - 1))
                        return inst
                    cst("pe", _gmm, reads=[gw_b] + [x_b[kb][ch] for kb in range(KB)],
                        writes=[pbb])
                for tb in range(4):
                    def _tr(e, tb=tb):
                        e.transpose(pb[:, 64 + 2 * tb:65 + 2 * tb],
                                    rstd_r[0:1, tb * 128:(tb + 1) * 128],
                                    identF[0:1, 0:1])
                        return e.transpose(pb[:, 65 + 2 * tb:66 + 2 * tb],
                                           nmr_r[0:1, tb * 128:(tb + 1) * 128],
                                           identF[0:1, 0:1])
                    cst("pe", _tr, reads=[rowsF_buf, identF_b], writes=[pbb])
                cst("dve", lambda e: e.tensor_copy(rc_ap, pb[:, 64:72]),
                    reads=[pbb, rc_b], writes=[rc_b])

            def unit_b():
                rc3 = rc_ap.rearrange("p (t c) -> p t c", c=2)
                cchain("dve", [
                    lambda e: e.tensor_tensor(
                        seg(l_ap), pb[:, 0:64].rearrange("p (t n) -> p t n", n=NE),
                        rc3[:, :, 0:1].broadcast_to([128, 4, NE]), ALU.mult),
                    lambda e: e.tensor_tensor(
                        seg(sb_ap), Sbc_sb[:].unsqueeze(1).broadcast_to([128, 4, NE]),
                        rc3[:, :, 1:2].broadcast_to([128, 4, NE]), ALU.mult),
                    lambda e: e.tensor_tensor(l_ap, l_ap, sb_ap, ALU.add),
                    lambda e: e.reduce_max(c1_ap, seg(l_ap), axis=AX.X, negate=True),
                    lambda e: e.tensor_tensor(
                        seg(l_ap), seg(l_ap),
                        c1_ap.unsqueeze(2).broadcast_to([128, 4, NE]), ALU.add),
                ], reads=[pbb, Sbc_b],
                   writes=[l_b, sb_b, c1_b, rc_b])
                cst("act", lambda e: e.activation(el_ap, l_ap, AF.Exp),
                    reads=[l_b, el_b], writes=[el_b])
                fns = [lambda e: e.tensor_copy(vv_ap, el_ap)]
                for _ in range(3):
                    fns += [
                        lambda e: e.reduce_max(c2_ap, seg(vv_ap), axis=AX.X),
                        lambda e: e.tensor_tensor(
                            seg(eqi_ap), seg(vv_ap),
                            c2_ap.unsqueeze(2).broadcast_to([128, 4, NE]),
                            ALU.is_equal),
                        lambda e: e.select(vv_ap, eqi_ap, tk_n1, vv_ap),
                    ]
                fns += [
                    lambda e: e.reduce_max(c2_ap, seg(vv_ap), axis=AX.X),
                    lambda e: e.tensor_tensor(
                        seg(t1_ap), seg(el_ap),
                        c2_ap.unsqueeze(2).broadcast_to([128, 4, NE]), ALU.is_ge),
                    lambda e: e.tensor_tensor(t1_ap, el_ap, t1_ap, ALU.mult),
                    lambda e: e.reduce_sum(c1_ap, seg(t1_ap), axis=AX.X),
                    lambda e: e.reciprocal(c1_ap, c1_ap),
                    lambda e: e.tensor_tensor(
                        seg(t1_ap), seg(t1_ap),
                        c1_ap.unsqueeze(2).broadcast_to([128, 4, NE]), ALU.mult),
                ]
                cchain("dve", fns,
                       reads=[el_b, tk_n1_b],
                       writes=[vv_b, eqi_b, c1_b, c2_b, t1_b])

            def unit_c():
                for tb in range(4):
                    co = 256 + 128 * (tb % 2)
                    cst("pe", lambda e, tb=tb, co=co: e.transpose(
                        pb[0:16, co:co + 128], t1_ap[:, 16 * tb:16 * tb + NE],
                        identF[:]),
                        reads=[t1_b, identF_b], writes=[pbb])
                    cst("dve", lambda e, tb=tb, co=co: e.tensor_copy(
                        comb_ap[par][:, tb * 128:(tb + 1) * 128], pb[0:16, co:co + 128]),
                        reads=[pbb, comb_buf[par]], writes=[comb_buf[par]])

            return [unit_a, unit_b, unit_c]

        wcnt = [0]
        cache_buf = {}

        def stream_w13(e_, hf):
            wp = wcnt[0] % 2
            wcnt[0] += 1
            key = ("w13", e_, hf)
            if (not USE_CACHE) or key not in cache_buf:
                if e_ < NE:
                    s1, s3 = w1T_h[e_], w3T_h[e_]
                else:
                    s1, s3 = sw1T_h, sw3T_h
                co = hf * 256
                dst("pool", f"w13c_{wp}",
                    [(w1h_ap[wp].rearrange("p (k h) -> p k h", k=KB),
                      s1[:, co:co + 256].rearrange("(k p) h -> p k h", p=128), {}),
                     (w3h_ap[wp].rearrange("p (k h) -> p k h", k=KB),
                      s3[:, co:co + 256].rearrange("(k p) h -> p k h", p=128), {})],
                    writes=[w1h_buf[wp], w3h_buf[wp]])
                cb13 = Buf(f"c13_{e_}_{hf}")
                dst("sp", f"wb13_{wp}",
                    [(w1c_h[e_, hf], w1h_ap[wp], {}),
                     (w3c_h[e_, hf], w3h_ap[wp], {})],
                    reads=[w1h_buf[wp], w3h_buf[wp]], writes=[cb13])
                cache_buf[key] = cb13
            else:
                dst("sp", f"w13_{wp}",
                    [(w1h_ap[wp], w1c_h[e_, hf], {}),
                     (w3h_ap[wp], w3c_h[e_, hf], {})],
                    reads=[cache_buf[key]], writes=[w1h_buf[wp], w3h_buf[wp]])
            return wp

        w2cnt = [0]

        def stream_w2(e_, th):
            wp = w2cnt[0] % 3
            w2cnt[0] += 1
            key = ("w2", e_, th)
            if (not USE_CACHE) or key not in cache_buf:
                s2 = w2T_h[e_] if e_ < NE else sw2T_h
                co = th * 512
                dst("pool", f"w2c_{wp}",
                    [(w2h_ap[wp].rearrange("p (k h) -> p k h", k=4),
                      s2[:, co:co + 512].rearrange("(k p) h -> p k h", p=128), {})],
                    writes=[w2h_buf[wp]])
                cb2 = Buf(f"c2_{e_}_{th}")
                dst("sp", f"wb2_{wp}", [(w2c_h[e_, th], w2h_ap[wp], {})],
                    reads=[w2h_buf[wp]], writes=[cb2])
                cache_buf[key] = cb2
            else:
                dst("act", f"w2_{wp}", [(w2h_ap[wp], w2c_h[e_, th], {})],
                    reads=[cache_buf[key]], writes=[w2h_buf[wp]])
            return wp

        def f_chunk_gen(fi, c, par):
            c0 = c * C
            ch = c
            rot = [0]
            FBANKS = [0, 1, 2, 4, 5]

            def nb():
                b = FBANKS[rot[0] % len(FBANKS)]
                rot[0] += 1
                return b

            flat13 = []
            for gi, grp in enumerate(GROUPS):
                for si, e_ in enumerate(grp):
                    flat13.append((e_, 0))
                    flat13.append((e_, 1))
            p13 = {}
            p13[0] = stream_w13(GROUPS[0][0], 0)
            p13[1] = stream_w13(GROUPS[0][0], 1)
            next13 = [2]

            def adv13():
                if next13[0] < len(flat13):
                    e_, hf = flat13[next13[0]]
                    p13[next13[0]] = stream_w13(e_, hf)
                    next13[0] += 1

            flat2 = []
            for gi, grp in enumerate(GROUPS):
                for th in range(2):
                    flat2.append((gi, th, grp[-1]))
            p2 = {}
            next2 = [0]

            def adv2():
                if next2[0] < len(flat2):
                    gi, th, e_last = flat2[next2[0]]
                    p2[next2[0]] = stream_w2(e_last, th)
                    next2[0] += 1
            adv2()
            adv2()

            EXP = [g[0] for g in GROUPS]
            k13 = [0]

            def hpart(j, e_):
                aset = j % 2
                if e_ < NE:
                    b = nb()
                    cst("pe", lambda e, e_=e_, b=b: e.matmul(
                        pb_t[b][:], identB[0:16, e_:e_ + 1].broadcast_to([16, 128]),
                        comb_ap[par][:], start=True, stop=True),
                        reads=[identB_b, comb_buf[par]], writes=[pb_b[b]])
                    cst("act", lambda e, aset=aset, b=b: e.copy(cb_ap[aset], pb_t[b][:]),
                        reads=[pb_b[b], cb_buf[aset]], writes=[cb_buf[aset]])
                for hb in range(4):
                    hf = hb // 2
                    idx13 = k13[0] * 2 + hf
                    wp = p13[idx13]
                    b1 = nb()
                    b3 = nb()
                    hoff = (hb % 2) * 128
                    def _h1(e, wp=wp, hoff=hoff, b1=b1):
                        for kb in range(KB):
                            inst = e.matmul(
                                pb_t[b1][:],
                                w1h_ap[wp][:, kb * 256 + hoff: kb * 256 + hoff + 128],
                                xln_ap[par][kb], start=(kb == 0), stop=(kb == KB - 1))
                        return inst
                    cst("pe", _h1, reads=[w1h_buf[wp]] + xln_buf[par],
                        writes=[pb_b[b1]])
                    def _h3(e, wp=wp, hoff=hoff, b3=b3):
                        for kb in range(KB):
                            inst = e.matmul(
                                pb_t[b3][:],
                                w3h_ap[wp][:, kb * 256 + hoff: kb * 256 + hoff + 128],
                                xln_ap[par][kb], start=(kb == 0), stop=(kb == KB - 1))
                        return inst
                    cst("pe", _h3, reads=[w3h_buf[wp]] + xln_buf[par],
                        writes=[pb_b[b3]])
                    spar = hb % 2
                    cst("act", lambda e, b1=b1, spar=spar: e.activation(
                        sil_ap[spar], pb_t[b1][:], AF.Silu),
                        reads=[pb_b[b1], sil_buf[spar]], writes=[sil_buf[spar]])
                    if e_ < NE:
                        cst("dve", lambda e, b3=b3, spar=spar, aset=aset:
                            e.tensor_tensor(h3c_ap[spar], pb_t[b3][:],
                                            cb_ap[aset], ALU.mult),
                            reads=[pb_b[b3], cb_buf[aset], h3c_buf[spar]],
                            writes=[h3c_buf[spar]])
                    else:
                        cst("dve", lambda e, b3=b3, spar=spar: e.tensor_copy(
                            h3c_ap[spar], pb_t[b3][:]),
                            reads=[pb_b[b3], h3c_buf[spar]], writes=[h3c_buf[spar]])
                    cst("pool", lambda e, aset=aset, hb=hb, spar=spar:
                        e.tensor_tensor(act_ap[aset][hb], sil_ap[spar],
                                        h3c_ap[spar], ALU.mult),
                        reads=[sil_buf[spar], h3c_buf[spar], act_buf[aset][hb]],
                        writes=[act_buf[aset][hb]])
                    if hb == 1 or hb == 3:
                        adv13()
                    yield
                k13[0] += 1

            def epart(j, e_):
                aset = j % 2
                for db in range(KB):
                    th = db // 4
                    wp2 = p2[j * 2 + th]
                    eb = nb()
                    doff = (db % 4) * 128
                    def _eo(e, wp2=wp2, doff=doff, eb=eb, aset=aset):
                        for hb in range(4):
                            inst = e.matmul(
                                pb_t[eb][:],
                                w2h_ap[wp2][:, hb * 512 + doff: hb * 512 + doff + 128],
                                act_ap[aset][hb],
                                start=(hb == 0), stop=(hb == 3))
                        return inst
                    cst("pe", _eo,
                        reads=[w2h_buf[wp2]] + [act_buf[aset][hb] for hb in range(4)],
                        writes=[pb_b[eb]])
                    cst("dve", lambda e, db=db, eb=eb: e.scalar_tensor_tensor(
                        x_t[db][:, c0:c0 + C], pb_t[eb][:], 0.5,
                        x_t[db][:, c0:c0 + C], ALU.mult, ALU.add),
                        reads=[pb_b[eb], x_b[db][ch]], writes=[x_b[db][ch]])
                    if db == 3:
                        adv2()
                    if db == 7:
                        adv2()
                    yield

            yield from hpart(0, EXP[0])
            for j in range(1, len(EXP)):
                yield from hpart(j, EXP[j])
                yield from epart(j - 1, EXP[j - 1])
            yield from epart(len(EXP) - 1, EXP[-1])

        def f_head_units(fi, c, par):
            c0 = c * C
            lu = ln_units(c0, xln_ap[par], xln_buf[par], rowsF_ap, rowsF_buf, 3)
            ru = routing_units(c0, par)
            skip = lambda: None
            u = [lu[0], lu[1], skip, skip, skip, lu[2], lu[3],
                 skip, ru[0], ru[1], skip, skip, skip, ru[2]]
            return u

        def m_chunk_gen(c):
            c0 = c * C
            ch = c

            def stream_ip(mb):
                pp = mb % 2
                key = ("ip", mb)
                if (not USE_CACHE) or key not in cache_buf:
                    dst("pool", f"mipc_{pp}",
                        [(ipw_ap[pp].rearrange("p (k m) -> p k m", k=KB),
                          ipT_h[:, mb * 128:(mb + 1) * 128].rearrange(
                              "(k p) m -> p k m", p=128), {})],
                        writes=[ipw_buf[pp]])
                    cbi = Buf(f"cip_{mb}")
                    dst("sp", f"wbip_{pp}", [(ipc_h[mb], ipw_ap[pp], {})],
                        reads=[ipw_buf[pp]], writes=[cbi])
                    cache_buf[key] = cbi
                else:
                    dst("sp", f"mip_{pp}", [(ipw_ap[pp], ipc_h[mb], {})],
                        reads=[cache_buf[key]], writes=[ipw_buf[pp]])

            def stream_dp(r):
                pp = r % 2
                key = ("dp", r)
                if (not USE_CACHE) or key not in cache_buf:
                    dst("pool", f"mdpc_{pp}",
                        [(dpw_ap[pp].rearrange("p (k m) -> p k m", k=KB),
                          dpT_h[:, r * 128:(r + 1) * 128].rearrange(
                              "(k p) m -> p k m", p=128), {})],
                        writes=[dpw_buf[pp]])
                    cbd = Buf(f"cdp_{r}")
                    dst("sp", f"wbdp_{pp}", [(dpc_h[r], dpw_ap[pp], {})],
                        reads=[dpw_buf[pp]], writes=[cbd])
                    cache_buf[key] = cbd
                else:
                    dst("sp", f"mdp_{pp}", [(dpw_ap[pp], dpc_h[r], {})],
                        reads=[cache_buf[key]], writes=[dpw_buf[pp]])

            for u in ln_units(c0, mxln_ap, mxln_buf, rowsM_ap, rowsM_buf, 6):
                u()
                yield
            if ch > 0:
                for r in range(KB):
                    cst("dve", lambda e, r=r: e.tensor_copy(
                        cpad_ap[r][:, 1:4], cpad_ap[r][:, C + 1:C + 4]),
                        reads=[cpad_buf[r]], writes=[cpad_buf[r]])
            yield
            stream_ip(0)
            stream_ip(1)
            for mb in range(KB):
                pp = mb % 2
                bank = 6 + pp
                def _ipmm(e, pp=pp, bank=bank):
                    for kb in range(KB):
                        inst = e.matmul(pb_t[bank][:],
                                        ipw_ap[pp][:, kb * 128:(kb + 1) * 128],
                                        mxln_ap[kb],
                                        start=(kb == 0), stop=(kb == KB - 1))
                    return inst
                cst("pe", _ipmm, reads=[ipw_buf[pp]] + mxln_buf,
                    writes=[pb_b[bank]])
                cst("dve", lambda e, mb=mb, bank=bank: e.tensor_scalar(
                    cpad_ap[mb][:, 4:4 + C], pb_t[bank][:],
                    ipb_sb[:, mb:mb + 1], None, ALU.add),
                    reads=[pb_b[bank], ipb_b, cpad_buf[mb]], writes=[cpad_buf[mb]])
                if mb + 2 < KB:
                    stream_ip(mb + 2)
                yield
            def _bmm(e):
                for kb in range(KB):
                    inst = e.matmul(pb_t[6][:], bpw_ap[:, kb * 128:(kb + 1) * 128],
                                    cpad_ap[kb][:, 4:4 + C],
                                    start=(kb == 0), stop=(kb == KB - 1))
                return inst
            cst("pe", _bmm, reads=[bpw_buf] + cpad_buf, writes=[pb_b[6]])
            cchain("dve", [
                lambda e: e.tensor_scalar(vs_ap[0], pb_t[6][:], bpb_sb[:], None, ALU.add),
                lambda e: e.tensor_copy(Bh_ap, vs_ap[0]),
                lambda e: e.tensor_tensor(vs_ap[1], vs_ap[0], Bh_ap, ALU.subtract),
                lambda e: e.tensor_copy(Bl_ap, vs_ap[1]),
            ], reads=[pb_b[6], bpb_b], writes=[Bh_buf, Bl_buf, vs_buf[0], vs_buf[1]])
            yield
            stream_dp(0)
            stream_dp(1)

            def delta_unit(r):
                rp = r % 2
                pp = r % 2
                bank = 6 + pp
                s0, s0b = dscr_ap[0], dscr_buf[0]
                def _dmm(e, pp=pp, bank=bank):
                    for kb in range(KB):
                        inst = e.matmul(pb_t[bank][:],
                                        dpw_ap[pp][:, kb * 128:(kb + 1) * 128],
                                        cpad_ap[kb][:, 4:4 + C],
                                        start=(kb == 0), stop=(kb == KB - 1))
                    return inst
                cst("pe", _dmm, reads=[dpw_buf[pp]] + cpad_buf,
                    writes=[pb_b[bank]])
                cchain("act", [
                    (lambda e, bank=bank, s0=s0: e.activation(
                        s0, pb_t[bank][:], AF.Abs)),
                    (lambda e, s0=s0: e.activation(s0, s0, AF.Exp, scale=-1.0)),
                    (lambda e, s0=s0: e.activation(s0, s0, AF.Ln, bias=1.0)),
                ], reads=[pb_b[bank]], writes=[s0b])
                cchain("dve", [
                    (lambda e, bank=bank, rp=rp: e.tensor_scalar(
                        delta_ap[rp], pb_t[bank][:], 0.0, None, ALU.max)),
                    (lambda e, rp=rp, s0=s0: e.tensor_tensor(
                        delta_ap[rp], delta_ap[rp], s0, ALU.add)),
                ], reads=[pb_b[bank], s0b], writes=[delta_buf[rp]])
                if r + 2 < KB:
                    stream_dp(r + 2)

            def convdu_unit(r):
                rp = r % 2
                cs, csb = vs_ap[0], vs_buf[0]
                cvfns = [(lambda e, r=r, cs=cs: e.tensor_scalar(
                    cs, cpad_ap[r][:, 1:1 + C],
                    conv_sb[:, r * 4:r * 4 + 1], None, ALU.mult))]
                for kk in range(1, 4):
                    cvfns.append(lambda e, r=r, cs=cs, kk=kk: e.scalar_tensor_tensor(
                        cs, cpad_ap[r][:, 1 + kk:1 + kk + C],
                        conv_sb[:, r * 4 + kk:r * 4 + kk + 1], cs,
                        ALU.mult, ALU.add))
                cchain("dve", cvfns, reads=[cpad_buf[r], conv_wb], writes=[csb])
                cst("act", lambda e, r=r, cs=cs: e.activation(
                    cs, cs, AF.Silu, bias=cvb_sb[:, r:r + 1]),
                    reads=[csb, cvb_b], writes=[csb])
                cst("dve", lambda e, rp=rp, cs=cs: e.tensor_tensor(
                    du_ap[rp], delta_ap[rp], cs, ALU.mult),
                    reads=[delta_buf[rp], csb, du_buf[rp]], writes=[du_buf[rp]])

            delta_unit(0)
            yield
            convdu_unit(0)
            yield
            for r in range(KB):
                rp = r % 2
                if r + 1 < KB:
                    delta_unit(r + 1)
                    yield
                    convdu_unit(r + 1)
                    yield
                stream_ip(8 + r)
                for pn in range(NST // 2):
                    for spar in range(2):
                        idx = r * NST + 2 * pn + spar
                        bank = 6 + spar
                        def _bc2(e, idx=idx, bank=bank):
                            e.matmul(pb_t[bank][:],
                                     identB[:, idx:idx + 1].broadcast_to([128, 128]),
                                     Bh_ap, start=True, stop=False)
                            return e.matmul(pb_t[bank][:],
                                            identB[:, idx:idx + 1].broadcast_to([128, 128]),
                                            Bl_ap, start=False, stop=True)
                        cst("pe", _bc2, reads=[identB_b, Bh_buf, Bl_buf],
                            writes=[pb_b[bank]])
                    for spar in range(2):
                        idx = r * NST + 2 * pn + spar
                        cst("act", lambda e, rp=rp, idx=idx, spar=spar: e.activation(
                            dA_ap[spar], delta_ap[rp], AF.Exp,
                            scale=A_sb[:, idx:idx + 1]),
                            reads=[delta_buf[rp], A_b, dA_buf[spar]],
                            writes=[dA_buf[spar]])
                    for spar in range(2):
                        idx = r * NST + 2 * pn + spar
                        bank = 6 + spar
                        cst("dve", lambda e, rp=rp, spar=spar, bank=bank, idx=idx:
                            e.scalar_tensor_tensor(
                                dbu_ap[spar], du_ap[rp], c_sb[:, idx:idx + 1],
                                pb_t[bank][:], ALU.mult, ALU.mult),
                            reads=[du_buf[rp], c_b, pb_b[bank], dbu_buf[spar]],
                            writes=[dbu_buf[spar]])
                    for spar in range(2):
                        idx = r * NST + 2 * pn + spar
                        init = 0.0 if ch == 0 else carry[:, idx:idx + 1]
                        cst("dve", lambda e, spar=spar, init=init: e.tensor_tensor_scan(
                            dbu_ap[spar], dA_ap[spar], dbu_ap[spar], init,
                            ALU.mult, ALU.add),
                            reads=[dA_buf[spar], carry_b, dbu_buf[spar]],
                            writes=[dbu_buf[spar]])
                    for spar in range(2):
                        idx = r * NST + 2 * pn + spar
                        yeng = "pool" if spar == 0 else "dve"
                        if pn == 0:
                            cst(yeng, lambda e, spar=spar: e.tensor_copy(
                                y_ap[spar], dbu_ap[spar]),
                                reads=[dbu_buf[spar], y_buf[spar]],
                                writes=[y_buf[spar]])
                        else:
                            cst(yeng, lambda e, spar=spar: e.tensor_tensor(
                                y_ap[spar], y_ap[spar], dbu_ap[spar], ALU.add),
                                reads=[dbu_buf[spar], y_buf[spar]],
                                writes=[y_buf[spar]])
                        cst("pool", lambda e, idx=idx, spar=spar: e.tensor_copy(
                            carry[:, idx:idx + 1], dbu_ap[spar][:, C - 1:C]),
                            reads=[dbu_buf[spar], carry_b], writes=[carry_b])
                    yield
                cst("dve", lambda e: e.tensor_tensor(
                    y_ap[0], y_ap[0], y_ap[1], ALU.add),
                    reads=[y_buf[1], y_buf[0]], writes=[y_buf[0]])
                pp = r % 2
                bank = 6 + pp
                def _vmm(e, pp=pp, bank=bank):
                    for kb in range(KB):
                        inst = e.matmul(pb_t[bank][:],
                                        ipw_ap[pp][:, kb * 128:(kb + 1) * 128],
                                        mxln_ap[kb],
                                        start=(kb == 0), stop=(kb == KB - 1))
                    return inst
                cst("pe", _vmm, reads=[ipw_buf[pp]] + mxln_buf,
                    writes=[pb_b[bank]])
                vsb, vsbb = vs_ap[0], vs_buf[0]
                t3, t3b = vs_ap[1], vs_buf[1]
                mm_, mmb = vs_ap[2], vs_buf[2]
                mb = 8 + r
                cst("dve", lambda e, mb=mb, bank=bank: e.tensor_scalar(
                    vsb, pb_t[bank][:], ipb_sb[:, mb:mb + 1], None, ALU.add),
                    reads=[pb_b[bank], ipb_b, vsbb], writes=[vsbb])
                cchain("act", [
                    lambda e: e.activation(t3, vsb, AF.Abs),
                    lambda e: e.activation(t3, t3, AF.Exp, scale=-1.0),
                    lambda e: e.activation(t3, t3, AF.Ln, bias=1.0),
                ], reads=[vsbb], writes=[t3b])
                cchain("dve", [
                    lambda e: e.tensor_scalar(mm_, vsb, 0.0, None, ALU.min),
                    lambda e: e.tensor_tensor(t3, t3, mm_, ALU.subtract),
                ], reads=[vsbb], writes=[mmb, t3b])
                cst("act", lambda e: e.activation(t3, t3, AF.Exp, scale=-1.0),
                    reads=[t3b], writes=[t3b])
                cchain("pool", [
                    (lambda e: e.tensor_tensor(
                        mm_, y_ap[0], vsb, ALU.mult)),
                    (lambda e: e.tensor_tensor(mm_, mm_, t3, ALU.mult)),
                    (lambda e, r=r, c0=c0: e.tensor_tensor(
                        x_t[r][:, c0:c0 + C], x_t[r][:, c0:c0 + C],
                        mm_, ALU.add)),
                ], reads=[y_buf[0], vsbb, t3b], writes=[mmb, x_b[r][ch]])
                yield

        setup()
        sched.barrier()
        stream_w13(GROUPS[0][0], 0)
        stream_w13(GROUPS[0][0], 1)
        stream_w2(GROUPS[0][0], 0)
        stream_w2(GROUPS[0][0], 1)

        m_gens = [m_chunk_gen(c) for c in range(NCH)]
        m_ready = []
        m_done = [False] * NCH

        def drain_m(k):
            n = 0
            while n < k and m_ready:
                c, g = m_ready[0]
                try:
                    next(g)
                    n += 1
                except StopIteration:
                    m_done[c] = True
                    m_ready.pop(0)

        def drain_m_until_done(c):
            while not m_done[c]:
                drain_m(1)
                if not m_ready:
                    break

        fi0, c0_ = F_ORDER[0]
        for u in f_head_units(fi0, c0_, 0):
            u()
        for k, (fi, c) in enumerate(F_ORDER):
            par = k % 2
            if k + 1 < len(F_ORDER):
                nfi, ncx = F_ORDER[k + 1]
                nhead = f_head_units(nfi, ncx, (k + 1) % 2)
            else:
                nfi, ncx, nhead = None, None, []
            nhq = [0]

            def poke_head():
                if nhq[0] < len(nhead) and (nfi == "f1" or m_done[ncx]):
                    nhead[nhq[0]]()
                    nhq[0] += 1

            gen = f_chunk_gen(fi, c, par)
            for _ in gen:
                drain_m(1 if ZIP_M else 0)
                poke_head()
            if fi == "f1":
                m_ready.append((c, m_gens[c]))
                if not ZIP_M:
                    drain_m_until_done(c)
            if nfi == "f2":
                drain_m_until_done(ncx)
            while nhq[0] < len(nhead):
                nhead[nhq[0]]()
                nhq[0] += 1
            if fi == "f2":
                def _st(e, c=c):
                    for kb in range(KB):
                        e.dma_start(out=yT_h[kb * 128:(kb + 1) * 128, c * C:(c + 1) * C],
                                    in_=x_t[kb][:, c * C:(c + 1) * C]
                                    ).then_inc(SEM["yst"], 16)
                getsem("yst")
                sched.stage("sp", _st, reads=[x_b[kb][c] for kb in range(KB)],
                            dmaq="yst", ndma=KB)
        while m_ready:
            drain_m(1)

        final_qx = sched.cnt["yst"]

        def mk_body(engkey):
            def run(eng):
                for waits, emit, dmaq in sched.plan[engkey]:
                    for k, v in waits.items():
                        eng.wait_ge(SEM[k], v)
                    emit(eng)
                eng.wait_ge(SEM["yst"], final_qx)
            return run

        blk.tensor(mk_body("pe"))
        blk.scalar(mk_body("act"))
        blk.vector(mk_body("dve"))
        blk.gpsimd(mk_body("pool"))
        blk.sync(mk_body("sp"))

    nstages = {k: len(v) for k, v in sched.plan.items()}
    print(f"[kernel] stages={nstages} arena_peak={AR.peak * 4}B/part")
    return nc


_cache = {}


def kernel(**inputs):
    x = np.asarray(inputs["L0_attn_x"])
    cc = np.ascontiguousarray
    shared = {
        "ipT": cc(np.asarray(inputs["L0_attn_in_proj_w"]).T),
        "dpT": cc(np.asarray(inputs["L0_attn_delta_proj_w"]).T),
        "bpT": cc(np.asarray(inputs["L0_attn_b_proj_w"]).T),
        "gwT": cc(np.asarray(inputs["L0_ffn_gate_w"]).T),
        "w1T": cc(np.asarray(inputs["L0_ffn_w1"]).transpose(0, 2, 1)),
        "w3T": cc(np.asarray(inputs["L0_ffn_w3"]).transpose(0, 2, 1)),
        "w2T": cc(np.asarray(inputs["L0_ffn_w2"]).transpose(0, 2, 1)),
        "sw1T": cc(np.asarray(inputs["L0_ffn_sw1"]).T),
        "sw3T": cc(np.asarray(inputs["L0_ffn_sw3"]).T),
        "sw2T": cc(np.asarray(inputs["L0_ffn_sw2"]).T),
        "cT": cc(np.asarray(inputs["L0_attn_c_param"]).T),
        "alog": cc(np.asarray(inputs["L0_attn_a_log"])),
        "convw": cc(np.asarray(inputs["L0_attn_conv_w"]).reshape(D, 4)),
        "ipb": cc(np.asarray(inputs["L0_attn_in_proj_b"]).reshape(16, 128)),
        "convb": cc(np.asarray(inputs["L0_attn_conv_b"]).reshape(8, 128)),
        "bpb": cc(np.asarray(inputs["L0_attn_b_proj_b"]).reshape(1, 128)),
    }
    in_maps = []
    for b in range(8):
        m = dict(shared)
        m["xT"] = cc(x[b].T)
        in_maps.append(m)

    key = (SERIAL, ZIP_M, USE_CACHE)
    if key not in _cache:
        _cache[key] = build_program()
    nc = _cache[key]
    res = run_bass_kernel_spmd(nc, in_maps, core_ids=list(range(8)))
    out = np.empty((8, T, D), np.float32)
    for b in range(8):
        out[b] = res.results[b]["yT"].T
    return out
